# version 24
# baseline (speedup 1.0000x reference)
import math
import numpy as np
import ml_dtypes
import concourse.bass as bass
import concourse.mybir as mybir
from concourse.bass import IndirectOffsetOnAxis
from concourse.bass_utils import run_bass_kernel_spmd

F32 = mybir.dt.float32
BF16 = mybir.dt.bfloat16
I32 = mybir.dt.int32
U8 = mybir.dt.uint8
AF = mybir.ActivationFunctionType
ALU = mybir.AluOpType
AX = mybir.AxisListType

S = 2048
D = 1024
DT = D // 128
NCH = S // 512
H, HD = 16, 64
DC, DR = 256, 128
EXP, HEXP = 32, 512
HS = 1024
HID1 = 3072
QSCALE = 1.0 / 8.0
INVQ = 1.0 / 127.0
LNEPS = 1e-5
CAP = 384


class Sem:
    __slots__ = ("name", "count", "handle")

    def __init__(self, name):
        self.name = name
        self.count = 0
        self.handle = None


class Prog:
    ENGINES = ("sync", "tensor", "vector", "scalar", "gpsimd")

    def __init__(self):
        self.ops = {e: [] for e in self.ENGINES}
        self._sems = []
        self.auto = {}

    def sem(self, name):
        s = Sem(name)
        self._sems.append(s)
        return s

    def enable_auto(self, engine, sem):
        self.auto[engine] = sem

    def op(self, engine, fn, waits=(), incs=(), _dma=False):
        waits = [(s, int(v)) for (s, v) in waits if v > 0]
        incs = list(incs)
        aut = self.auto.get(engine)
        if aut is not None and not _dma:
            if aut.count > 0:
                waits.append((aut, aut.count))
            incs = [(s, a) for (s, a) in incs if s is not aut]
            incs.append((aut, 1))
        for s, amt in incs:
            s.count += amt
        self.ops[engine].append((fn, tuple(waits), tuple(incs)))

    def dma(self, queue, out, in_, sem, waits=()):
        self.op(queue, lambda e, o=out, i=in_: e.dma_start(out=o, in_=i),
                waits, [(sem, 16)], _dma=True)

    def emit(self, nc):
        for s in self._sems:
            s.handle = nc.alloc_semaphore(name=s.name)
        with nc.Block() as blk:
            for eng_name in self.ENGINES:
                ops = self.ops[eng_name]
                if not ops:
                    continue

                def body(eng, ops=ops):
                    for fn, waits, incs in ops:
                        for s, v in waits:
                            eng.wait_ge(s.handle, v)
                        instr = fn(eng)
                        for s, amt in incs:
                            instr.then_inc(s.handle, amt)

                getattr(blk, eng_name)(body)


class Arena:

    def __init__(self, scratch_ap, nbytes):
        self.sc = scratch_ap
        self.nbytes = nbytes
        self.live = {}
        self.hwm = 0

    def alloc(self, name, nbytes_per_part, dtype, free_elems, side="low"):
        nbytes_per_part = int(nbytes_per_part)
        assert name not in self.live
        taken = sorted(self.live.values())
        if side == "low":
            off = 0
            for a, sz in taken:
                aligned = (off + 511) & ~511
                if aligned + nbytes_per_part <= a:
                    break
                off = max(off, a + sz)
            off = (off + 511) & ~511
            ok = off + nbytes_per_part <= self.nbytes
        else:
            off = self.nbytes
            for a, sz in reversed(taken):
                cand = (off - nbytes_per_part) & ~511
                if cand >= a + sz:
                    break
                off = min(off, a)
            off = (off - nbytes_per_part) & ~511
            ok = off >= 0
            for a, sz in taken:
                if not (off + nbytes_per_part <= a or off >= a + sz):
                    ok = False
        assert ok, (
            f"arena overflow: {name} needs {nbytes_per_part}B (side={side}), "
            f"cap {self.nbytes}, live {self.live}")
        self.live[name] = (off, nbytes_per_part)
        self.hwm = max(self.hwm, off + nbytes_per_part)
        return self.sc[:, off:off + nbytes_per_part].bitcast(dtype)[:, :free_elems]

    def free(self, *names):
        for n in names:
            del self.live[n]


class Banks:

    def __init__(self, nc, s_mm, esems):
        self.ap = [nc.alloc_psum_tensor(f"bank{i}", [128, 512], F32).ap()
                   for i in range(8)]
        self.s_mm = s_mm
        self.esems = esems
        self.readers = [dict() for _ in range(8)]
        self.mm_done = [0] * 8

    def war_waits(self, i):
        return [(self.esems[e], c) for e, c in self.readers[i].items()]

    def wrote(self, i):
        self.mm_done[i] = self.s_mm.count
        self.readers[i] = {}

    def read(self, i, engine):
        self.readers[i][engine] = self.esems[engine].count

    def raw_wait(self, i):
        return [(self.s_mm, self.mm_done[i])]


class Build:
    def __init__(self, phases="ABCD"):
        self.phases = phases
        nc = bass.Bass("TRN2", target_bir_lowering=False)
        self.nc = nc
        pr = Prog()
        self.pr = pr
        self.s_x = pr.sem("x_ld")
        self.s_small = pr.sem("small")
        self.s_wA1 = pr.sem("wA1")
        self.s_wA2 = pr.sem("wA2")
        self.s_rope = [pr.sem("rope0"), pr.sem("rope1")]
        self.s_ctx = [pr.sem("ctx0"), pr.sem("ctx1")]
        self.s_mm = pr.sem("mm")
        self.s_sc = pr.sem("sc")
        self.s_ve = pr.sem("ve")
        self.s_st = pr.sem("st")
        self.esems = {"scalar": self.s_sc, "vector": self.s_ve}
        self.s_gp = pr.sem("gp")
        pr.enable_auto("vector", self.s_ve)
        pr.enable_auto("scalar", self.s_sc)
        pr.enable_auto("gpsimd", self.s_gp)
        self.banks = Banks(nc, self.s_mm, self.esems)

        dram = {}

        def din(name, shape, dt=F32):
            dram[name] = nc.dram_tensor(name, shape, dt, kind="ExternalInput").ap()
            return dram[name]

        self.d = dram
        din("xT", [D, S])
        if "A" in phases:
            din("wq", [D, DC], BF16); din("wqc", [DC, D - DR], BF16)
            din("wqr", [D, DR], BF16)
            din("wkv", [D, DC], BF16); din("wkr", [D, DR], BF16)
            din("wkc", [DC, D - DR], BF16)
            din("wv", [DC, D], BF16); din("wo", [D, D], BF16)
            din("bq", [128, 2]); din("bqc", [128, 7]); din("bqr", [128, 1])
            din("bkv", [128, 2]); din("bkr", [128, 1]); din("bkc", [128, 7])
            din("bv", [128, 8]); din("bo", [128, 8])
            din("ropecos", [128, S]); din("ropesin", [128, S])
            din("masks", [128, 4 * 512], BF16)
        if "B" in phases:
            self.w_scr = nc.dram_tensor("w_scr", [EXP, S], BF16, kind="Internal").ap()
            self.Xall = nc.dram_tensor("Xall", [EXP * CAP + 128, D], BF16, kind="Internal").ap()
            self.OEall = nc.dram_tensor("OEall", [EXP * CAP + 128, D], BF16, kind="Internal").ap()
            din("iotae", [128, 16 * EXP])
            din("identb", [128, 128], BF16)
            din("identf", [128, 128])
            din("oneh", [32, 32 * 128], BF16)
            din("Wg", [D, EXP])
            din("W1", [EXP, D, HEXP], BF16); din("W2", [EXP, HEXP, D], BF16)
            din("W3", [EXP, D, HEXP], BF16)
            din("Ws1", [D, HS], BF16); din("Ws2", [HS, D], BF16)
            din("Ws3", [D, HS], BF16)
        if "C" in phases:
            din("l1wqT", [D, 64], BF16); din("l1wkT", [D, 64], BF16)
            din("l1wvT", [D, 64], BF16)
            din("l1woT", [64, D], BF16)
        if "D" in phases:
            din("wgT", [D, HID1], BF16); din("wuT", [D, HID1], BF16)
            din("wdT", [HID1, D], BF16)
        self.out_d = nc.dram_tensor("y_out", [D, S], F32, kind="ExternalOutput").ap()

        sb = lambda name, shape, dt: nc.alloc_sbuf_tensor(name, shape, dt).ap()
        self.sb = sb
        self.y = sb("y_res", [128, DT * S], F32)
        self.ones_bf = sb("ones_bf", [128, 1], BF16)
        self.ones_f1 = sb("ones_f1", [1, 128], F32)
        self.mq = sb("mq", [1, DT], F32)
        self.mk = sb("mk", [1, DT], F32)
        self.red = sb("mred", [1, 8], F32)
        self.negmb = sb("negmb", [128, DT], F32)
        self.smallb = {}
        if "A" in phases:
            for nm in ("bq", "bqc", "bqr", "bkv", "bkr", "bkc", "bv", "bo"):
                self.smallb[nm] = sb(nm + "_s", [128, dram[nm].shape[1]], F32)
        if "B" in phases:
            self.gb = sb("gb", [128, EXP], F32)
            self.pw = sb("pw", [128, EXP], F32)
            self.msk = sb("msk", [128, EXP], F32)
            self.sel = sb("sel", [128, EXP], F32)
            self.mx1 = sb("mx1", [128, 1], F32)

        rem = int(nc.sbuf_bytes_remaining)
        scr_b = (rem - 1024) & ~1023
        self.scr_t = sb("scratch", [128, scr_b], U8)
        self.ar = Arena(self.scr_t, scr_b)
        self.scr_b = scr_b

        pr.op("vector", lambda v: v.memset(self.ones_bf, 1.0))
        pr.op("vector", lambda v: v.memset(self.ones_f1, 1.0), incs=[(self.s_ve, 1)])

        xin = dram["xT"].rearrange("(a p) t -> p a t", p=128)
        yv = self.y.rearrange("p (a t) -> p a t", a=DT)
        self.s_x2 = pr.sem("x_ld2")
        pr.dma("sync", yv[:, 0:4, :], xin[:, 0:4, :], self.s_x)
        pr.dma("sync", yv[:, 4:8, :], xin[:, 4:8, :], self.s_x2)
        self.y_ready = ("x", self.s_x.count)

    def ywait(self):
        kind, v = self.y_ready
        if kind == "x":
            return [(self.s_x, self.s_x.count), (self.s_x2, self.s_x2.count)]
        sem = {"ve": self.s_ve, "sc": self.s_sc}[kind]
        return [(sem, v)]

    def wload(self, name, dram_ap, ktiles, dout, sem, waits=()):
        t = self.ar.alloc(name, ktiles * dout * 2, BF16, ktiles * dout, side="high")
        src = dram_ap.rearrange("(a p) o -> p a o", p=128)
        dst = t.rearrange("p (a o) -> p a o", a=ktiles)
        self.pr.dma("gpsimd", dst, src, sem, waits=waits)
        return t

    def ln_start(self, tag, ybf_buf=None):
        ar = self.ar
        st = {
            "tag": tag,
            "ybf": ybf_buf if ybf_buf is not None
            else ar.alloc("ybf_" + tag, DT * S * 2, BF16, DT * S),
            "sq": ar.alloc("sq_" + tag, DT * 512 * 2, BF16, DT * 512, side="high"),
            "m": ar.alloc("stm_" + tag, S * 4, F32, S, side="high")[0:1, :],
            "v": ar.alloc("stv_" + tag, S * 4, F32, S, side="high")[0:1, :],
            "t": ar.alloc("stt_" + tag, S * 4, F32, S, side="high")[0:1, :],
        }
        return st

    def ln_chunk(self, st, c, wait=None, emit_ybf=True):
        pr, bk = self.pr, self.banks
        ybf, sq = st["ybf"], st["sq"]
        st_m, st_v, st_t = st["m"], st["v"], st["t"]
        if wait is None:
            wait = self.ywait()
        else:
            wait = [wait]
        for d in range(DT):
            pr.op("vector",
                  lambda v, d=d, c=c: v.tensor_copy(
                      ybf[:, d * S + c * 512: d * S + (c + 1) * 512],
                      self.y[:, d * S + c * 512: d * S + (c + 1) * 512]),
                  waits=wait if d == 0 else ())
        cast_c = self.s_ve.count
        sq_m = []
        for d in range(DT):
            w = [(self.s_ve, cast_c)] if d == 0 else []
            w += [(self.s_mm, bk.mm_done[5])]
            pr.op("scalar",
                  lambda sx, d=d, c=c: sx.activation(
                      sq[:, d * 512:(d + 1) * 512],
                      ybf[:, d * S + c * 512: d * S + (c + 1) * 512],
                      AF.Square),
                  waits=w)
            sq_m.append(self.s_sc.count)
        w4 = bk.war_waits(4) + [(self.s_ve, cast_c)]
        for d in range(DT):
            pr.op("tensor",
                  lambda t, d=d, c=c: t.matmul(
                      bk.ap[4][0:1, :], self.ones_bf,
                      ybf[:, d * S + c * 512: d * S + (c + 1) * 512],
                      start=(d == 0), stop=(d == DT - 1)),
                  waits=w4 if d == 0 else (),
                  incs=[(self.s_mm, 1)] if d == DT - 1 else ())
        bk.wrote(4)
        w5 = bk.war_waits(5)
        for d in range(DT):
            pr.op("tensor",
                  lambda t, d=d: t.matmul(
                      bk.ap[5][0:1, :], self.ones_bf, sq[:, d * 512:(d + 1) * 512],
                      start=(d == 0), stop=(d == DT - 1)),
                  waits=(w5 if d == 0 else []) + [(self.s_sc, sq_m[d])],
                  incs=[(self.s_mm, 1)])
        bk.wrote(5)
        pr.op("vector",
              lambda v, c=c: v.tensor_scalar(st_m[:, c * 512:(c + 1) * 512],
                                             bk.ap[4][0:1, :], 1.0 / D, None, ALU.mult),
              waits=bk.raw_wait(4))
        bk.read(4, "vector")
        pr.op("vector",
              lambda v, c=c: v.tensor_scalar(st_v[:, c * 512:(c + 1) * 512],
                                             bk.ap[5][0:1, :], 1.0 / D, None, ALU.mult),
              waits=bk.raw_wait(5))
        bk.read(5, "vector")
        pr.op("vector",
              lambda v, c=c: v.tensor_tensor(st_t[:, c * 512:(c + 1) * 512],
                                             st_m[:, c * 512:(c + 1) * 512],
                                             st_m[:, c * 512:(c + 1) * 512], ALU.mult))
        pr.op("vector",
              lambda v, c=c: v.tensor_tensor(st_v[:, c * 512:(c + 1) * 512],
                                             st_v[:, c * 512:(c + 1) * 512],
                                             st_t[:, c * 512:(c + 1) * 512], ALU.subtract))
        pr.op("vector",
              lambda v, c=c: v.tensor_scalar(st_v[:, c * 512:(c + 1) * 512],
                                             st_v[:, c * 512:(c + 1) * 512],
                                             LNEPS, None, ALU.add))
        ve_var = self.s_ve.count
        pr.op("scalar", lambda sx, c=c: sx.activation(st_v[:, c * 512:(c + 1) * 512],
                                                      st_v[:, c * 512:(c + 1) * 512], AF.Ln),
              waits=[(self.s_ve, ve_var)])
        pr.op("scalar", lambda sx, c=c: sx.activation(st_v[:, c * 512:(c + 1) * 512],
                                                      st_v[:, c * 512:(c + 1) * 512],
                                                      AF.Exp, scale=-0.5))
        r_sc = self.s_sc.count
        pr.op("vector",
              lambda v, c=c: v.tensor_tensor(st_t[:, c * 512:(c + 1) * 512],
                                             st_m[:, c * 512:(c + 1) * 512],
                                             st_v[:, c * 512:(c + 1) * 512], ALU.mult),
              waits=[(self.s_sc, r_sc)])
        mr_ve = self.s_ve.count
        w6 = bk.war_waits(6) + [(self.s_sc, r_sc)]
        pr.op("tensor",
              lambda t, c=c: t.matmul(bk.ap[6], self.ones_f1,
                                      st_v[:, c * 512:(c + 1) * 512],
                                      start=True, stop=True),
              waits=w6, incs=[(self.s_mm, 1)])
        bk.wrote(6)
        w7 = bk.war_waits(7) + [(self.s_ve, mr_ve)]
        pr.op("tensor",
              lambda t, c=c: t.matmul(bk.ap[7], self.ones_f1,
                                      st_t[:, c * 512:(c + 1) * 512],
                                      start=True, stop=True),
              waits=w7, incs=[(self.s_mm, 1)])
        bk.wrote(7)
        for d in range(DT):
            pr.op("vector",
                  lambda v, d=d, c=c: v.tensor_tensor(
                      self.y[:, d * S + c * 512: d * S + (c + 1) * 512],
                      self.y[:, d * S + c * 512: d * S + (c + 1) * 512],
                      bk.ap[6], ALU.mult),
                  waits=bk.raw_wait(6) + bk.raw_wait(7) if d == 0 else ())
            pr.op("vector",
                  lambda v, d=d, c=c: v.tensor_tensor(
                      self.y[:, d * S + c * 512: d * S + (c + 1) * 512],
                      self.y[:, d * S + c * 512: d * S + (c + 1) * 512],
                      bk.ap[7], ALU.subtract))
            ve_d = self.s_ve.count
            if emit_ybf:
                pr.op("scalar",
                      lambda sx, d=d, c=c: sx.copy(
                          ybf[:, d * S + c * 512: d * S + (c + 1) * 512],
                          self.y[:, d * S + c * 512: d * S + (c + 1) * 512]),
                      waits=[(self.s_ve, ve_d)])
        bk.read(6, "vector")
        bk.read(7, "vector")
        st.setdefault("ve", {})[c] = self.s_ve.count
        st.setdefault("sc", {})[c] = self.s_sc.count

    def ln_end(self, st):
        tag = st["tag"]
        self.ar.free("sq_" + tag, "stm_" + tag, "stv_" + tag, "stt_" + tag)
        self.y_ready = ("ve", self.s_ve.count)
        return st["ybf"], self.s_sc.count

    def ln(self, tag, chunk_waits=None):
        st = self.ln_start(tag)
        for c in range(NCH):
            self.ln_chunk(st, c,
                          wait=None if chunk_waits is None else chunk_waits[c])
        return self.ln_end(st)

    def phase_A(self):
        pr, bk, ar, d = self.pr, self.banks, self.ar, self.d
        small = self.smallb
        v_aug = ar.alloc("v_aug", 16 * 1040 * 2, BF16, 16 * 1040)
        xbf = ar.alloc("xbf", DT * S * 2, BF16, DT * S)
        ckv = ar.alloc("ckv", 2 * S * 2, BF16, 2 * S)
        cos = ar.alloc("cos", S * 4, F32, S)
        sin = ar.alloc("sin", S * 4, F32, S)
        qr = ar.alloc("qr", S * 4, F32, S)
        kr = ar.alloc("kr", S * 4, F32, S)
        cq = ar.alloc("cq", 2 * S * 2, BF16, 2 * S)

        for dd in range(DT):
            w = []
            if dd == 0:
                w = [(self.s_x, self.s_x.count)]
            if dd == 4:
                w = [(self.s_x2, self.s_x2.count)]
            pr.op("scalar",
                  lambda s, dd=dd: s.copy(xbf[:, dd * S:(dd + 1) * S],
                                          self.y[:, dd * S:(dd + 1) * S]),
                  waits=w)
        xbf_done = self.s_sc.count

        for nm, t in small.items():
            pr.dma("sync", t, d[nm], self.s_small)
        pr.dma("sync", cos, d["ropecos"], self.s_small)
        pr.dma("sync", sin, d["ropesin"], self.s_small)
        masks = ar.alloc("masks", 4 * 512 * 2, BF16, 4 * 512, side="high")
        pr.dma("sync", masks, d["masks"], self.s_small)
        small_done = self.s_small.count

        wq = self.wload("wq", d["wq"], DT, DC, self.s_wA1)
        wqr = self.wload("wqr", d["wqr"], DT, DR, self.s_wA1)
        wkv = self.wload("wkv", d["wkv"], DT, DC, self.s_wA1)
        wkr = self.wload("wkr", d["wkr"], DT, DR, self.s_wA1)
        wA1_done = self.s_wA1.count

        def proj(dst, w, bias, mtiles, gelu):
            for m in range(mtiles):
                for c in range(NCH):
                    b = (m * NCH + c) % 2
                    waits = bk.war_waits(b) + [(self.s_wA1, wA1_done),
                                               (self.s_sc, xbf_done)]
                    for k in range(DT):
                        pr.op("tensor",
                              lambda t, b=b, m=m, c=c, k=k, w=w, mt=mtiles: t.matmul(
                                  bk.ap[b],
                                  w[:, k * mt * 128 + m * 128: k * mt * 128 + (m + 1) * 128],
                                  xbf[:, k * S + c * 512: k * S + (c + 1) * 512],
                                  start=(k == 0), stop=(k == DT - 1)),
                              waits=waits if k == 0 else (),
                              incs=[(self.s_mm, 1)] if k == DT - 1 else ())
                    bk.wrote(b)
                    fn = AF.Gelu if gelu else AF.Identity
                    pr.op("scalar",
                          lambda s, b=b, m=m, c=c, fn=fn, dst=dst, bias=bias: s.activation(
                              dst[:, m * S + c * 512: m * S + (c + 1) * 512],
                              bk.ap[b], fn, bias=bias[:, m:m + 1]),
                          waits=bk.raw_wait(b) + [(self.s_small, small_done)],
                          incs=[(self.s_sc, 1)])
                    bk.read(b, "scalar")

        proj(cq, wq, small["bq"], 2, True)
        proj(ckv, wkv, small["bkv"], 2, True)
        proj(qr, wqr, small["bqr"], 1, False)
        proj(kr, wkr, small["bkr"], 1, False)
        proj_done = self.s_sc.count
        ar.free("wq", "wqr", "wkv", "wkr")

        wv = self.wload("wv", d["wv"], 2, D, self.s_wA2,
                        waits=[(self.s_mm, self.s_mm.count),
                               (self.s_ve, self.s_ve.count),
                               (self.s_sc, self.s_sc.count)])
        wv_done = self.s_wA2.count
        for tt in range(16):
            pr.op("vector",
                  lambda v, tt=tt: v.memset(
                      v_aug[:, tt * 1040:(tt + 1) * 1040]
                      .rearrange("p (h c) -> p h c", h=16)[:, :, 64:65], 1.0))
            for n in range(2):
                b = 2 + (tt * 2 + n) % 2
                waits = bk.war_waits(b) + [(self.s_wA2, wv_done), (self.s_sc, proj_done)]
                for k in range(2):
                    pr.op("tensor",
                          lambda t, b=b, tt=tt, n=n, k=k: t.matmul(
                              bk.ap[b],
                              ckv[:, k * S + tt * 128:k * S + (tt + 1) * 128],
                              wv[:, k * D + n * 512: k * D + (n + 1) * 512],
                              start=(k == 0), stop=(k == 1)),
                          waits=waits if k == 0 else (),
                          incs=[(self.s_mm, 1)] if k == 1 else ())
                bk.wrote(b)
                pr.op("vector",
                      lambda v, b=b, tt=tt, n=n: v.tensor_copy(
                          v_aug[:, tt * 1040 + n * 8 * 65: tt * 1040 + (n + 1) * 8 * 65]
                          .rearrange("p (h c) -> p h c", h=8)[:, :, 0:64],
                          bk.ap[b].rearrange("p (h c) -> p h c", h=8)),
                      waits=bk.raw_wait(b), incs=[(self.s_ve, 1)])
                bk.read(b, "vector")
        v_done = self.s_ve.count
        mm_after_v = self.s_mm.count
        ar.free("wv")

        rotr = ar.alloc("rotr", 2 * 512 * 4, F32, 2 * 512, side="high")
        rot_free = {}
        for i, src in enumerate((qr, kr)):
            for c in range(NCH):
                par = (i * NCH + c) % 2
                rsem = self.s_rope[par]
                rsl = rotr[:, par * 512:(par + 1) * 512]
                w = [(self.s_sc, proj_done), (self.s_small, small_done),
                     (self.s_mm, mm_after_v)]
                if par in rot_free:
                    w.append((self.s_ve, rot_free[par]))
                pr.dma("sync", rsl[0:64, :],
                       src[64:128, c * 512:(c + 1) * 512], rsem, waits=w)
                pr.dma("sync", rsl[64:128, :],
                       src[0:64, c * 512:(c + 1) * 512], rsem)
                swap_done = (rsem, rsem.count)
                pr.op("vector",
                      lambda v, rsl=rsl, c=c: v.tensor_tensor(
                          rsl, rsl, sin[:, c * 512:(c + 1) * 512], ALU.mult),
                      waits=[swap_done])
                pr.op("vector",
                      lambda v, src=src, c=c: v.tensor_tensor(
                          src[:, c * 512:(c + 1) * 512],
                          src[:, c * 512:(c + 1) * 512],
                          cos[:, c * 512:(c + 1) * 512], ALU.mult))
                pr.op("vector",
                      lambda v, src=src, c=c, rsl=rsl: v.tensor_tensor(
                          src[:, c * 512:(c + 1) * 512],
                          src[:, c * 512:(c + 1) * 512], rsl, ALU.add),
                      incs=[(self.s_ve, 1)])
                rot_free[par] = self.s_ve.count
        rope_done = self.s_ve.count
        ar.free("rotr", "cos", "sin")

        ar.free("xbf")
        q_fm = ar.alloc("q_fm", DT * S * 2, BF16, DT * S)
        wqc = self.wload("wqc", d["wqc"], 2, D - DR, self.s_wA2,
                         waits=[(self.s_mm, self.s_mm.count),
                                (self.s_ve, self.s_ve.count),
                                (self.s_sc, self.s_sc.count)])

        def projc(dst, src, w, bias, w_done):
            for m in range(7):
                for c in range(NCH):
                    b = (m * NCH + c) % 2
                    waits = bk.war_waits(b) + [(self.s_wA2, w_done),
                                               (self.s_sc, proj_done)]
                    for k in range(2):
                        pr.op("tensor",
                              lambda t, b=b, m=m, c=c, k=k, w=w, src=src: t.matmul(
                                  bk.ap[b],
                                  w[:, k * 896 + m * 128: k * 896 + (m + 1) * 128],
                                  src[:, k * S + c * 512: k * S + (c + 1) * 512],
                                  start=(k == 0), stop=(k == 1)),
                              waits=waits if k == 0 else (),
                              incs=[(self.s_mm, 1)] if k == 1 else ())
                    bk.wrote(b)
                    pr.op("scalar",
                          lambda s, b=b, m=m, c=c, dst=dst, bias=bias: s.activation(
                              dst[:, m * S + c * 512: m * S + (c + 1) * 512],
                              bk.ap[b], AF.Gelu, bias=bias[:, m:m + 1]),
                          waits=bk.raw_wait(b) + [(self.s_small, small_done)],
                          incs=[(self.s_sc, 1)])
                    bk.read(b, "scalar")

        projc(q_fm, cq, wqc, small["bqc"], self.s_wA2.count)
        pr.op("scalar", lambda s: s.activation(q_fm[:, 7 * S:8 * S], qr, AF.Gelu),
              waits=[(self.s_ve, rope_done)], incs=[(self.s_sc, 1)])
        ar.free("cq", "qr", "wqc")

        wkc = self.wload("wkc", d["wkc"], 2, D - DR, self.s_wA2,
                         waits=[(self.s_mm, self.s_mm.count),
                                (self.s_ve, self.s_ve.count),
                                (self.s_sc, self.s_sc.count)])
        k7 = ar.alloc("k7", S * 2, BF16, S, side="high")
        pr.op("scalar", lambda s: s.activation(k7, kr, AF.Gelu),
              waits=[(self.s_ve, rope_done)], incs=[(self.s_sc, 1)])
        k7_done = self.s_sc.count
        ar.free("kr")
        k_fm = ar.alloc("k_fm", DT * S * 2, BF16, DT * S)
        projc(k_fm, ckv, wkc, small["bkc"], self.s_wA2.count)
        pr.op("vector", lambda v: v.tensor_copy(k_fm[:, 7 * S:8 * S], k7),
              waits=[(self.s_sc, k7_done)], incs=[(self.s_ve, 1)])
        qk_done_sc = self.s_sc.count
        qk_done_ve = self.s_ve.count
        ar.free("k7", "wkc")

        ar.free("ckv")

        sqt = ar.alloc("sqt_mh", 512 * 2, BF16, 512, side="high")
        pr.op("vector", lambda v: v.memset(self.mq, 0.0))
        pr.op("vector", lambda v: v.memset(self.mk, 0.0), incs=[(self.s_ve, 1)])
        for src, dst, rdy in ((q_fm, self.mq, qk_done_sc), (k_fm, self.mk, qk_done_ve)):
            for dd in range(DT):
                for c in range(NCH):
                    w = ([(self.s_sc, qk_done_sc), (self.s_ve, qk_done_ve),
                          (self.s_mm, mm_after_v)]
                         if (dd == 0 and c == 0) else [])
                    w += [(self.s_mm, bk.mm_done[4])]
                    pr.op("scalar",
                          lambda s, dd=dd, c=c, src=src: s.activation(
                              sqt, src[:, dd * S + c * 512: dd * S + (c + 1) * 512],
                              AF.Square),
                          waits=w, incs=[(self.s_sc, 1)])
                    sqd = self.s_sc.count
                    w4 = bk.war_waits(4) + [(self.s_sc, sqd)]
                    pr.op("tensor",
                          lambda t: t.matmul(bk.ap[4][0:1, :], self.ones_bf, sqt,
                                             start=True, stop=True),
                          waits=w4, incs=[(self.s_mm, 1)])
                    bk.wrote(4)
                    pr.op("vector",
                          lambda v, c=c: v.reduce_max(self.red[:, c:c + 1],
                                                      bk.ap[4][0:1, :], axis=AX.X),
                          waits=bk.raw_wait(4))
                    bk.read(4, "vector")
                    pr.op("vector",
                          lambda v, dd=dd, c=c, dst=dst: v.tensor_tensor(
                              dst[:, dd:dd + 1], dst[:, dd:dd + 1],
                              self.red[:, c:c + 1], ALU.max),
                          incs=[(self.s_ve, 1)])
        ar.free("sqt_mh")
        stat_done = self.s_ve.count
        pr.op("scalar", lambda s: s.activation(self.mq, self.mq, AF.Sqrt),
              waits=[(self.s_ve, stat_done)])
        pr.op("scalar", lambda s: s.activation(self.mk, self.mk, AF.Sqrt),
              incs=[(self.s_sc, 1)])
        pr.op("vector", lambda v: v.tensor_tensor(self.mq, self.mq, self.mk, ALU.mult),
              waits=[(self.s_sc, self.s_sc.count)])
        pr.op("vector", lambda v: v.tensor_scalar(self.mq, self.mq, -0.125, None, ALU.mult),
              incs=[(self.s_ve, 1)])
        w4 = bk.war_waits(4) + [(self.s_ve, self.s_ve.count)]
        pr.op("tensor", lambda t: t.matmul(bk.ap[4][:, 0:DT], self.ones_f1, self.mq,
                                           start=True, stop=True),
              waits=w4, incs=[(self.s_mm, 1)])
        bk.wrote(4)
        pr.op("scalar", lambda s: s.copy(self.negmb, bk.ap[4][:, 0:DT]),
              waits=bk.raw_wait(4), incs=[(self.s_sc, 1)])
        bk.read(4, "scalar")
        negmb_done = self.s_sc.count

        self.rz = ar.alloc("rz", 2 * 512 * 4, F32, 2 * 512, side="high")[0:1, :]
        self.rzb = ar.alloc("rzb", 2 * 512 * 4, F32, 2 * 512, side="high")[0:64, :]
        self.ctx32 = ar.alloc("ctx32", 2 * 512 * 4, F32, 2 * 512, side="high")[0:64, :]
        self.ctx16 = ar.alloc("ctx16", 2 * 512 * 2, BF16, 2 * 512, side="high")[0:64, :]
        masks_done = small_done
        wo = self.wload("wo", d["wo"], DT, D, self.s_wA2,
                        waits=[(self.s_mm, self.s_mm.count),
                               (self.s_ve, self.s_ve.count),
                               (self.s_sc, self.s_sc.count)])
        wo_done = self.s_wA2.count
        E = ar.alloc("E", 8 * 512 * 2, BF16, 8 * 512, side="high")
        ectx = {}
        ei = 0
        ctx16_dma = {}
        prev = None
        SCB = (0, 1, 6, 7)

        def tail_rz(u):
            for i, cb in enumerate(u["cb"]):
                pr.op("scalar", lambda sx, cb=cb, i=i: sx.activation(
                    self.rz[:, i * 512:(i + 1) * 512], bk.ap[cb][64:65, :], AF.Ln),
                    waits=bk.raw_wait(cb))
                pr.op("scalar", lambda sx, i=i: sx.activation(
                    self.rz[:, i * 512:(i + 1) * 512],
                    self.rz[:, i * 512:(i + 1) * 512], AF.Exp, scale=-1.0))
                bk.read(cb, "scalar")
            u["rz_sc"] = self.s_sc.count

        def tail_bcast(u, i, bb):
            w5 = bk.war_waits(bb) + [(self.s_sc, u["rz_sc"])]
            pr.op("tensor", lambda t, bb=bb, i=i: t.matmul(
                bk.ap[bb][0:64, :], self.ones_f1[:, 0:64],
                self.rz[:, i * 512:(i + 1) * 512], start=True, stop=True),
                waits=w5, incs=[(self.s_mm, 1)])
            bk.wrote(bb)
            u["bb%d" % i] = bb

        def tail_evac1(u):
            for i, cb in enumerate(u["cb"]):
                bb = u["bb%d" % i]
                pr.op("scalar", lambda sx, bb=bb, i=i: sx.copy(
                    self.rzb[:, i * 512:(i + 1) * 512], bk.ap[bb][0:64, :]),
                    waits=bk.raw_wait(bb))
                bk.read(bb, "scalar")
                pr.op("vector",
                      lambda v, cb=cb, i=i: v.tensor_tensor(
                          self.ctx32[:, i * 512:(i + 1) * 512], bk.ap[cb][0:64, :],
                          self.rzb[:, i * 512:(i + 1) * 512], ALU.mult),
                      waits=[(self.s_sc, self.s_sc.count)])
                bk.read(cb, "vector")
            u["tt_ve"] = self.s_ve.count

        def tail_evac2(u):
            dd, qc = u["dd"], u["qc"]
            pr.op("scalar",
                  lambda sx, dd=dd, qc=qc: sx.activation(
                      q_fm[0:64, dd * S + qc * 512: dd * S + (qc + 1) * 512],
                      self.ctx32[:, 0:512], AF.Identity,
                      bias=small["bv"][0:64, dd:dd + 1]),
                  waits=[(self.s_ve, u["tt_ve"])])
            par = (dd * NCH + qc) % 2
            csem = self.s_ctx[par]
            w = []
            if par in ctx16_dma:
                w.append((csem, ctx16_dma[par]))
            pr.op("scalar",
                  lambda sx, dd=dd, par=par: sx.activation(
                      self.ctx16[:, par * 512:(par + 1) * 512],
                      self.ctx32[:, 512:1024], AF.Identity,
                      bias=small["bv"][64:128, dd:dd + 1]),
                  waits=w)
            pr.dma("sync",
                   q_fm[64:128, dd * S + qc * 512: dd * S + (qc + 1) * 512],
                   self.ctx16[:, par * 512:(par + 1) * 512],
                   csem, waits=[(self.s_sc, self.s_sc.count)])
            ctx16_dma[par] = csem.count

        def emit_sT(kb, dd, qc, first):
            nonlocal ei
            out = []
            for i in range(2):
                sb_ = SCB[(2 * ei + i) % 4]
                ew = (2 * ei + i) % 8
                po = i * 64
                waits = bk.war_waits(sb_) + (
                    [(self.s_sc, qk_done_sc), (self.s_ve, v_done)]
                    if (first and i == 0) else [])
                pr.op("tensor",
                      lambda t, sb_=sb_, dd=dd, po=po, kb=kb, qc=qc, i=i: t.matmul(
                          bk.ap[sb_],
                          k_fm[po:po + 64, dd * S + kb * 128: dd * S + (kb + 1) * 128],
                          q_fm[po:po + 64, dd * S + qc * 512: dd * S + (qc + 1) * 512],
                          start=True, stop=True),
                      waits=waits, incs=[(self.s_mm, 1)])
                bk.wrote(sb_)
                ew_war = [(self.s_mm, ectx[ew])] if ew in ectx else []
                pr.op("scalar",
                      lambda sx, sb_=sb_, ew=ew, dd=dd: sx.activation(
                          E[:, ew * 512:(ew + 1) * 512], bk.ap[sb_], AF.Exp,
                          bias=self.negmb[:, dd:dd + 1], scale=QSCALE),
                      waits=bk.raw_wait(sb_) + ew_war +
                      ([(self.s_sc, negmb_done)] if (first and i == 0) else []))
                bk.read(sb_, "scalar")
                e_ready = self.s_sc.count
                mask_j = kb - 4 * qc
                if mask_j >= 0:
                    pr.op("vector",
                          lambda v, ew=ew, j=mask_j: v.tensor_tensor(
                              E[:, ew * 512:(ew + 1) * 512],
                              E[:, ew * 512:(ew + 1) * 512],
                              masks[:, j * 512:(j + 1) * 512], ALU.mult),
                          waits=[(self.s_sc, e_ready)] +
                          ([(self.s_small, masks_done)] if ei == 0 else []))
                    ewait = [(self.s_ve, self.s_ve.count)]
                else:
                    ewait = [(self.s_sc, e_ready)]
                out.append((ew, ewait))
            ei += 1
            return out

        def emit_ctx(cbs, kb, dd, pair, nkb, ctx_war):
            for i in range(2):
                ew, ewait = pair[i]
                h = 2 * dd + i
                cb = cbs[i]
                pr.op("tensor",
                      lambda t, cb=cb, kb=kb, h=h, ew=ew, nkb=nkb: t.matmul(
                          bk.ap[cb][0:65, :],
                          v_aug[:, kb * 1040 + h * 65: kb * 1040 + (h + 1) * 65],
                          E[:, ew * 512:(ew + 1) * 512],
                          start=(kb == 0), stop=(kb == nkb - 1)),
                      waits=ewait + (ctx_war[i] if kb == 0 else []),
                      incs=[(self.s_mm, 1)])
                ectx[ew] = self.s_mm.count

        for dd in range(DT):
            for qc in range(NCH):
                nkb = 4 * (qc + 1)
                ui = dd * NCH + qc
                cbs = (2, 3) if ui % 2 == 0 else (4, 5)
                ctx_war = [bk.war_waits(cbs[0]), bk.war_waits(cbs[1])]
                pend = []
                if prev is not None:
                    tail_rz(prev)
                for kb in range(nkb):
                    pair = emit_sT(kb, dd, qc, ei == 0)
                    pend.append((kb, pair))
                    if kb == 1 and prev is not None:
                        tail_bcast(prev, 0, SCB[(2 * ei) % 4])
                        tail_bcast(prev, 1, SCB[(2 * ei + 1) % 4])
                        tail_evac1(prev)
                    if kb == 2 and prev is not None:
                        tail_evac2(prev)
                    if len(pend) > 2:
                        kbp, pairp = pend.pop(0)
                        emit_ctx(cbs, kbp, dd, pairp, nkb, ctx_war)
                while pend:
                    kbp, pairp = pend.pop(0)
                    emit_ctx(cbs, kbp, dd, pairp, nkb, ctx_war)
                bk.wrote(cbs[0])
                bk.wrote(cbs[1])
                prev = {"cb": cbs, "dd": dd, "qc": qc}
        tail_rz(prev)
        tail_bcast(prev, 0, SCB[0])
        tail_bcast(prev, 1, SCB[1])
        tail_evac1(prev)
        tail_evac2(prev)
        ctx_sc = self.s_sc.count
        ar.free("k_fm", "v_aug", "E", "masks", "rz", "rzb", "ctx32", "ctx16")

        stA = self.ln_start("a")
        for c in range(NCH):
            for m in range(DT):
                b = (c * DT + m) % 4
                waits = bk.war_waits(b) + [(self.s_wA2, wo_done), (self.s_sc, ctx_sc),
                                           (self.s_ctx[0], self.s_ctx[0].count),
                                           (self.s_ctx[1], self.s_ctx[1].count)]
                for k in range(DT):
                    pr.op("tensor",
                          lambda t, b=b, m=m, c=c, k=k: t.matmul(
                              bk.ap[b],
                              wo[:, k * D + m * 128: k * D + (m + 1) * 128],
                              q_fm[:, k * S + c * 512: k * S + (c + 1) * 512],
                              start=(k == 0), stop=(k == DT - 1)),
                          waits=waits if k == 0 else (),
                          incs=[(self.s_mm, 1)] if k == DT - 1 else ())
                bk.wrote(b)
                pr.op("vector",
                      lambda v, b=b, m=m, c=c: v.scalar_tensor_tensor(
                          self.y[:, m * S + c * 512: m * S + (c + 1) * 512],
                          bk.ap[b], small["bo"][:, m:m + 1],
                          self.y[:, m * S + c * 512: m * S + (c + 1) * 512],
                          ALU.add, ALU.add),
                      waits=bk.raw_wait(b) + [(self.s_small, small_done)],
                      incs=[(self.s_ve, 1)])
                bk.read(b, "vector")
            self.ln_chunk(stA, c, wait=(self.s_ve, self.s_ve.count))
        ar.free("q_fm", "wo")
        ybf_a, done_a = self.ln_end(stA)
        return ybf_a, done_a, stA

    def mk_ybf(self, tag):
        pr, ar = self.pr, self.ar
        ybf = ar.alloc("ybf_" + tag, DT * S * 2, BF16, DT * S)
        for dd in range(DT):
            pr.op("scalar",
                  lambda s, dd=dd: s.copy(ybf[:, dd * S:(dd + 1) * S],
                                          self.y[:, dd * S:(dd + 1) * S]),
                  waits=self.ywait() if dd == 0 else ())
        return ybf, self.s_sc.count

    def phase_B_dense(self, ybf, ybf_done):
        pr, bk, ar, d = self.pr, self.banks, self.ar, self.d
        s_wg = pr.sem("wgld")
        s_bc = pr.sem("bcld")
        s_wB = [pr.sem("wB0"), pr.sem("wB1")]
        s_wsc = pr.sem("wscr")
        s_wbc = [pr.sem("wbcA"), pr.sem("wbcB")]

        cw = [(self.s_mm, self.s_mm.count), (self.s_ve, self.s_ve.count),
              (self.s_sc, self.s_sc.count)]
        identf = ar.alloc("identf", 128 * 4, F32, 128, side="high")
        self.oneh = ar.alloc("oneh", 32 * 128 * 2, BF16, 32 * 128, side="high")[0:32, :]
        pr.dma("sync", identf, d["identf"], s_bc, waits=cw)
        pr.dma("sync", self.oneh, d["oneh"], s_bc)
        wg = ar.alloc("wgmat", DT * EXP * 4, F32, DT * EXP, side="high")
        pr.dma("sync", wg.rearrange("p (a o) -> p a o", a=DT),
               d["Wg"].rearrange("(a p) o -> p a o", p=128), s_wg,
               waits=[(self.s_mm, self.s_mm.count), (self.s_ve, self.s_ve.count),
                      (self.s_sc, self.s_sc.count)])
        wg_done = s_wg.count
        bc_done = s_bc.count

        w_tm = ar.alloc("w_tm", 16 * EXP * 4, F32, 16 * EXP, side="high")
        w_fmT = ar.alloc("w_fmT", 16 * 128 * 2, BF16, 16 * 128, side="high")[0:32, :]
        gb, pw, msk, sel, mx1 = self.gb, self.pw, self.msk, self.sel, self.mx1

        w1b = [ar.alloc(f"w1b{p}", DT * HEXP * 2, BF16, DT * HEXP, side="high") for p in range(2)]
        w3b = [ar.alloc(f"w3b{p}", DT * HEXP * 2, BF16, DT * HEXP, side="high") for p in range(2)]
        w2b = [ar.alloc(f"w2b{p}", 4 * D * 2, BF16, 4 * D, side="high") for p in range(2)]
        wbcb = [ar.alloc(f"wbc{p}", S * 2, BF16, S, side="high") for p in range(2)]
        silu_s = ar.alloc("silu_s", 4 * 512 * 2, BF16, 4 * 512, side="high")
        hb = ar.alloc("hb", 4 * 512 * 2, BF16, 4 * 512, side="high")

        gball = ar.alloc("gball", 512 * 4, F32, 512, side="high")
        gsub = ar.alloc("gsub", 512 * 4, F32, 512, side="high")
        pwb = ar.alloc("pwb", 512 * 4, F32, 512, side="high")
        mskb = ar.alloc("mskb", 512 * 4, F32, 512, side="high")
        rst = ar.alloc("rst", 3 * 16 * 4, F32, 3 * 16, side="high")

        def routing_mms():
            for tt in range(16):
                b = 4 + tt % 2
                waits = bk.war_waits(b) + [(s_wg, wg_done)] + yw_chunk(tt // 4)
                for k in range(DT):
                    pr.op("tensor",
                          lambda t, b=b, tt=tt, k=k: t.matmul(
                              bk.ap[b][:, 0:EXP],
                              self.y[:, k * S + tt * 128:k * S + (tt + 1) * 128],
                              wg[:, k * EXP:(k + 1) * EXP],
                              start=(k == 0), stop=(k == DT - 1)),
                          waits=waits if k == 0 else (),
                          incs=[(self.s_mm, 1)] if k == DT - 1 else ())
                bk.wrote(b)
                pr.op("scalar",
                      lambda sx, b=b, tt=tt: sx.copy(gball[:, tt * 32:(tt + 1) * 32],
                                                     bk.ap[b][:, 0:EXP]),
                      waits=bk.raw_wait(b))
                bk.read(b, "scalar")
            return self.s_sc.count

        def routing_top4(g_sc):
            g3 = gball.rearrange("p (t e) -> p t e", t=16)
            pw3 = pwb.rearrange("p (t e) -> p t e", t=16)
            msk3 = mskb.rearrange("p (t e) -> p t e", t=16)
            w3 = w_tm.rearrange("p (t e) -> p t e", t=16)
            mx = rst[:, 0:16].rearrange("p (t o) -> p t o", o=1)
            m2 = rst[:, 16:32].rearrange("p (t o) -> p t o", o=1)
            bc = lambda apx: apx.broadcast_to([128, 16, 32])
            V, SX = "vector", "scalar"
            pr.op(V, lambda v: v.reduce_max(mx, g3, axis=AX.X),
                  waits=[(self.s_sc, g_sc)])
            pr.op(V, lambda v: v.tensor_tensor(
                gsub.rearrange("p (t e) -> p t e", t=16), g3, bc(mx), ALU.subtract))
            ve = self.s_ve.count
            pr.op(SX, lambda sx: sx.activation(gsub, gsub, AF.Exp),
                  waits=[(self.s_ve, ve)])
            e_sc = self.s_sc.count
            eg3 = gsub.rearrange("p (t e) -> p t e", t=16)
            pr.op(V, lambda v: v.reduce_sum(m2, eg3, axis=AX.X),
                  waits=[(self.s_sc, e_sc)])
            pr.op(V, lambda v: v.reciprocal(rst[:, 16:32], rst[:, 16:32]))
            pr.op(V, lambda v: v.tensor_tensor(w3, eg3, bc(m2), ALU.mult))
            pr.op(V, lambda v: v.tensor_copy(pwb, gball))
            for it in range(3):
                pr.op(V, lambda v: v.reduce_max(mx, pw3, axis=AX.X))
                pr.op(V, lambda v: v.tensor_tensor(msk3, pw3, bc(mx), ALU.is_ge))
                pr.op(V, lambda v: v.scalar_tensor_tensor(
                    pwb, mskb, -1e30, pwb, ALU.mult, ALU.add))
            pr.op(V, lambda v: v.reduce_max(mx, pw3, axis=AX.X))
            pr.op(V, lambda v: v.tensor_tensor(msk3, g3, bc(mx), ALU.is_ge))
            pr.op(V, lambda v: v.tensor_tensor(w_tm, w_tm, mskb, ALU.mult))
            return self.s_ve.count

        def routing_transposes(w_ve):
            for tt in range(16):
                b2 = 6 + tt % 2
                w2w = bk.war_waits(b2) + ([(self.s_ve, w_ve), (s_bc, bc_done)]
                                          if tt == 0 else [])
                pr.op("tensor",
                      lambda t, b2=b2, tt=tt: t.transpose(
                          bk.ap[b2][0:32, 0:128], w_tm[:, tt * EXP:(tt + 1) * EXP], identf),
                      waits=w2w, incs=[(self.s_mm, 1)])
                bk.wrote(b2)
                pr.op("scalar",
                      lambda sx, b2=b2, tt=tt: sx.copy(
                          w_fmT[:, tt * 128:(tt + 1) * 128], bk.ap[b2][0:32, 0:128]),
                      waits=bk.raw_wait(b2))
                bk.read(b2, "scalar")
            return self.s_sc.count

        def routing_wscr(r_sc):
            pr.dma("sync", self.w_scr, w_fmT, s_wsc, waits=[(self.s_sc, r_sc)])
            return s_wsc.count

        def emit_load(ei):
            par = ei % 2
            lw = [(self.s_mm, mm_end[ei - 2])] if ei - 2 in mm_end else [
                (self.s_mm, self.s_mm.count), (self.s_ve, self.s_ve.count),
                (self.s_sc, self.s_sc.count)]
            if ei < 2:
                half = ei
                src1 = d["Ws1"][:, half * HEXP:(half + 1) * HEXP]
                src3 = d["Ws3"][:, half * HEXP:(half + 1) * HEXP]
                src2 = d["Ws2"][half * HEXP:(half + 1) * HEXP, :]
            else:
                e = ei - 2
                src1, src3, src2 = d["W1"][e], d["W3"][e], d["W2"][e]
            pr.dma("gpsimd", w1b[par].rearrange("p (a o) -> p a o", a=DT),
                   src1.rearrange("(a p) o -> p a o", p=128), s_wB[par], waits=lw)
            pr.dma("gpsimd", w3b[par].rearrange("p (a o) -> p a o", a=DT),
                   src3.rearrange("(a p) o -> p a o", p=128), s_wB[par])
            pr.dma("gpsimd", w2b[par].rearrange("p (a o) -> p a o", a=4),
                   src2.rearrange("(a p) o -> p a o", p=128), s_wB[par])
            if ei >= 4:
                e = ei - 2
                wl = [(s_wsc, s_wsc.count)]
                if wbc_gp_free[par] > 0:
                    wl.append((self.s_gp, wbc_gp_free[par]))
                pr.dma("sync", wbcb[par],
                       self.w_scr[e:e + 1, :].broadcast_to([128, S]),
                       s_wbc[par], waits=wl)
                wbc_loaded[ei] = s_wbc[par].count
            wB_loaded[ei] = s_wB[par].count

        wB_loaded = {}
        wbc_loaded = {}
        mm_end = {}
        wB_cnt = [0, 0]
        wbc_free = [0, 0]
        wbc_gp_free = [0, 0]

        def emit_expert(ei):
            par = ei % 2
            shared = ei < 2
            e = ei - 2
            wB_cnt[par] = wB_loaded[ei]

            wbc_done = wbc_loaded.get(ei, 0)

            NCH2 = 8
            pend_c = None
            hw_gp_c = {}
            hready = {}

            def emit_w13(c):
                for m in range(4):
                    b = m // 2
                    off = (m % 2) * 256
                    w = bk.war_waits(b) + [(s_wB[par], wB_cnt[par])]
                    if ei == 0 and c == 0 and m == 0:
                        w.append((self.s_sc, ybf_done))
                    for k in range(DT):
                        pr.op("tensor",
                              lambda t, b=b, off=off, m=m, c=c, k=k, par=par: t.matmul(
                                  bk.ap[b][:, off:off + 256],
                                  w1b[par][:, k * HEXP + m * 128: k * HEXP + (m + 1) * 128],
                                  ybf[:, k * S + c * 256: k * S + (c + 1) * 256],
                                  start=(k == 0), stop=(k == DT - 1)),
                              waits=w if k == 0 else (),
                              incs=[(self.s_mm, 1)] if k == DT - 1 else ())
                    if m % 2 == 1:
                        bk.wrote(b)
                for m in range(4):
                    b = 2 + m // 2
                    off = (m % 2) * 256
                    w = bk.war_waits(b) + [(s_wB[par], wB_cnt[par])]
                    for k in range(DT):
                        pr.op("tensor",
                              lambda t, b=b, off=off, m=m, c=c, k=k, par=par: t.matmul(
                                  bk.ap[b][:, off:off + 256],
                                  w3b[par][:, k * HEXP + m * 128: k * HEXP + (m + 1) * 128],
                                  ybf[:, k * S + c * 256: k * S + (c + 1) * 256],
                                  start=(k == 0), stop=(k == DT - 1)),
                              waits=w if k == 0 else (),
                              incs=[(self.s_mm, 1)] if k == DT - 1 else ())
                    if m % 2 == 1:
                        bk.wrote(b)

            def emit_h(c):
                w2mm = self.s_mm.count
                prev_hv = hready[c - 1][0] if c - 1 in hready else 0
                sil_m = []
                for b in range(2):
                    pr.op("scalar",
                          lambda sx, b=b: sx.activation(
                              silu_s[:, b * 512:(b + 1) * 512], bk.ap[b], AF.Silu),
                          waits=bk.raw_wait(b) +
                          ([(self.s_ve, prev_hv)] if b == 0 else []))
                    bk.read(b, "scalar")
                    sil_m.append(self.s_sc.count)
                for b in range(2):
                    pr.op("vector",
                          lambda v, b=b: v.tensor_tensor(
                              hb[:, b * 512:(b + 1) * 512],
                              silu_s[:, b * 512:(b + 1) * 512],
                              bk.ap[2 + b], ALU.mult),
                          waits=bk.raw_wait(2 + b) + [(self.s_sc, sil_m[b]),
                                                      (self.s_mm, w2mm)])
                    bk.read(2 + b, "vector")
                hv = self.s_ve.count
                if not shared:
                    for m in range(4):
                        pr.op("gpsimd",
                              lambda g, m=m, par=par, c=c: g.tensor_tensor(
                                  hb[:, m * 256:(m + 1) * 256],
                                  hb[:, m * 256:(m + 1) * 256],
                                  wbcb[par][:, c * 256:(c + 1) * 256],
                                  ALU.mult),
                              waits=[(self.s_ve, hv - (1 - m // 2))] +
                              ([(s_wbc[par], wbc_done)] if (c == 0 and m == 0) else []))
                hready[c] = (self.s_ve.count, self.s_gp.count)

            def emit_w2(c):
                hv, hg = hready[c]
                for m2 in range(DT):
                    b = 4 + m2 // 2
                    off = (m2 % 2) * 256
                    w = bk.war_waits(b) + [(self.s_ve, hv), (self.s_gp, hg)]
                    for k in range(4):
                        pr.op("tensor",
                              lambda t, b=b, off=off, m2=m2, k=k, c=c, par=par: t.matmul(
                                  bk.ap[b][:, off:off + 256],
                                  w2b[par][:, k * D + m2 * 128: k * D + (m2 + 1) * 128],
                                  hb[:, k * 256:(k + 1) * 256],
                                  start=(k == 0), stop=(k == 3)),
                              waits=w if k == 0 else (),
                              incs=[(self.s_mm, 1)] if k == 3 else ())
                    if m2 % 2 == 1:
                        bk.wrote(b)
                for bb in range(4):
                    b = 4 + bb
                    pr.op("vector",
                          lambda v, b=b, bb=bb, c=c: v.tensor_tensor(
                              self.y.rearrange("p (a t) -> p a t", a=DT)
                              [:, 2 * bb:2 * bb + 2, c * 256:(c + 1) * 256],
                              self.y.rearrange("p (a t) -> p a t", a=DT)
                              [:, 2 * bb:2 * bb + 2, c * 256:(c + 1) * 256],
                              bk.ap[b].rearrange("p (a t) -> p a t", a=2),
                              ALU.add),
                          waits=bk.raw_wait(b))
                    bk.read(b, "vector")

            for c in range(NCH2):
                emit_w13(c)
                if pend_c is not None:
                    emit_w2(pend_c)
                emit_h(c)
                pend_c = c
            emit_w2(pend_c)
            mm_end[ei] = self.s_mm.count
            wbc_free[par] = 0
            wbc_gp_free[par] = self.s_gp.count
            if ei + 2 < 34:
                emit_load(ei + 2)

        emit_load(0)
        emit_load(1)
        g_sc = routing_mms()
        top4_ve = routing_top4(g_sc)
        emit_expert(0)
        tr_sc = routing_transposes(top4_ve)
        wsc_done = routing_wscr(tr_sc)
        for _e in (2, 3):
            par2 = _e % 2
            pr.dma("sync", wbcb[par2],
                   self.w_scr[_e - 2:_e - 1, :].broadcast_to([128, S]),
                   s_wbc[par2], waits=[(s_wsc, wsc_done)])
            wbc_loaded[_e] = s_wbc[par2].count
        emit_expert(1)
        for ei in range(2, 34):
            emit_expert(ei)
        ar.free("w1b0", "w1b1", "w3b0", "w3b1", "w2b0", "w2b1", "wbc0", "wbc1",
                "silu_s", "hb", "wgmat", "identf", "oneh", "w_tm", "w_fmT")
        self.y_ready = ("ve", self.s_ve.count)

    def phase_B(self, ybf, ybf_done, ln_hook=None, ln_st=None):
        def yw_chunk(c512):
            if ln_st is not None:
                return [(self.s_ve, ln_st["ve"][c512]), (self.s_sc, ln_st["sc"][c512])]
            return self.ywait() + [(self.s_sc, ybf_done)]
        pr, bk, ar, d = self.pr, self.banks, self.ar, self.d
        s_wg = pr.sem("wgld")
        s_wB = [pr.sem("wB0g"), pr.sem("wB1g")]
        s_w2 = pr.sem("w2g")
        s_xz = pr.sem("xzero")
        s_sct = [pr.sem("sct0"), pr.sem("sct1")]
        s_xg = [pr.sem("xg0"), pr.sem("xg1")]
        s_oe = pr.sem("oest")
        s_cg = [pr.sem("cg0"), pr.sem("cg1"), pr.sem("cg2"), pr.sem("cg3")]

        cw = [(self.s_mm, self.s_mm.count), (self.s_ve, self.s_ve.count),
              (self.s_sc, self.s_sc.count)]
        identf = ar.alloc("identf", 128 * 4, F32, 128, side="high")
        iotae = ar.alloc("iotae", 16 * EXP * 4, F32, 16 * EXP, side="high")
        wg = ar.alloc("wgmat", DT * EXP * 4, F32, DT * EXP, side="high")
        pr.dma("sync", identf, d["identf"], s_wg, waits=cw)
        pr.dma("sync", iotae, d["iotae"], s_wg)
        pr.dma("sync", wg.rearrange("p (a o) -> p a o", a=DT),
               d["Wg"].rearrange("(a p) o -> p a o", p=128), s_wg)
        identb = ar.alloc("identb", 128 * 2, BF16, 128, side="high")
        pr.dma("sync", identb, d["identb"], s_wg)
        wg_done = s_wg.count

        sidx = self.sb("sidx", [128, 4 * 16], I32)
        wr = self.sb("wr", [128, 4 * 16], F32)

        w_tm = ar.alloc("w_tm", 16 * EXP * 4, F32, 16 * EXP, side="high")
        w_fmT = ar.alloc("w_fmT", 16 * 128 * 2, BF16, 16 * 128, side="high")[0:32, :]
        gball = ar.alloc("gball", 512 * 4, F32, 512, side="high")
        gsub = ar.alloc("gsub", 512 * 4, F32, 512, side="high")
        pwb = ar.alloc("pwb", 512 * 4, F32, 512, side="high")
        mskb = ar.alloc("mskb", 512 * 4, F32, 512, side="high")
        mxs = ar.alloc("mxs", 4 * 16 * 4, F32, 4 * 16, side="high")
        rs1 = ar.alloc("rs1", 16 * 4, F32, 16, side="high")
        pos_tm = ar.alloc("pos_tm", 16 * EXP * 4, F32, 16 * EXP, side="high")

        w1b = [ar.alloc(f"w1g{p}", DT * HEXP * 2, BF16, DT * HEXP, side="high") for p in range(2)]
        w3b = [ar.alloc(f"w3g{p}", DT * HEXP * 2, BF16, DT * HEXP, side="high") for p in range(2)]
        w2b = ar.alloc("w2g", 4 * D * 2, BF16, 4 * D, side="high")
        silu_s = ar.alloc("silu_sg", 4 * 512 * 2, BF16, 4 * 512, side="high")
        hb = ar.alloc("hbg", 4 * 512 * 2, BF16, 4 * 512, side="high")

        mm_end = {}
        wB_cnt = {}
        w2_cnt = {}

        def load13(ei):
            par = ei % 2
            lw = [(self.s_mm, mm_end[ei - 2])] if ei - 2 in mm_end else cw
            if ei < 2:
                src1 = d["Ws1"][:, ei * HEXP:(ei + 1) * HEXP]
                src3 = d["Ws3"][:, ei * HEXP:(ei + 1) * HEXP]
            else:
                src1, src3 = d["W1"][ei - 2], d["W3"][ei - 2]
            pr.dma("gpsimd", w1b[par].rearrange("p (a o) -> p a o", a=DT),
                   src1.rearrange("(a p) o -> p a o", p=128), s_wB[par], waits=lw)
            pr.dma("gpsimd", w3b[par].rearrange("p (a o) -> p a o", a=DT),
                   src3.rearrange("(a p) o -> p a o", p=128), s_wB[par])
            wB_cnt[ei] = s_wB[par].count

        def loadw2(ei):
            lw = [(self.s_mm, mm_end[ei - 1])] if ei - 1 in mm_end else cw
            src2 = (d["Ws2"][ei * HEXP:(ei + 1) * HEXP, :] if ei < 2
                    else d["W2"][ei - 2])
            pr.dma("gpsimd", w2b.rearrange("p (a o) -> p a o", a=4),
                   src2.rearrange("(a p) o -> p a o", p=128), s_w2, waits=lw)
            w2_cnt[ei] = s_w2.count

        zt = ar.alloc("zt", 1024 * 2, BF16, 1024, side="high")
        pr.op("vector", lambda v: v.memset(zt, 0.0))
        zt_ve = self.s_ve.count
        nxt = (EXP * CAP + 128) // 128
        xall_t = self.Xall.rearrange("(g p) d -> g p d", p=128)
        for g in range(nxt):
            pr.dma("sync", xall_t[g], zt, s_xz,
                   waits=[(self.s_ve, zt_ve)] if g == 0 else ())
        pr.dma("sync", self.OEall.rearrange("(g p) d -> g p d", p=128)[nxt - 1],
               zt, s_xz)
        xz_done = s_xz.count

        def routing_mms():
            for tt in range(16):
                b = 4 + tt % 2
                waits = bk.war_waits(b) + [(s_wg, wg_done)] + yw_chunk(tt // 4)
                for k in range(DT):
                    pr.op("tensor",
                          lambda t, b=b, tt=tt, k=k: t.matmul(
                              bk.ap[b][:, 0:EXP],
                              self.y[:, k * S + tt * 128:k * S + (tt + 1) * 128],
                              wg[:, k * EXP:(k + 1) * EXP],
                              start=(k == 0), stop=(k == DT - 1)),
                          waits=waits if k == 0 else (),
                          incs=[(self.s_mm, 1)] if k == DT - 1 else ())
                bk.wrote(b)
                pr.op("scalar",
                      lambda sx, b=b, tt=tt: sx.copy(gball[:, tt * 32:(tt + 1) * 32],
                                                     bk.ap[b][:, 0:EXP]),
                      waits=bk.raw_wait(b))
                bk.read(b, "scalar")
            return self.s_sc.count

        g3 = gball.rearrange("p (t e) -> p t e", t=16)
        pw3 = pwb.rearrange("p (t e) -> p t e", t=16)
        msk3 = mskb.rearrange("p (t e) -> p t e", t=16)
        bc = lambda apx: apx.broadcast_to([128, 16, 32])
        mxv = lambda r: mxs[:, r * 16:(r + 1) * 16].rearrange("p (t o) -> p t o", o=1)

        def routing_top4(g_sc):
            w3 = w_tm.rearrange("p (t e) -> p t e", t=16)
            m2 = rs1.rearrange("p (t o) -> p t o", o=1)
            V, SX = "vector", "scalar"
            pr.op(V, lambda v: v.tensor_copy(pwb, gball), waits=[(self.s_sc, g_sc)])
            for it in range(4):
                pr.op(V, lambda v, it=it: v.reduce_max(mxv(it), pw3, axis=AX.X))
                if it < 3:
                    pr.op(V, lambda v, it=it: v.tensor_tensor(msk3, pw3, bc(mxv(it)), ALU.is_ge))
                    pr.op(V, lambda v: v.scalar_tensor_tensor(
                        pwb, mskb, -1e30, pwb, ALU.mult, ALU.add))
            pr.op(V, lambda v: v.tensor_tensor(
                gsub.rearrange("p (t e) -> p t e", t=16), g3, bc(mxv(0)), ALU.subtract))
            ve = self.s_ve.count
            pr.op(SX, lambda sx: sx.activation(gsub, gsub, AF.Exp),
                  waits=[(self.s_ve, ve)])
            e_sc = self.s_sc.count
            eg3 = gsub.rearrange("p (t e) -> p t e", t=16)
            pr.op(V, lambda v: v.reduce_sum(m2, eg3, axis=AX.X),
                  waits=[(self.s_sc, e_sc)])
            pr.op(V, lambda v: v.reciprocal(rs1, rs1))
            pr.op(V, lambda v: v.tensor_tensor(msk3, g3, bc(mxv(3)), ALU.is_ge))
            pr.op(V, lambda v: v.tensor_tensor(
                w3, eg3, bc(m2), ALU.mult))
            pr.op(V, lambda v: v.tensor_tensor(w_tm, w_tm, mskb, ALU.mult))
            for r in range(4):
                pr.op(V, lambda v, r=r: v.tensor_tensor(
                    wr[:, r * 16:(r + 1) * 16], mxs[:, r * 16:(r + 1) * 16],
                    mxs[:, 0:16], ALU.subtract))
            ve2 = self.s_ve.count
            pr.op(SX, lambda sx: sx.activation(wr, wr, AF.Exp),
                  waits=[(self.s_ve, ve2)])
            e2 = self.s_sc.count
            pr.op(V, lambda v: v.tensor_tensor(
                wr.rearrange("p (r t) -> p r t", r=4), wr.rearrange("p (r t) -> p r t", r=4),
                rs1.rearrange("p (o t) -> p o t", o=1).broadcast_to([128, 4, 16]),
                ALU.mult), waits=[(self.s_sc, e2)])
            return self.s_ve.count

        def sel_transposes(w_ve):
            for tt in range(16):
                b2 = 6 + tt % 2
                w2w = bk.war_waits(b2) + ([(self.s_ve, w_ve), (s_wg, wg_done)]
                                          if tt == 0 else [])
                pr.op("tensor",
                      lambda t, b2=b2, tt=tt: t.transpose(
                          bk.ap[b2][0:32, 0:128], w_tm[:, tt * EXP:(tt + 1) * EXP], identf),
                      waits=w2w, incs=[(self.s_mm, 1)])
                bk.wrote(b2)
                pr.op("scalar",
                      lambda sx, b2=b2, tt=tt: sx.copy(
                          w_fmT[:, tt * 128:(tt + 1) * 128], bk.ap[b2][0:32, 0:128]),
                      waits=bk.raw_wait(b2))
                bk.read(b2, "scalar")
            return self.s_sc.count

        def positions(tr_sc):
            sel32 = ar.alloc("sel32", S * 4, F32, S, side="high")
            ones32 = ar.alloc("ones32", S * 4, F32, S, side="high")
            pos_fm = ar.alloc("pos_fm", S * 4, F32, S, side="high")
            pr.op("vector", lambda v: v.memset(ones32[0:32, :], 1.0))
            pr.op("vector", lambda v: v.tensor_scalar(
                sel32[0:32, :], w_fmT, 0.0, None, ALU.is_gt),
                waits=[(self.s_sc, tr_sc)])
            pr.op("vector", lambda v: v.tensor_tensor_scan(
                pos_fm[0:32, :], ones32[0:32, :], sel32[0:32, :], 0.0,
                ALU.mult, ALU.add))
            sc_ve = self.s_ve.count
            for tt in range(16):
                b2 = 6 + tt % 2
                w2w = bk.war_waits(b2) + ([(self.s_ve, sc_ve)] if tt == 0 else [])
                pr.op("tensor",
                      lambda t, b2=b2, tt=tt: t.transpose(
                          bk.ap[b2][0:128, 0:32],
                          pos_fm[0:32, tt * 128:(tt + 1) * 128], identf[0:32, 0:32]),
                      waits=w2w, incs=[(self.s_mm, 1)])
                bk.wrote(b2)
                pr.op("scalar",
                      lambda sx, b2=b2, tt=tt: sx.copy(
                          pos_tm[:, tt * 32:(tt + 1) * 32], bk.ap[b2][:, 0:32]),
                      waits=bk.raw_wait(b2))
                bk.read(b2, "scalar")
            ar.free("sel32", "ones32", "pos_fm")
            return self.s_sc.count

        def indices(pos_sc):
            iot3 = iotae.rearrange("p (t e) -> p t e", t=16)
            pos3 = pos_tm.rearrange("p (t e) -> p t e", t=16)
            tmp = gsub
            tmp3 = tmp.rearrange("p (t e) -> p t e", t=16)
            er = pwb
            for r in range(4):
                w = [(self.s_sc, pos_sc)] if r == 0 else []
                pr.op("vector", lambda v, r=r: v.tensor_tensor(
                    msk3, g3, bc(mxv(r)), ALU.is_equal), waits=w)
                pr.op("vector", lambda v: v.tensor_tensor(tmp3, msk3, iot3, ALU.mult))
                pr.op("vector", lambda v: v.reduce_sum(
                    er[:, 0:16].rearrange("p (t o) -> p t o", o=1), tmp3, axis=AX.X))
                pr.op("vector", lambda v: v.tensor_tensor(tmp3, msk3, pos3, ALU.mult))
                pr.op("vector", lambda v: v.reduce_sum(
                    er[:, 16:32].rearrange("p (t o) -> p t o", o=1), tmp3, axis=AX.X))
                pr.op("vector", lambda v: v.scalar_tensor_tensor(
                    er[:, 32:48], er[:, 0:16], float(CAP), er[:, 16:32],
                    ALU.mult, ALU.add))
                pr.op("vector", lambda v: v.tensor_scalar(
                    er[:, 32:48], er[:, 32:48], -1.0, None, ALU.add))
                pr.op("vector", lambda v, r=r: v.tensor_copy(
                    sidx[:, r * 16:(r + 1) * 16], er[:, 32:48]))
            return self.s_ve.count

        def emit_shared(ei):
            par = ei % 2
            NCH2 = 8
            pend_c = None
            hready = {}

            def emit_w13(c):
                for m in range(4):
                    b = m // 2
                    off = (m % 2) * 256
                    w = bk.war_waits(b) + [(s_wB[par], wB_cnt[ei])]
                    if ei == 0 and m == 0:
                        w += yw_chunk(c // 2)
                    for k in range(DT):
                        pr.op("tensor",
                              lambda t, b=b, off=off, m=m, c=c, k=k: t.matmul(
                                  bk.ap[b][:, off:off + 256],
                                  w1b[par][:, k * HEXP + m * 128: k * HEXP + (m + 1) * 128],
                                  ybf[:, k * S + c * 256: k * S + (c + 1) * 256],
                                  start=(k == 0), stop=(k == DT - 1)),
                              waits=w if k == 0 else (),
                              incs=[(self.s_mm, 1)] if k == DT - 1 else ())
                    if m % 2 == 1:
                        bk.wrote(b)
                for m in range(4):
                    b = 2 + m // 2
                    off = (m % 2) * 256
                    w = bk.war_waits(b) + [(s_wB[par], wB_cnt[ei])]
                    for k in range(DT):
                        pr.op("tensor",
                              lambda t, b=b, off=off, m=m, c=c, k=k: t.matmul(
                                  bk.ap[b][:, off:off + 256],
                                  w3b[par][:, k * HEXP + m * 128: k * HEXP + (m + 1) * 128],
                                  ybf[:, k * S + c * 256: k * S + (c + 1) * 256],
                                  start=(k == 0), stop=(k == DT - 1)),
                              waits=w if k == 0 else (),
                              incs=[(self.s_mm, 1)] if k == DT - 1 else ())
                    if m % 2 == 1:
                        bk.wrote(b)

            def emit_h(c):
                w2mm = self.s_mm.count
                prev_hv = hready[c - 1] if c - 1 in hready else 0
                sil_m = []
                for b in range(2):
                    pr.op("scalar",
                          lambda sx, b=b: sx.activation(
                              silu_s[:, b * 512:(b + 1) * 512], bk.ap[b], AF.Silu),
                          waits=bk.raw_wait(b) +
                          ([(self.s_ve, prev_hv)] if b == 0 else []))
                    bk.read(b, "scalar")
                    sil_m.append(self.s_sc.count)
                for b in range(2):
                    pr.op("vector",
                          lambda v, b=b: v.tensor_tensor(
                              hb[:, b * 512:(b + 1) * 512],
                              silu_s[:, b * 512:(b + 1) * 512],
                              bk.ap[2 + b], ALU.mult),
                          waits=bk.raw_wait(2 + b) + [(self.s_sc, sil_m[b]),
                                                      (self.s_mm, w2mm)])
                    bk.read(2 + b, "vector")
                hready[c] = self.s_ve.count

            def emit_w2(c):
                hv = hready[c]
                for m2 in range(DT):
                    b = 4 + m2 // 2
                    off = (m2 % 2) * 256
                    w = bk.war_waits(b) + [(self.s_ve, hv), (s_w2, w2_cnt[ei])]
                    for k in range(4):
                        pr.op("tensor",
                              lambda t, b=b, off=off, m2=m2, k=k, c=c: t.matmul(
                                  bk.ap[b][:, off:off + 256],
                                  w2b[:, k * D + m2 * 128: k * D + (m2 + 1) * 128],
                                  hb[:, k * 256:(k + 1) * 256],
                                  start=(k == 0), stop=(k == 3)),
                              waits=w if k == 0 else (),
                              incs=[(self.s_mm, 1)] if k == 3 else ())
                    if m2 % 2 == 1:
                        bk.wrote(b)
                for bb in range(4):
                    b = 4 + bb
                    pr.op("vector",
                          lambda v, b=b, bb=bb, c=c: v.tensor_tensor(
                              self.y.rearrange("p (a t) -> p a t", a=DT)
                              [:, 2 * bb:2 * bb + 2, c * 256:(c + 1) * 256],
                              self.y.rearrange("p (a t) -> p a t", a=DT)
                              [:, 2 * bb:2 * bb + 2, c * 256:(c + 1) * 256],
                              bk.ap[b].rearrange("p (a t) -> p a t", a=2),
                              ALU.add),
                          waits=bk.raw_wait(b))
                    bk.read(b, "vector")

            for c in range(NCH2):
                emit_w13(c)
                if pend_c is not None:
                    emit_w2(pend_c)
                emit_h(c)
                pend_c = c
            emit_w2(pend_c)
            mm_end[ei] = self.s_mm.count

        def x_stage(idx_ve):
            xst = [ar.alloc(f"xst{p}", 1024 * 2, BF16, 1024, side="high") for p in range(2)]
            sct_done = {}
            for g in range(16):
                par = g % 2
                for dd in range(DT):
                    b2 = 6 + dd % 2
                    w2w = bk.war_waits(b2) + (yw_chunk(g // 4) if dd == 0 else [])
                    pr.op("tensor",
                          lambda t, b2=b2, g=g, dd=dd: t.transpose(
                              bk.ap[b2].bitcast(BF16)[:, 0:128],
                              ybf[:, dd * S + g * 128:dd * S + (g + 1) * 128], identb),
                          waits=w2w, incs=[(self.s_mm, 1)])
                    bk.wrote(b2)
                    w = bk.raw_wait(b2)
                    if dd < 2 and par in sct_done:
                        w = w + [(s_sct[par], sct_done[par])]
                    pr.op("scalar",
                          lambda sx, b2=b2, par=par, dd=dd: sx.copy(
                              xst[par][:, dd * 128:(dd + 1) * 128],
                              bk.ap[b2].bitcast(BF16)[:, 0:128]),
                          waits=w)
                    bk.read(b2, "scalar")
                ev_sc = self.s_sc.count
                for r in range(4):
                    w = [(self.s_sc, ev_sc), (s_xz, xz_done)]
                    if r == 0:
                        w.append((self.s_ve, idx_ve))
                    pr.op("gpsimd",
                          lambda gp, par=par, r=r, g=g: gp.indirect_dma_start(
                              self.Xall,
                              IndirectOffsetOnAxis(sidx[:, r * 16 + g:r * 16 + g + 1], 0),
                              xst[par], None),
                          waits=w, incs=[(s_sct[par], 16)], _dma=True)
                sct_done[par] = s_sct[par].count
            ar.free("xst0", "xst1")
            return [(s_sct[0], s_sct[0].count), (s_sct[1], s_sct[1].count)]

        xe_trmm = {}

        def emit_xe_gather(e, scat_waits):
            par = e % 2
            w = list(scat_waits)
            if e - 2 in xe_trmm:
                w.append((self.s_mm, xe_trmm[e - 2]))
            else:
                w.append((self.s_mm, self.s_mm.count))
            for j in range(3):
                pr.dma("sync", xetm3[par][:, j * 1024:(j + 1) * 1024],
                       self.Xall.rearrange("(g p) d -> g p d", p=128)[(e * CAP) // 128 + j],
                       s_xg[par], waits=w if j == 0 else ())
            return s_xg[par].count

        def emit_xe_transpose(e, xg_c):
            par = e % 2
            for j in range(3):
                for dd in range(DT):
                    b2 = 6 + dd % 2
                    w2w = bk.war_waits(b2) + (
                        [(s_xg[par], xg_c)] if (j == 0 and dd == 0) else [])
                    pr.op("tensor",
                          lambda t, b2=b2, par=par, j=j, dd=dd: t.transpose(
                              bk.ap[b2].bitcast(BF16)[:, 0:128],
                              xetm3[par][:, j * 1024 + dd * 128: j * 1024 + (dd + 1) * 128],
                              identb),
                          waits=w2w, incs=[(self.s_mm, 1)])
                    bk.wrote(b2)
                    pr.op("scalar",
                          lambda sx, b2=b2, par=par, dd=dd, j=j: sx.copy(
                              xefm[par][:, dd * CAP + j * 128: dd * CAP + (j + 1) * 128],
                              bk.ap[b2].bitcast(BF16)[:, 0:128]),
                          waits=bk.raw_wait(b2))
                    bk.read(b2, "scalar")
            xe_trmm[e] = self.s_mm.count
            return self.s_sc.count

        def emit_routed(e, xe_sc, mid=None):
            ei = e + 2
            par = e % 2
            for which, wb, b0 in ((0, w1b, 0), (1, w3b, 4)):
                for m in range(4):
                    b = b0 + m
                    w = bk.war_waits(b) + [(s_wB[par], wB_cnt[ei]),
                                           (self.s_sc, xe_sc)]
                    for k in range(DT):
                        pr.op("tensor",
                              lambda t, b=b, m=m, k=k, wb=wb: t.matmul(
                                  bk.ap[b][:, 0:CAP],
                                  wb[par][:, k * HEXP + m * 128: k * HEXP + (m + 1) * 128],
                                  xefm[par][:, k * CAP:(k + 1) * CAP],
                                  start=(k == 0), stop=(k == DT - 1)),
                              waits=w if k == 0 else (),
                              incs=[(self.s_mm, 1)] if k == DT - 1 else ())
                    bk.wrote(b)
            sil_m = []
            for m in range(4):
                pr.op("scalar",
                      lambda sx, m=m: sx.activation(
                          hbr[:, m * CAP:(m + 1) * CAP], bk.ap[m][:, 0:CAP], AF.Silu),
                      waits=bk.raw_wait(m))
                bk.read(m, "scalar")
                sil_m.append(self.s_sc.count)
            for m in range(4):
                pr.op("vector",
                      lambda v, m=m: v.tensor_tensor(
                          hbr[:, m * CAP:(m + 1) * CAP],
                          hbr[:, m * CAP:(m + 1) * CAP],
                          bk.ap[4 + m][:, 0:CAP], ALU.mult),
                      waits=bk.raw_wait(4 + m) + [(self.s_sc, sil_m[m])])
                bk.read(4 + m, "vector")
            h_ve = self.s_ve.count
            mid_sc = mid() if mid is not None else None
            oev = []
            for st in range(3):
                for do in range(2):
                    b = st * 2 + do
                    w = bk.war_waits(b) + [(self.s_ve, h_ve), (s_w2, w2_cnt[ei])]
                    for k in range(4):
                        pr.op("tensor",
                              lambda t, b=b, st=st, do=do, k=k: t.matmul(
                                  bk.ap[b],
                                  hbr[:, k * CAP + st * 128: k * CAP + (st + 1) * 128],
                                  w2b[:, k * D + do * 512:(k * D) + (do + 1) * 512],
                                  start=(k == 0), stop=(k == 3)),
                              waits=w if k == 0 else (),
                              incs=[(self.s_mm, 1)] if k == 3 else ())
                    bk.wrote(b)
                    pr.op("vector",
                          lambda v, b=b, st=st, do=do: v.tensor_copy(
                              oest[:, st * 1024 + do * 512: st * 1024 + (do + 1) * 512],
                              bk.ap[b]),
                          waits=bk.raw_wait(b) + ([(s_oe, s_oe.count)]
                                                  if (st == 0 and do == 0) else []))
                    bk.read(b, "vector")
            oe_ve = self.s_ve.count
            pr.dma("sync",
                   self.OEall.rearrange("(g p) d -> p g d", p=128)
                   [:, (e * CAP) // 128:(e * CAP) // 128 + 3, :],
                   oest.rearrange("p (g d) -> p g d", g=3),
                   s_oe, waits=[(self.s_ve, oe_ve)])
            mm_end[ei] = self.s_mm.count

        def combine(oe_total):
            gbuf = [ar.alloc(f"gcb{p}", 4 * 1024 * 2, BF16, 4 * 1024, side="high")
                    for p in range(4)]
            dgall = ar.alloc("dgall", 64 * 128 * 2, BF16, 64 * 128, side="high")
            dgc = {}
            for g in range(16):
                for r in range(4):
                    gr = r * 16 + g
                    pr.op("vector",
                          lambda v, gr=gr: v.tensor_scalar(
                              dgall[:, gr * 128:(gr + 1) * 128], identb,
                              wr[:, gr:gr + 1], None, ALU.mult))
                dgc[g] = self.s_ve.count
            trdone = {}
            for g in range(16):
                par = g % 4
                for r in range(4):
                    w = [(s_oe, oe_total)] if (g < 4 and r == 0) else []
                    if r == 0 and par in trdone:
                        w.append((self.s_mm, trdone[par]))
                    pr.op("gpsimd",
                          lambda gp, par=par, r=r, g=g: gp.indirect_dma_start(
                              gbuf[par][:, r * 1024:(r + 1) * 1024], None,
                              self.OEall,
                              IndirectOffsetOnAxis(sidx[:, r * 16 + g:r * 16 + g + 1], 0)),
                          waits=w, incs=[(s_cg[par], 16)], _dma=True)
                cg_c = s_cg[par].count
                for half in range(2):
                    b2 = 6 + half
                    w2w = bk.war_waits(b2) + [(s_cg[par], cg_c),
                                              (self.s_ve, dgc[g])]
                    first = True
                    for j in range(4):
                        dd = half * 4 + j
                        for r in range(4):
                            pr.op("tensor",
                                  lambda t, b2=b2, par=par, dd=dd, j=j, r=r, g=g: t.matmul(
                                      bk.ap[b2][:, j * 128:(j + 1) * 128],
                                      gbuf[par][:, r * 1024 + dd * 128:
                                                r * 1024 + (dd + 1) * 128],
                                      dgall[:, (r * 16 + g) * 128:
                                             (r * 16 + g + 1) * 128],
                                      start=(r == 0), stop=(r == 3)),
                                  waits=w2w if first else (),
                                  incs=[(self.s_mm, 1)]
                                  if (j == 3 and r == 3) else ())
                            first = False
                    bk.wrote(b2)
                    pr.op("vector",
                          lambda v, b2=b2, g=g, half=half: v.tensor_tensor(
                              self.y.rearrange("p (a t) -> p a t", a=DT)
                              [:, half * 4:(half + 1) * 4, g * 128:(g + 1) * 128],
                              self.y.rearrange("p (a t) -> p a t", a=DT)
                              [:, half * 4:(half + 1) * 4, g * 128:(g + 1) * 128],
                              bk.ap[b2].rearrange("p (a t) -> p a t", a=4),
                              ALU.add),
                          waits=bk.raw_wait(b2))
                    bk.read(b2, "vector")
                trdone[par] = self.s_mm.count
                if g % 4 == 3 and ln_hook is not None:
                    ln_hook(g // 4, (self.s_ve, self.s_ve.count))
            ar.free("gcb0", "gcb1", "gcb2", "gcb3", "dgall")

        load13(0)
        load13(1)
        loadw2(0)
        g_sc = routing_mms()
        top4_ve = routing_top4(g_sc)
        emit_shared(0)
        loadw2(1)
        load13(2)
        tr_sc = sel_transposes(top4_ve)
        pos_sc = positions(tr_sc)
        idx_ve = indices(pos_sc)
        scat_waits = x_stage(idx_ve)
        emit_shared(1)
        loadw2(2)
        load13(3)
        ar.free("ybf_" + self.ybf_tag)
        xetm3 = [ar.alloc(f"xetm{p}", 3 * 1024 * 2, BF16, 3 * 1024, side="high") for p in range(2)]
        xefm = [ar.alloc(f"xefm{p}", DT * CAP * 2, BF16, DT * CAP, side="high") for p in range(2)]
        hbr = ar.alloc("hbr", 4 * CAP * 2, BF16, 4 * CAP, side="high")
        oest = ar.alloc("oest", 3 * 1024 * 2, BF16, 3 * 1024, side="high")
        xg_c = {0: emit_xe_gather(0, scat_waits)}
        xe_sc = {0: emit_xe_transpose(0, xg_c[0])}
        for e in range(32):
            if e + 1 < 32:
                xg_c[e + 1] = emit_xe_gather(e + 1, scat_waits)

            def mid(e=e):
                if e + 1 < 32:
                    xe_sc[e + 1] = emit_xe_transpose(e + 1, xg_c[e + 1])
                return None
            emit_routed(e, xe_sc[e], mid=mid)
            if e + 4 < 34:
                load13(e + 4)
            if e + 3 < 34:
                loadw2(e + 3)
        ar.free("iotae", "wgmat", "w_tm", "w_fmT", "gball",
                "gsub", "pwb", "mskb", "mxs", "rs1", "pos_tm",
                "w1g0", "w1g1", "w3g0", "w3g1", "w2g", "silu_sg", "hbg", "zt",
                "xetm0", "xetm1", "xefm0", "xefm1", "hbr", "oest")
        combine(s_oe.count)
        ar.free("identf", "identb")
        self.y_ready = ("ve", self.s_ve.count)

    def phase_C(self, ybf, ybf_done, free_ybf=None, ln_st=None):
        def yw_chunk(c512):
            if ln_st is not None:
                return [(self.s_sc, ln_st["sc"][c512])]
            return [(self.s_sc, ybf_done)]
        pr, bk, ar, d = self.pr, self.banks, self.ar, self.d
        s_wC = pr.sem("wC")
        cw = [(self.s_mm, self.s_mm.count), (self.s_ve, self.s_ve.count),
              (self.s_sc, self.s_sc.count)]
        wqT = ar.alloc("l1wq", DT * 64 * 2, BF16, DT * 64, side="high")
        wkT = ar.alloc("l1wk", DT * 64 * 2, BF16, DT * 64, side="high")
        wvT = ar.alloc("l1wv", DT * 64 * 2, BF16, DT * 64, side="high")
        woT = ar.alloc("l1wo", D * 2, BF16, D, side="high")[0:64, :]
        pr.dma("gpsimd", wqT.rearrange("p (a o) -> p a o", a=DT),
               d["l1wqT"].rearrange("(a p) o -> p a o", p=128), s_wC, waits=cw)
        pr.dma("gpsimd", wkT.rearrange("p (a o) -> p a o", a=DT),
               d["l1wkT"].rearrange("(a p) o -> p a o", p=128), s_wC)
        pr.dma("gpsimd", wvT.rearrange("p (a o) -> p a o", a=DT),
               d["l1wvT"].rearrange("(a p) o -> p a o", p=128), s_wC)
        pr.dma("gpsimd", woT, d["l1woT"], s_wC)
        wC_done = s_wC.count

        Q_s = ar.alloc("l1Q", S * 2, BF16, S, side="high")[0:64, :]
        K_s = ar.alloc("l1K", 16 * 64 * 2, BF16, 16 * 64, side="high")
        V_s = ar.alloc("l1V", 16 * 64 * 2, BF16, 16 * 64, side="high")
        en_s = ar.alloc("l1en", 64 * 2, BF16, 64, side="high")[0:64, :]
        ctx_s = ar.alloc("l1ctx", S * 2, BF16, S, side="high")[0:64, :]
        CSC = math.sqrt(3.0) / 8.0

        for c in range(NCH):
            b = c % 2
            w = bk.war_waits(b) + [(s_wC, wC_done)] + yw_chunk(c)
            for k in range(DT):
                pr.op("tensor",
                      lambda t, b=b, c=c, k=k: t.matmul(
                          bk.ap[b][0:64, :],
                          wqT[:, k * 64:(k + 1) * 64],
                          ybf[:, k * S + c * 512:k * S + (c + 1) * 512],
                          start=(k == 0), stop=(k == DT - 1)),
                      waits=w if k == 0 else (),
                      incs=[(self.s_mm, 1)] if k == DT - 1 else ())
            bk.wrote(b)
            pr.op("scalar",
                  lambda s, b=b, c=c: s.activation(
                      Q_s[:, c * 512:(c + 1) * 512], bk.ap[b][0:64, :],
                      AF.Identity, scale=CSC),
                  waits=bk.raw_wait(b))
            bk.read(b, "scalar")
        kv_half = {}
        for half in range(2):
            for which, wT, dst in ((0, wkT, K_s), (1, wvT, V_s)):
                b = 2 + which
                w = bk.war_waits(b) + [(s_wC, wC_done)] + yw_chunk(half * 2)
                if which == 0:
                    w += yw_chunk(half * 2 + 1)
                for sub in range(8):
                    tt = half * 8 + sub
                    for k in range(DT):
                        pr.op("tensor",
                              lambda t, b=b, tt=tt, k=k, sub=sub, wT=wT: t.matmul(
                                  bk.ap[b][:, sub * 64:(sub + 1) * 64],
                                  ybf[:, k * S + tt * 128:k * S + (tt + 1) * 128],
                                  wT[:, k * 64:(k + 1) * 64],
                                  start=(k == 0), stop=(k == DT - 1)),
                              waits=w if (sub == 0 and k == 0) else (),
                              incs=[(self.s_mm, 1)] if k == DT - 1 else ())
                bk.wrote(b)
                pr.op("vector",
                      lambda v, b=b, half=half, dst=dst: v.tensor_copy(
                          dst[:, half * 512:(half + 1) * 512], bk.ap[b]),
                      waits=bk.raw_wait(b))
                bk.read(b, "vector")
            kv_half[half] = self.s_ve.count
        kv_done = self.s_ve.count
        for half in range(2):
            ve_h = kv_half[half]
            w = (bk.war_waits(4) if half == 0 else []) + [(self.s_ve, ve_h)]
            for sub in range(8):
                tt = half * 8 + sub
                pr.op("tensor",
                      lambda t, tt=tt: t.matmul(
                          bk.ap[4][0:64, 0:64],
                          K_s[:, tt * 64:(tt + 1) * 64],
                          V_s[:, tt * 64:(tt + 1) * 64],
                          start=(tt == 0), stop=(tt == 15)),
                      waits=w if sub == 0 else (),
                      incs=[(self.s_mm, 1)] if tt == 15 else ())
        bk.wrote(4)
        pr.op("scalar", lambda s: s.copy(en_s, bk.ap[4][0:64, 0:64]),
              waits=bk.raw_wait(4))
        bk.read(4, "scalar")
        en_done = self.s_sc.count
        for c in range(NCH):
            b = 5 + c % 2
            w = bk.war_waits(b) + [(self.s_sc, en_done)]
            pr.op("tensor",
                  lambda t, b=b, c=c: t.matmul(
                      bk.ap[b][0:64, :], en_s, Q_s[:, c * 512:(c + 1) * 512],
                      start=True, stop=True),
                  waits=w, incs=[(self.s_mm, 1)])
            bk.wrote(b)
            pr.op("scalar",
                  lambda s, b=b, c=c: s.copy(ctx_s[:, c * 512:(c + 1) * 512],
                                             bk.ap[b][0:64, :]),
                  waits=bk.raw_wait(b))
            bk.read(b, "scalar")
        ctx_done = self.s_sc.count
        if free_ybf is not None:
            ar.free(free_ybf)
        stC = self.ln_start("c")
        for c in range(NCH):
            for m in range(DT):
                b = (c * DT + m) % 4
                w = bk.war_waits(b) + [(self.s_sc, ctx_done)]
                pr.op("tensor",
                      lambda t, b=b, m=m, c=c: t.matmul(
                          bk.ap[b],
                          woT[:, m * 128:(m + 1) * 128],
                          ctx_s[:, c * 512:(c + 1) * 512],
                          start=True, stop=True),
                      waits=w, incs=[(self.s_mm, 1)])
                bk.wrote(b)
                pr.op("vector",
                      lambda v, b=b, m=m, c=c: v.tensor_tensor(
                          self.y[:, m * S + c * 512: m * S + (c + 1) * 512],
                          self.y[:, m * S + c * 512: m * S + (c + 1) * 512],
                          bk.ap[b], ALU.add),
                      waits=bk.raw_wait(b), incs=[(self.s_ve, 1)])
                bk.read(b, "vector")
            self.ln_chunk(stC, c, wait=(self.s_ve, self.s_ve.count))
        ar.free("l1wq", "l1wk", "l1wv", "l1wo", "l1Q", "l1K", "l1V", "l1en", "l1ctx")
        ybf_c, done_c = self.ln_end(stC)
        return ybf_c, done_c, stC

    def phase_D_pre(self):
        pr, ar, d = self.pr, self.ar, self.d
        self._s_wD = [pr.sem("wD0"), pr.sem("wD1")]
        self._wgb = [ar.alloc(f"wgD{p}", DT * HEXP * 2, BF16, DT * HEXP, side="high") for p in range(2)]
        self._wub = [ar.alloc(f"wuD{p}", DT * HEXP * 2, BF16, DT * HEXP, side="high") for p in range(2)]
        self._wdb = [ar.alloc(f"wdD{p}", 4 * D * 2, BF16, 4 * D, side="high") for p in range(2)]
        self._wD_loaded = {}
        self._mm_endD = {}

        def emit_loadD(g):
            par = g % 2
            lw = ([(self.s_mm, self._mm_endD[g - 2])] if g - 2 in self._mm_endD
                  else [(self.s_mm, self.s_mm.count), (self.s_ve, self.s_ve.count),
                        (self.s_sc, self.s_sc.count)])
            pr.dma("gpsimd", self._wgb[par].rearrange("p (a o) -> p a o", a=DT),
                   d["wgT"][:, g * HEXP:(g + 1) * HEXP]
                   .rearrange("(a p) o -> p a o", p=128), self._s_wD[par], waits=lw)
            pr.dma("gpsimd", self._wub[par].rearrange("p (a o) -> p a o", a=DT),
                   d["wuT"][:, g * HEXP:(g + 1) * HEXP]
                   .rearrange("(a p) o -> p a o", p=128), self._s_wD[par])
            pr.dma("gpsimd", self._wdb[par].rearrange("p (a o) -> p a o", a=4),
                   d["wdT"][g * HEXP:(g + 1) * HEXP, :]
                   .rearrange("(a p) o -> p a o", p=128), self._s_wD[par])
            self._wD_loaded[g] = self._s_wD[par].count

        self._emit_loadD = emit_loadD
        emit_loadD(0)
        emit_loadD(1)

    def phase_D(self, ybf, ybf_done, ln_st=None, ln_hook=None):
        def yw_chunk(c512):
            if ln_st is not None:
                return [(self.s_sc, ln_st["sc"][c512])]
            return [(self.s_sc, ybf_done)]
        pr, bk, ar, d = self.pr, self.banks, self.ar, self.d
        if not hasattr(self, "_s_wD"):
            self.phase_D_pre()
        s_wD = self._s_wD
        NG = HID1 // HEXP
        wgb, wub, wdb = self._wgb, self._wub, self._wdb
        silu_s = ar.alloc("siluD", 4 * 512 * 2, BF16, 4 * 512, side="high")
        hb = ar.alloc("hbD", 4 * 512 * 2, BF16, 4 * 512, side="high")
        SQ = float(INVQ) * float(INVQ)

        mm_end = self._mm_endD
        wD_cnt = [0, 0]
        wD_loaded = self._wD_loaded
        emit_loadD = self._emit_loadD
        for g in range(NG):
            par = g % 2
            wD_cnt[par] = wD_loaded[g]

            NCH2 = 8
            pend_c = None
            hready = {}

            def emit_gu(c):
                for m in range(4):
                    b = m // 2
                    off = (m % 2) * 256
                    w = bk.war_waits(b) + [(s_wD[par], wD_cnt[par])]
                    if g == 0 and m == 0:
                        w += yw_chunk(c // 2)
                    for k in range(DT):
                        pr.op("tensor",
                              lambda t, b=b, off=off, m=m, c=c, k=k, par=par: t.matmul(
                                  bk.ap[b][:, off:off + 256],
                                  wgb[par][:, k * HEXP + m * 128: k * HEXP + (m + 1) * 128],
                                  ybf[:, k * S + c * 256: k * S + (c + 1) * 256],
                                  start=(k == 0), stop=(k == DT - 1)),
                              waits=w if k == 0 else (),
                              incs=[(self.s_mm, 1)] if k == DT - 1 else ())
                    if m % 2 == 1:
                        bk.wrote(b)
                for m in range(4):
                    b = 2 + m // 2
                    off = (m % 2) * 256
                    w = bk.war_waits(b) + [(s_wD[par], wD_cnt[par])]
                    for k in range(DT):
                        pr.op("tensor",
                              lambda t, b=b, off=off, m=m, c=c, k=k, par=par: t.matmul(
                                  bk.ap[b][:, off:off + 256],
                                  wub[par][:, k * HEXP + m * 128: k * HEXP + (m + 1) * 128],
                                  ybf[:, k * S + c * 256: k * S + (c + 1) * 256],
                                  start=(k == 0), stop=(k == DT - 1)),
                              waits=w if k == 0 else (),
                              incs=[(self.s_mm, 1)] if k == DT - 1 else ())
                    if m % 2 == 1:
                        bk.wrote(b)

            def emit_h(c):
                w2mm = self.s_mm.count
                prev_hv = hready[c - 1][0] if c - 1 in hready else 0
                sil_m = []
                for b in range(2):
                    pr.op("scalar",
                          lambda sx, b=b: sx.activation(
                              silu_s[:, b * 512:(b + 1) * 512], bk.ap[b],
                              AF.Silu, scale=INVQ),
                          waits=bk.raw_wait(b) +
                          ([(self.s_ve, prev_hv)] if b == 0 else []))
                    bk.read(b, "scalar")
                    sil_m.append(self.s_sc.count)
                for b in range(2):
                    pr.op("vector",
                          lambda v, b=b: v.tensor_tensor(
                              hb[:, b * 512:(b + 1) * 512],
                              silu_s[:, b * 512:(b + 1) * 512],
                              bk.ap[2 + b], ALU.mult),
                          waits=bk.raw_wait(2 + b) + [(self.s_sc, sil_m[b]),
                                                      (self.s_mm, w2mm)])
                    bk.read(2 + b, "vector")
                hready[c] = (self.s_ve.count, 0)

            def emit_down(c):
                hv, _ = hready[c]
                for m2 in range(DT):
                    b = 4 + m2 // 2
                    off = (m2 % 2) * 256
                    w = bk.war_waits(b) + [(self.s_ve, hv)]
                    for k in range(4):
                        pr.op("tensor",
                              lambda t, b=b, off=off, m2=m2, k=k, par=par: t.matmul(
                                  bk.ap[b][:, off:off + 256],
                                  wdb[par][:, k * D + m2 * 128: k * D + (m2 + 1) * 128],
                                  hb[:, k * 256:(k + 1) * 256],
                                  start=(k == 0), stop=(k == 3)),
                              waits=w if k == 0 else (),
                              incs=[(self.s_mm, 1)] if k == 3 else ())
                    if m2 % 2 == 1:
                        bk.wrote(b)
                for bb in range(4):
                    b = 4 + bb
                    pr.op("vector",
                          lambda v, b=b, bb=bb, c=c: v.scalar_tensor_tensor(
                              self.y.rearrange("p (a t) -> p a t", a=DT)
                              [:, 2 * bb:2 * bb + 2, c * 256:(c + 1) * 256],
                              bk.ap[b].rearrange("p (a t) -> p a t", a=2), SQ,
                              self.y.rearrange("p (a t) -> p a t", a=DT)
                              [:, 2 * bb:2 * bb + 2, c * 256:(c + 1) * 256],
                              ALU.mult, ALU.add),
                          waits=bk.raw_wait(b))
                    bk.read(b, "vector")

            lnw = {}
            hook_k = None
            for c in range(NCH2):
                emit_gu(c)
                if pend_c is not None:
                    emit_down(pend_c)
                    if g == NG - 1 and ln_hook is not None and pend_c % 2 == 1:
                        k = pend_c // 2
                        lnw[k] = (self.s_ve, self.s_ve.count)
                        if k - 1 >= 0:
                            hook_k = k - 1
                emit_h(c)
                if hook_k is not None:
                    ln_hook(hook_k, lnw[hook_k])
                    hook_k = None
                pend_c = c
            emit_down(pend_c)
            if g == NG - 1 and ln_hook is not None:
                lnw[pend_c // 2] = (self.s_ve, self.s_ve.count)
                ln_hook(pend_c // 2 - 1, lnw[pend_c // 2 - 1])
                ln_hook(pend_c // 2, lnw[pend_c // 2])
            mm_end[g] = self.s_mm.count
            if g + 2 < NG:
                emit_loadD(g + 2)
        ar.free("wgD0", "wgD1", "wuD0", "wuD1", "wdD0", "wdD1", "siluD", "hbD")
        self.y_ready = ("ve", self.s_ve.count)

    def finish(self, stream_st=None):
        pr = self.pr
        yv = self.y.rearrange("p (a t) -> p a t", a=DT)
        out = self.out_d.rearrange("(a p) t -> p a t", p=128)
        if stream_st is None:
            for half in range(2):
                pr.dma("sync", out[:, half * 4:(half + 1) * 4, :],
                       yv[:, half * 4:(half + 1) * 4, :], self.s_st,
                       waits=self.ywait() if half == 0 else ())
        else:
            for c in range(NCH):
                pr.dma("sync", out[:, :, c * 512:(c + 1) * 512],
                       yv[:, :, c * 512:(c + 1) * 512], self.s_st,
                       waits=[(self.s_ve, stream_st["ve"][c])])
        pr.op("sync", lambda s: s.nop(), waits=[(self.s_st, self.s_st.count)])


def build_program(phases="A"):
    b = Build(phases)
    ybf = ybf_done = None
    ln_st = None
    if "A" in phases:
        ybf, ybf_done, ln_st = b.phase_A()
        ybf_tag = "a"
    if "B" in phases:
        if ybf is None:
            ybf, ybf_done = b.mk_ybf("b_in")
            ybf_tag = "b_in"
        else:
            ybf_tag = "a"
        b.ybf_tag = ybf_tag
        stB = [None]

        def hookB(c, w):
            if stB[0] is None:
                stB[0] = b.ln_start("b")
            b.ln_chunk(stB[0], c, wait=w)
        b.phase_B(ybf, ybf_done, ln_hook=hookB, ln_st=ln_st)
        ybf, ybf_done = b.ln_end(stB[0])
        ln_st = stB[0]
        ybf_tag = "b"
    if "C" in phases:
        if "D" in phases:
            b.phase_D_pre()
        if ybf is None:
            ybf, ybf_done = b.mk_ybf("c_in")
            ybf_tag = "c_in"
        ybf, ybf_done, ln_st = b.phase_C(ybf, ybf_done, free_ybf="ybf_" + ybf_tag,
                                         ln_st=ln_st)
        ybf_tag = "c"
    std = None
    if "D" in phases:
        if ybf is None:
            ybf, ybf_done = b.mk_ybf("d_in")
            ybf_tag = "d_in"
        std = b.ln_start("d", ybf_buf=ybf)

        def hookD(c, w):
            b.ln_chunk(std, c, wait=w, emit_ybf=False)
        b.phase_D(ybf, ybf_done, ln_st=ln_st, ln_hook=hookD)
        b.ln_end(std)
        b.ar.free("ybf_" + ybf_tag)
    b.finish(stream_st=std)
    b.pr.emit(b.nc)
    return b


def _prep_shared(inputs, phases="ABCD"):
    f = np.float32
    out = {}
    bf = ml_dtypes.bfloat16
    if "A" in phases:
        for nm, key in (("wq", "L0_attn_wq"), ("wqc", "L0_attn_wqc"),
                        ("wqr", "L0_attn_wqr"), ("wkv", "L0_attn_wkv"),
                        ("wkr", "L0_attn_wkr"), ("wkc", "L0_attn_wkc"),
                        ("wv", "L0_attn_wv"), ("wo", "L0_attn_wo")):
            out[nm] = np.ascontiguousarray(np.asarray(inputs[key], f).astype(bf))

        def bias_tile(b, n):
            return np.ascontiguousarray(np.asarray(b, f).reshape(n, 128).T)

        out["bq"] = bias_tile(inputs["L0_attn_bq"], 2)
        out["bqc"] = bias_tile(np.concatenate([np.asarray(inputs["L0_attn_bqc"], f)]), 7)
        out["bqr"] = bias_tile(inputs["L0_attn_bqr"], 1)
        out["bkv"] = bias_tile(inputs["L0_attn_bkv"], 2)
        out["bkr"] = bias_tile(inputs["L0_attn_bkr"], 1)
        out["bkc"] = bias_tile(inputs["L0_attn_bkc"], 7)
        out["bv"] = bias_tile(inputs["L0_attn_bv"], 8)
        out["bo"] = bias_tile(inputs["L0_attn_bo"], 8)

        inv_freq = (1.0 / 10000.0 ** (np.arange(0, DR, 2, dtype=f) / DR)).astype(f)
        t = np.arange(S, dtype=f)
        freqs = (t[:, None] * inv_freq[None, :]).astype(f)
        emb = np.concatenate([freqs, freqs], axis=1)
        cos = np.cos(emb).astype(f).T
        sin = np.sin(emb).astype(f).T
        sin_eff = sin.copy()
        sin_eff[0:64, :] *= -1.0
        out["ropecos"] = np.ascontiguousarray(cos)
        out["ropesin"] = np.ascontiguousarray(sin_eff)

        m = np.zeros((128, 4 * 512), dtype=ml_dtypes.bfloat16)
        p = np.arange(128)[:, None]
        fe = np.arange(512)[None, :]
        for j in range(4):
            m[:, j * 512:(j + 1) * 512] = (p <= fe - 128 * j).astype(ml_dtypes.bfloat16)
        out["masks"] = m
    if "B" in phases:
        out["identf"] = np.eye(128, dtype=f)
        out["iotae"] = np.tile(np.arange(EXP, dtype=f)[None, :], (128, 16))
        out["identb"] = np.eye(128, dtype=ml_dtypes.bfloat16)
        oneh = np.zeros((32, 32 * 128), dtype=ml_dtypes.bfloat16)
        for e in range(32):
            oneh[e, e * 128:(e + 1) * 128] = 1
        out["oneh"] = oneh
        out["Wg"] = np.ascontiguousarray(np.asarray(inputs["L0_ffn_Wg"], f))
        for nm, key in (("W1", "L0_ffn_W1"), ("W2", "L0_ffn_W2"),
                        ("W3", "L0_ffn_W3"), ("Ws1", "L0_ffn_Ws1"),
                        ("Ws2", "L0_ffn_Ws2"), ("Ws3", "L0_ffn_Ws3")):
            out[nm] = np.ascontiguousarray(np.asarray(inputs[key], f).astype(bf))
    if "C" in phases:
        out["l1wqT"] = np.ascontiguousarray(np.asarray(inputs["L1_attn_wq"], f).T.astype(bf))
        out["l1wkT"] = np.ascontiguousarray(np.asarray(inputs["L1_attn_wk"], f).T.astype(bf))
        out["l1wvT"] = np.ascontiguousarray(np.asarray(inputs["L1_attn_wv"], f).T.astype(bf))
        out["l1woT"] = np.ascontiguousarray(np.asarray(inputs["L1_attn_wo"], f).T.astype(bf))
    if "D" in phases:
        out["wgT"] = np.ascontiguousarray(np.asarray(inputs["L1_ffn_w_gate"]).T.astype(bf))
        out["wuT"] = np.ascontiguousarray(np.asarray(inputs["L1_ffn_w_up"]).T.astype(bf))
        out["wdT"] = np.ascontiguousarray(np.asarray(inputs["L1_ffn_w_down"]).T.astype(bf))
    return out


def run_phases(x, inputs, phases):
    shared = _prep_shared(inputs, phases)
    b = build_program(phases)
    in_maps = []
    for c in range(8):
        m = dict(shared)
        m["xT"] = np.ascontiguousarray(np.asarray(x[c], np.float32).T)
        in_maps.append(m)
    res = run_bass_kernel_spmd(b.nc, in_maps, core_ids=list(range(8)))
    return np.stack([np.asarray(r["y_out"]).T for r in res.results], axis=0)


def kernel(**inputs):
    x = np.asarray(inputs["L0_attn_x"], np.float32)
    return run_phases(x, inputs, "ABCD")


# version 30
# speedup vs baseline: 1.0129x; 1.0129x over previous
import math
import numpy as np
import ml_dtypes
import concourse.bass as bass
import concourse.mybir as mybir
from concourse.bass import IndirectOffsetOnAxis
from concourse.bass_utils import run_bass_kernel_spmd

F32 = mybir.dt.float32
BF16 = mybir.dt.bfloat16
I32 = mybir.dt.int32
U8 = mybir.dt.uint8
AF = mybir.ActivationFunctionType
ALU = mybir.AluOpType
AX = mybir.AxisListType

S = 2048
D = 1024
DT = D // 128
NCH = S // 512
H, HD = 16, 64
DC, DR = 256, 128
EXP, HEXP = 32, 512
HS = 1024
HID1 = 3072
QSCALE = 1.0 / 8.0
INVQ = 1.0 / 127.0
LNEPS = 1e-5
CAP = 384


class Sem:
    __slots__ = ("name", "count", "handle")

    def __init__(self, name):
        self.name = name
        self.count = 0
        self.handle = None


class Prog:
    ENGINES = ("sync", "tensor", "vector", "scalar", "gpsimd")

    def __init__(self):
        self.ops = {e: [] for e in self.ENGINES}
        self._sems = []
        self.auto = {}

    def sem(self, name):
        s = Sem(name)
        self._sems.append(s)
        return s

    def enable_auto(self, engine, sem):
        self.auto[engine] = sem

    def op(self, engine, fn, waits=(), incs=(), _dma=False):
        waits = [(s, int(v)) for (s, v) in waits if v > 0]
        incs = list(incs)
        aut = self.auto.get(engine)
        if aut is not None and not _dma:
            if aut.count > 0:
                waits.append((aut, aut.count))
            incs = [(s, a) for (s, a) in incs if s is not aut]
            incs.append((aut, 1))
        for s, amt in incs:
            s.count += amt
        self.ops[engine].append((fn, tuple(waits), tuple(incs)))

    def dma(self, queue, out, in_, sem, waits=()):
        self.op(queue, lambda e, o=out, i=in_: e.dma_start(out=o, in_=i),
                waits, [(sem, 16)], _dma=True)

    def emit(self, nc):
        for s in self._sems:
            s.handle = nc.alloc_semaphore(name=s.name)
        with nc.Block() as blk:
            for eng_name in self.ENGINES:
                ops = self.ops[eng_name]
                if not ops:
                    continue

                def body(eng, ops=ops):
                    for fn, waits, incs in ops:
                        for s, v in waits:
                            eng.wait_ge(s.handle, v)
                        instr = fn(eng)
                        for s, amt in incs:
                            instr.then_inc(s.handle, amt)

                getattr(blk, eng_name)(body)


class Arena:

    def __init__(self, scratch_ap, nbytes):
        self.sc = scratch_ap
        self.nbytes = nbytes
        self.live = {}
        self.hwm = 0

    def alloc(self, name, nbytes_per_part, dtype, free_elems, side="low"):
        nbytes_per_part = int(nbytes_per_part)
        assert name not in self.live
        taken = sorted(self.live.values())
        if side == "low":
            off = 0
            for a, sz in taken:
                aligned = (off + 511) & ~511
                if aligned + nbytes_per_part <= a:
                    break
                off = max(off, a + sz)
            off = (off + 511) & ~511
            ok = off + nbytes_per_part <= self.nbytes
        else:
            off = self.nbytes
            for a, sz in reversed(taken):
                cand = (off - nbytes_per_part) & ~511
                if cand >= a + sz:
                    break
                off = min(off, a)
            off = (off - nbytes_per_part) & ~511
            ok = off >= 0
            for a, sz in taken:
                if not (off + nbytes_per_part <= a or off >= a + sz):
                    ok = False
        assert ok, (
            f"arena overflow: {name} needs {nbytes_per_part}B (side={side}), "
            f"cap {self.nbytes}, live {self.live}")
        self.live[name] = (off, nbytes_per_part)
        self.hwm = max(self.hwm, off + nbytes_per_part)
        return self.sc[:, off:off + nbytes_per_part].bitcast(dtype)[:, :free_elems]

    def free(self, *names):
        for n in names:
            del self.live[n]


class Banks:

    def __init__(self, nc, s_mm, esems):
        self.ap = [nc.alloc_psum_tensor(f"bank{i}", [128, 512], F32).ap()
                   for i in range(8)]
        self.s_mm = s_mm
        self.esems = esems
        self.readers = [dict() for _ in range(8)]
        self.mm_done = [0] * 8

    def war_waits(self, i):
        return [(self.esems[e], c) for e, c in self.readers[i].items()]

    def wrote(self, i):
        self.mm_done[i] = self.s_mm.count
        self.readers[i] = {}

    def read(self, i, engine):
        self.readers[i][engine] = self.esems[engine].count

    def raw_wait(self, i):
        return [(self.s_mm, self.mm_done[i])]


class Build:
    def __init__(self, phases="ABCD"):
        self.phases = phases
        nc = bass.Bass("TRN2", target_bir_lowering=False)
        self.nc = nc
        pr = Prog()
        self.pr = pr
        self.s_x = pr.sem("x_ld")
        self.s_small = pr.sem("small")
        self.s_wA1 = pr.sem("wA1")
        self.s_wA2 = pr.sem("wA2")
        self.s_rope = [pr.sem("rope0"), pr.sem("rope1")]
        self.s_ctx = [pr.sem("ctx0"), pr.sem("ctx1")]
        self.s_mm = pr.sem("mm")
        self.s_sc = pr.sem("sc")
        self.s_ve = pr.sem("ve")
        self.s_st = pr.sem("st")
        self.esems = {"scalar": self.s_sc, "vector": self.s_ve}
        self.s_gp = pr.sem("gp")
        pr.enable_auto("vector", self.s_ve)
        pr.enable_auto("scalar", self.s_sc)
        pr.enable_auto("gpsimd", self.s_gp)
        self.banks = Banks(nc, self.s_mm, self.esems)

        dram = {}

        def din(name, shape, dt=F32):
            dram[name] = nc.dram_tensor(name, shape, dt, kind="ExternalInput").ap()
            return dram[name]

        self.d = dram
        din("xT", [D, S])
        if "A" in phases:
            din("wq", [D, DC]); din("wqc", [DC, D - DR]); din("wqr", [D, DR])
            din("wkv", [D, DC]); din("wkr", [D, DR]); din("wkc", [DC, D - DR])
            din("wv", [DC, D]); din("wo", [D, D])
            din("bq", [128, 2]); din("bqc", [128, 7]); din("bqr", [128, 1])
            din("bkv", [128, 2]); din("bkr", [128, 1]); din("bkc", [128, 7])
            din("bv", [128, 8]); din("bo", [128, 8])
            din("ropecos", [128, S]); din("ropesin", [128, S])
            din("masks", [128, 4 * 512], BF16)
        if "B" in phases:
            self.w_scr = nc.dram_tensor("w_scr", [EXP, S], BF16, kind="Internal").ap()
            self.Xall = nc.dram_tensor("Xall", [EXP * CAP + 128, D], BF16, kind="Internal").ap()
            self.OEall = nc.dram_tensor("OEall", [EXP * CAP + 128, D], BF16, kind="Internal").ap()
            din("iotae", [128, 16 * EXP])
            din("identb", [128, 128], BF16)
            din("identf", [128, 128])
            din("oneh", [32, 32 * 128], BF16)
            din("Wg", [D, EXP])
            din("W1", [EXP, D, HEXP]); din("W2", [EXP, HEXP, D]); din("W3", [EXP, D, HEXP])
            din("Ws1", [D, HS]); din("Ws2", [HS, D]); din("Ws3", [D, HS])
        if "C" in phases:
            din("l1wqT", [D, 64]); din("l1wkT", [D, 64]); din("l1wvT", [D, 64])
            din("l1woT", [64, D])
        if "D" in phases:
            din("wgT", [D, HID1], I32); din("wuT", [D, HID1], I32)
            din("wdT", [HID1, D], I32)
        self.out_d = nc.dram_tensor("y_out", [D, S], F32, kind="ExternalOutput").ap()

        sb = lambda name, shape, dt: nc.alloc_sbuf_tensor(name, shape, dt).ap()
        self.sb = sb
        self.y = sb("y_res", [128, DT * S], F32)
        self.ones_bf = sb("ones_bf", [128, 1], BF16)
        self.ones_f1 = sb("ones_f1", [1, 128], F32)
        self.mq = sb("mq", [1, DT], F32)
        self.mk = sb("mk", [1, DT], F32)
        self.red = sb("mred", [1, 8], F32)
        self.negmb = sb("negmb", [128, DT], F32)
        self.smallb = {}
        if "A" in phases:
            for nm in ("bq", "bqc", "bqr", "bkv", "bkr", "bkc", "bv", "bo"):
                self.smallb[nm] = sb(nm + "_s", [128, dram[nm].shape[1]], F32)
        if "B" in phases:
            self.gb = sb("gb", [128, EXP], F32)
            self.pw = sb("pw", [128, EXP], F32)
            self.msk = sb("msk", [128, EXP], F32)
            self.sel = sb("sel", [128, EXP], F32)
            self.mx1 = sb("mx1", [128, 1], F32)

        rem = int(nc.sbuf_bytes_remaining)
        scr_b = (rem - 1024) & ~1023
        self.scr_t = sb("scratch", [128, scr_b], U8)
        self.ar = Arena(self.scr_t, scr_b)
        self.scr_b = scr_b

        pr.op("vector", lambda v: v.memset(self.ones_bf, 1.0))
        pr.op("vector", lambda v: v.memset(self.ones_f1, 1.0), incs=[(self.s_ve, 1)])

        xin = dram["xT"].rearrange("(a p) t -> p a t", p=128)
        yv = self.y.rearrange("p (a t) -> p a t", a=DT)
        self.s_x2 = pr.sem("x_ld2")
        pr.dma("sync", yv[:, 0:4, :], xin[:, 0:4, :], self.s_x)
        pr.dma("sync", yv[:, 4:8, :], xin[:, 4:8, :], self.s_x2)
        self.y_ready = ("x", self.s_x.count)

    def ywait(self):
        kind, v = self.y_ready
        if kind == "x":
            return [(self.s_x, self.s_x.count), (self.s_x2, self.s_x2.count)]
        sem = {"ve": self.s_ve, "sc": self.s_sc}[kind]
        return [(sem, v)]

    def wload(self, name, dram_ap, ktiles, dout, sem, waits=()):
        t = self.ar.alloc(name, ktiles * dout * 2, BF16, ktiles * dout, side="high")
        src = dram_ap.rearrange("(a p) o -> p a o", p=128)
        dst = t.rearrange("p (a o) -> p a o", a=ktiles)
        self.pr.dma("gpsimd", dst, src, sem, waits=waits)
        return t

    def ln_start(self, tag, ybf_buf=None):
        ar = self.ar
        st = {
            "tag": tag,
            "ybf": ybf_buf if ybf_buf is not None
            else ar.alloc("ybf_" + tag, DT * S * 2, BF16, DT * S),
            "sq": ar.alloc("sq_" + tag, DT * 512 * 2, BF16, DT * 512, side="high"),
            "m": ar.alloc("stm_" + tag, S * 4, F32, S, side="high")[0:1, :],
            "v": ar.alloc("stv_" + tag, S * 4, F32, S, side="high")[0:1, :],
            "t": ar.alloc("stt_" + tag, S * 4, F32, S, side="high")[0:1, :],
        }
        return st

    def ln_chunk(self, st, c, wait=None, emit_ybf=True):
        pr, bk = self.pr, self.banks
        ybf, sq = st["ybf"], st["sq"]
        st_m, st_v, st_t = st["m"], st["v"], st["t"]
        if wait is None:
            wait = self.ywait()
        else:
            wait = [wait]
        for d in range(DT):
            pr.op("vector",
                  lambda v, d=d, c=c: v.tensor_copy(
                      ybf[:, d * S + c * 512: d * S + (c + 1) * 512],
                      self.y[:, d * S + c * 512: d * S + (c + 1) * 512]),
                  waits=wait if d == 0 else ())
        cast_c = self.s_ve.count
        sq_m = []
        for d in range(DT):
            w = [(self.s_ve, cast_c)] if d == 0 else []
            w += [(self.s_mm, bk.mm_done[5])]
            pr.op("scalar",
                  lambda sx, d=d, c=c: sx.activation(
                      sq[:, d * 512:(d + 1) * 512],
                      ybf[:, d * S + c * 512: d * S + (c + 1) * 512],
                      AF.Square),
                  waits=w)
            sq_m.append(self.s_sc.count)
        w4 = bk.war_waits(4) + [(self.s_ve, cast_c)]
        for d in range(DT):
            pr.op("tensor",
                  lambda t, d=d, c=c: t.matmul(
                      bk.ap[4][0:1, :], self.ones_bf,
                      ybf[:, d * S + c * 512: d * S + (c + 1) * 512],
                      start=(d == 0), stop=(d == DT - 1)),
                  waits=w4 if d == 0 else (),
                  incs=[(self.s_mm, 1)] if d == DT - 1 else ())
        bk.wrote(4)
        w5 = bk.war_waits(5)
        for d in range(DT):
            pr.op("tensor",
                  lambda t, d=d: t.matmul(
                      bk.ap[5][0:1, :], self.ones_bf, sq[:, d * 512:(d + 1) * 512],
                      start=(d == 0), stop=(d == DT - 1)),
                  waits=(w5 if d == 0 else []) + [(self.s_sc, sq_m[d])],
                  incs=[(self.s_mm, 1)])
        bk.wrote(5)
        pr.op("vector",
              lambda v, c=c: v.tensor_scalar(st_m[:, c * 512:(c + 1) * 512],
                                             bk.ap[4][0:1, :], 1.0 / D, None, ALU.mult),
              waits=bk.raw_wait(4))
        bk.read(4, "vector")
        pr.op("vector",
              lambda v, c=c: v.tensor_scalar(st_v[:, c * 512:(c + 1) * 512],
                                             bk.ap[5][0:1, :], 1.0 / D, None, ALU.mult),
              waits=bk.raw_wait(5))
        bk.read(5, "vector")
        pr.op("vector",
              lambda v, c=c: v.tensor_tensor(st_t[:, c * 512:(c + 1) * 512],
                                             st_m[:, c * 512:(c + 1) * 512],
                                             st_m[:, c * 512:(c + 1) * 512], ALU.mult))
        pr.op("vector",
              lambda v, c=c: v.tensor_tensor(st_v[:, c * 512:(c + 1) * 512],
                                             st_v[:, c * 512:(c + 1) * 512],
                                             st_t[:, c * 512:(c + 1) * 512], ALU.subtract))
        pr.op("vector",
              lambda v, c=c: v.tensor_scalar(st_v[:, c * 512:(c + 1) * 512],
                                             st_v[:, c * 512:(c + 1) * 512],
                                             LNEPS, None, ALU.add))
        ve_var = self.s_ve.count
        pr.op("scalar", lambda sx, c=c: sx.activation(st_v[:, c * 512:(c + 1) * 512],
                                                      st_v[:, c * 512:(c + 1) * 512], AF.Ln),
              waits=[(self.s_ve, ve_var)])
        pr.op("scalar", lambda sx, c=c: sx.activation(st_v[:, c * 512:(c + 1) * 512],
                                                      st_v[:, c * 512:(c + 1) * 512],
                                                      AF.Exp, scale=-0.5))
        r_sc = self.s_sc.count
        pr.op("vector",
              lambda v, c=c: v.tensor_tensor(st_t[:, c * 512:(c + 1) * 512],
                                             st_m[:, c * 512:(c + 1) * 512],
                                             st_v[:, c * 512:(c + 1) * 512], ALU.mult),
              waits=[(self.s_sc, r_sc)])
        mr_ve = self.s_ve.count
        w6 = bk.war_waits(6) + [(self.s_sc, r_sc)]
        pr.op("tensor",
              lambda t, c=c: t.matmul(bk.ap[6], self.ones_f1,
                                      st_v[:, c * 512:(c + 1) * 512],
                                      start=True, stop=True),
              waits=w6, incs=[(self.s_mm, 1)])
        bk.wrote(6)
        w7 = bk.war_waits(7) + [(self.s_ve, mr_ve)]
        pr.op("tensor",
              lambda t, c=c: t.matmul(bk.ap[7], self.ones_f1,
                                      st_t[:, c * 512:(c + 1) * 512],
                                      start=True, stop=True),
              waits=w7, incs=[(self.s_mm, 1)])
        bk.wrote(7)
        for d in range(DT):
            pr.op("vector",
                  lambda v, d=d, c=c: v.tensor_tensor(
                      self.y[:, d * S + c * 512: d * S + (c + 1) * 512],
                      self.y[:, d * S + c * 512: d * S + (c + 1) * 512],
                      bk.ap[6], ALU.mult),
                  waits=bk.raw_wait(6) + bk.raw_wait(7) if d == 0 else ())
            pr.op("vector",
                  lambda v, d=d, c=c: v.tensor_tensor(
                      self.y[:, d * S + c * 512: d * S + (c + 1) * 512],
                      self.y[:, d * S + c * 512: d * S + (c + 1) * 512],
                      bk.ap[7], ALU.subtract))
            ve_d = self.s_ve.count
            if emit_ybf:
                pr.op("scalar",
                      lambda sx, d=d, c=c: sx.copy(
                          ybf[:, d * S + c * 512: d * S + (c + 1) * 512],
                          self.y[:, d * S + c * 512: d * S + (c + 1) * 512]),
                      waits=[(self.s_ve, ve_d)])
        bk.read(6, "vector")
        bk.read(7, "vector")
        st.setdefault("ve", {})[c] = self.s_ve.count
        st.setdefault("sc", {})[c] = self.s_sc.count

    def ln_end(self, st):
        tag = st["tag"]
        self.ar.free("sq_" + tag, "stm_" + tag, "stv_" + tag, "stt_" + tag)
        self.y_ready = ("ve", self.s_ve.count)
        return st["ybf"], self.s_sc.count

    def ln(self, tag, chunk_waits=None):
        st = self.ln_start(tag)
        for c in range(NCH):
            self.ln_chunk(st, c,
                          wait=None if chunk_waits is None else chunk_waits[c])
        return self.ln_end(st)

    def phase_A(self):
        pr, bk, ar, d = self.pr, self.banks, self.ar, self.d
        small = self.smallb
        v_aug = ar.alloc("v_aug", 16 * 1040 * 2, BF16, 16 * 1040)
        xbf = ar.alloc("xbf", DT * S * 2, BF16, DT * S)
        ckv = ar.alloc("ckv", 2 * S * 2, BF16, 2 * S)
        cos = ar.alloc("cos", S * 4, F32, S)
        sin = ar.alloc("sin", S * 4, F32, S)
        qr = ar.alloc("qr", S * 4, F32, S)
        kr = ar.alloc("kr", S * 4, F32, S)
        cq = ar.alloc("cq", 2 * S * 2, BF16, 2 * S)

        for dd in range(DT):
            w = []
            if dd == 0:
                w = [(self.s_x, self.s_x.count)]
            if dd == 4:
                w = [(self.s_x2, self.s_x2.count)]
            pr.op("scalar",
                  lambda s, dd=dd: s.copy(xbf[:, dd * S:(dd + 1) * S],
                                          self.y[:, dd * S:(dd + 1) * S]),
                  waits=w)
        xbf_done = self.s_sc.count

        for nm, t in small.items():
            pr.dma("sync", t, d[nm], self.s_small)
        pr.dma("sync", cos, d["ropecos"], self.s_small)
        pr.dma("sync", sin, d["ropesin"], self.s_small)
        masks = ar.alloc("masks", 4 * 512 * 2, BF16, 4 * 512, side="high")
        pr.dma("sync", masks, d["masks"], self.s_small)
        small_done = self.s_small.count

        wq = self.wload("wq", d["wq"], DT, DC, self.s_wA1)
        wqr = self.wload("wqr", d["wqr"], DT, DR, self.s_wA1)
        wkv = self.wload("wkv", d["wkv"], DT, DC, self.s_wA1)
        wkr = self.wload("wkr", d["wkr"], DT, DR, self.s_wA1)
        wA1_done = self.s_wA1.count

        def proj(dst, w, bias, mtiles, gelu):
            for m in range(mtiles):
                for c in range(NCH):
                    b = (m * NCH + c) % 2
                    waits = bk.war_waits(b) + [(self.s_wA1, wA1_done),
                                               (self.s_sc, xbf_done)]
                    for k in range(DT):
                        pr.op("tensor",
                              lambda t, b=b, m=m, c=c, k=k, w=w, mt=mtiles: t.matmul(
                                  bk.ap[b],
                                  w[:, k * mt * 128 + m * 128: k * mt * 128 + (m + 1) * 128],
                                  xbf[:, k * S + c * 512: k * S + (c + 1) * 512],
                                  start=(k == 0), stop=(k == DT - 1)),
                              waits=waits if k == 0 else (),
                              incs=[(self.s_mm, 1)] if k == DT - 1 else ())
                    bk.wrote(b)
                    fn = AF.Gelu if gelu else AF.Identity
                    pr.op("scalar",
                          lambda s, b=b, m=m, c=c, fn=fn, dst=dst, bias=bias: s.activation(
                              dst[:, m * S + c * 512: m * S + (c + 1) * 512],
                              bk.ap[b], fn, bias=bias[:, m:m + 1]),
                          waits=bk.raw_wait(b) + [(self.s_small, small_done)],
                          incs=[(self.s_sc, 1)])
                    bk.read(b, "scalar")

        proj(cq, wq, small["bq"], 2, True)
        proj(ckv, wkv, small["bkv"], 2, True)
        proj(qr, wqr, small["bqr"], 1, False)
        proj(kr, wkr, small["bkr"], 1, False)
        proj_done = self.s_sc.count
        ar.free("wq", "wqr", "wkv", "wkr")

        wv = self.wload("wv", d["wv"], 2, D, self.s_wA2,
                        waits=[(self.s_mm, self.s_mm.count),
                               (self.s_ve, self.s_ve.count),
                               (self.s_sc, self.s_sc.count)])
        wv_done = self.s_wA2.count
        for tt in range(16):
            pr.op("vector",
                  lambda v, tt=tt: v.memset(
                      v_aug[:, tt * 1040:(tt + 1) * 1040]
                      .rearrange("p (h c) -> p h c", h=16)[:, :, 64:65], 1.0))
            for n in range(2):
                b = 2 + (tt * 2 + n) % 2
                waits = bk.war_waits(b) + [(self.s_wA2, wv_done), (self.s_sc, proj_done)]
                for k in range(2):
                    pr.op("tensor",
                          lambda t, b=b, tt=tt, n=n, k=k: t.matmul(
                              bk.ap[b],
                              ckv[:, k * S + tt * 128:k * S + (tt + 1) * 128],
                              wv[:, k * D + n * 512: k * D + (n + 1) * 512],
                              start=(k == 0), stop=(k == 1)),
                          waits=waits if k == 0 else (),
                          incs=[(self.s_mm, 1)] if k == 1 else ())
                bk.wrote(b)
                pr.op("vector",
                      lambda v, b=b, tt=tt, n=n: v.tensor_copy(
                          v_aug[:, tt * 1040 + n * 8 * 65: tt * 1040 + (n + 1) * 8 * 65]
                          .rearrange("p (h c) -> p h c", h=8)[:, :, 0:64],
                          bk.ap[b].rearrange("p (h c) -> p h c", h=8)),
                      waits=bk.raw_wait(b), incs=[(self.s_ve, 1)])
                bk.read(b, "vector")
        v_done = self.s_ve.count
        mm_after_v = self.s_mm.count
        ar.free("wv")

        rotr = ar.alloc("rotr", 2 * 512 * 4, F32, 2 * 512, side="high")
        rot_free = {}
        for i, src in enumerate((qr, kr)):
            for c in range(NCH):
                par = (i * NCH + c) % 2
                rsem = self.s_rope[par]
                rsl = rotr[:, par * 512:(par + 1) * 512]
                w = [(self.s_sc, proj_done), (self.s_small, small_done),
                     (self.s_mm, mm_after_v)]
                if par in rot_free:
                    w.append((self.s_ve, rot_free[par]))
                pr.dma("sync", rsl[0:64, :],
                       src[64:128, c * 512:(c + 1) * 512], rsem, waits=w)
                pr.dma("sync", rsl[64:128, :],
                       src[0:64, c * 512:(c + 1) * 512], rsem)
                swap_done = (rsem, rsem.count)
                pr.op("vector",
                      lambda v, rsl=rsl, c=c: v.tensor_tensor(
                          rsl, rsl, sin[:, c * 512:(c + 1) * 512], ALU.mult),
                      waits=[swap_done])
                pr.op("vector",
                      lambda v, src=src, c=c: v.tensor_tensor(
                          src[:, c * 512:(c + 1) * 512],
                          src[:, c * 512:(c + 1) * 512],
                          cos[:, c * 512:(c + 1) * 512], ALU.mult))
                pr.op("vector",
                      lambda v, src=src, c=c, rsl=rsl: v.tensor_tensor(
                          src[:, c * 512:(c + 1) * 512],
                          src[:, c * 512:(c + 1) * 512], rsl, ALU.add),
                      incs=[(self.s_ve, 1)])
                rot_free[par] = self.s_ve.count
        rope_done = self.s_ve.count
        ar.free("rotr", "cos", "sin")

        ar.free("xbf")
        q_fm = ar.alloc("q_fm", DT * S * 2, BF16, DT * S)
        wqc = self.wload("wqc", d["wqc"], 2, D - DR, self.s_wA2,
                         waits=[(self.s_mm, self.s_mm.count),
                                (self.s_ve, self.s_ve.count),
                                (self.s_sc, self.s_sc.count)])

        def projc(dst, src, w, bias, w_done):
            for m in range(7):
                for c in range(NCH):
                    b = (m * NCH + c) % 2
                    waits = bk.war_waits(b) + [(self.s_wA2, w_done),
                                               (self.s_sc, proj_done)]
                    for k in range(2):
                        pr.op("tensor",
                              lambda t, b=b, m=m, c=c, k=k, w=w, src=src: t.matmul(
                                  bk.ap[b],
                                  w[:, k * 896 + m * 128: k * 896 + (m + 1) * 128],
                                  src[:, k * S + c * 512: k * S + (c + 1) * 512],
                                  start=(k == 0), stop=(k == 1)),
                              waits=waits if k == 0 else (),
                              incs=[(self.s_mm, 1)] if k == 1 else ())
                    bk.wrote(b)
                    pr.op("scalar",
                          lambda s, b=b, m=m, c=c, dst=dst, bias=bias: s.activation(
                              dst[:, m * S + c * 512: m * S + (c + 1) * 512],
                              bk.ap[b], AF.Gelu, bias=bias[:, m:m + 1]),
                          waits=bk.raw_wait(b) + [(self.s_small, small_done)],
                          incs=[(self.s_sc, 1)])
                    bk.read(b, "scalar")

        projc(q_fm, cq, wqc, small["bqc"], self.s_wA2.count)
        pr.op("scalar", lambda s: s.activation(q_fm[:, 7 * S:8 * S], qr, AF.Gelu),
              waits=[(self.s_ve, rope_done)], incs=[(self.s_sc, 1)])
        ar.free("cq", "qr", "wqc")

        wkc = self.wload("wkc", d["wkc"], 2, D - DR, self.s_wA2,
                         waits=[(self.s_mm, self.s_mm.count),
                                (self.s_ve, self.s_ve.count),
                                (self.s_sc, self.s_sc.count)])
        k7 = ar.alloc("k7", S * 2, BF16, S, side="high")
        pr.op("scalar", lambda s: s.activation(k7, kr, AF.Gelu),
              waits=[(self.s_ve, rope_done)], incs=[(self.s_sc, 1)])
        k7_done = self.s_sc.count
        ar.free("kr")
        k_fm = ar.alloc("k_fm", DT * S * 2, BF16, DT * S)
        projc(k_fm, ckv, wkc, small["bkc"], self.s_wA2.count)
        pr.op("vector", lambda v: v.tensor_copy(k_fm[:, 7 * S:8 * S], k7),
              waits=[(self.s_sc, k7_done)], incs=[(self.s_ve, 1)])
        qk_done_sc = self.s_sc.count
        qk_done_ve = self.s_ve.count
        ar.free("k7", "wkc")

        ar.free("ckv")

        sqt = ar.alloc("sqt_mh", 512 * 2, BF16, 512, side="high")
        pr.op("vector", lambda v: v.memset(self.mq, 0.0))
        pr.op("vector", lambda v: v.memset(self.mk, 0.0), incs=[(self.s_ve, 1)])
        for src, dst, rdy in ((q_fm, self.mq, qk_done_sc), (k_fm, self.mk, qk_done_ve)):
            for dd in range(DT):
                for c in range(NCH):
                    w = ([(self.s_sc, qk_done_sc), (self.s_ve, qk_done_ve),
                          (self.s_mm, mm_after_v)]
                         if (dd == 0 and c == 0) else [])
                    w += [(self.s_mm, bk.mm_done[4])]
                    pr.op("scalar",
                          lambda s, dd=dd, c=c, src=src: s.activation(
                              sqt, src[:, dd * S + c * 512: dd * S + (c + 1) * 512],
                              AF.Square),
                          waits=w, incs=[(self.s_sc, 1)])
                    sqd = self.s_sc.count
                    w4 = bk.war_waits(4) + [(self.s_sc, sqd)]
                    pr.op("tensor",
                          lambda t: t.matmul(bk.ap[4][0:1, :], self.ones_bf, sqt,
                                             start=True, stop=True),
                          waits=w4, incs=[(self.s_mm, 1)])
                    bk.wrote(4)
                    pr.op("vector",
                          lambda v, c=c: v.reduce_max(self.red[:, c:c + 1],
                                                      bk.ap[4][0:1, :], axis=AX.X),
                          waits=bk.raw_wait(4))
                    bk.read(4, "vector")
                    pr.op("vector",
                          lambda v, dd=dd, c=c, dst=dst: v.tensor_tensor(
                              dst[:, dd:dd + 1], dst[:, dd:dd + 1],
                              self.red[:, c:c + 1], ALU.max),
                          incs=[(self.s_ve, 1)])
        ar.free("sqt_mh")
        stat_done = self.s_ve.count
        pr.op("scalar", lambda s: s.activation(self.mq, self.mq, AF.Sqrt),
              waits=[(self.s_ve, stat_done)])
        pr.op("scalar", lambda s: s.activation(self.mk, self.mk, AF.Sqrt),
              incs=[(self.s_sc, 1)])
        pr.op("vector", lambda v: v.tensor_tensor(self.mq, self.mq, self.mk, ALU.mult),
              waits=[(self.s_sc, self.s_sc.count)])
        pr.op("vector", lambda v: v.tensor_scalar(self.mq, self.mq, -0.125, None, ALU.mult),
              incs=[(self.s_ve, 1)])
        w4 = bk.war_waits(4) + [(self.s_ve, self.s_ve.count)]
        pr.op("tensor", lambda t: t.matmul(bk.ap[4][:, 0:DT], self.ones_f1, self.mq,
                                           start=True, stop=True),
              waits=w4, incs=[(self.s_mm, 1)])
        bk.wrote(4)
        pr.op("scalar", lambda s: s.copy(self.negmb, bk.ap[4][:, 0:DT]),
              waits=bk.raw_wait(4), incs=[(self.s_sc, 1)])
        bk.read(4, "scalar")
        negmb_done = self.s_sc.count

        self.rz = ar.alloc("rz", 2 * 512 * 4, F32, 2 * 512, side="high")[0:1, :]
        self.rzb = ar.alloc("rzb", 2 * 512 * 4, F32, 2 * 512, side="high")[0:64, :]
        self.ctx32 = ar.alloc("ctx32", 2 * 512 * 4, F32, 2 * 512, side="high")[0:64, :]
        self.ctx16 = ar.alloc("ctx16", 2 * 512 * 2, BF16, 2 * 512, side="high")[0:64, :]
        masks_done = small_done
        wo = self.wload("wo", d["wo"], DT, D, self.s_wA2,
                        waits=[(self.s_mm, self.s_mm.count),
                               (self.s_ve, self.s_ve.count),
                               (self.s_sc, self.s_sc.count)])
        wo_done = self.s_wA2.count
        E = ar.alloc("E", 8 * 512 * 2, BF16, 8 * 512, side="high")
        ectx = {}
        ei = 0
        ctx16_dma = {}
        prev = None
        SCB = (0, 1, 6, 7)

        def tail_rz(u):
            for i, cb in enumerate(u["cb"]):
                pr.op("scalar", lambda sx, cb=cb, i=i: sx.activation(
                    self.rz[:, i * 512:(i + 1) * 512], bk.ap[cb][64:65, :], AF.Ln),
                    waits=bk.raw_wait(cb))
                pr.op("scalar", lambda sx, i=i: sx.activation(
                    self.rz[:, i * 512:(i + 1) * 512],
                    self.rz[:, i * 512:(i + 1) * 512], AF.Exp, scale=-1.0))
                bk.read(cb, "scalar")
            u["rz_sc"] = self.s_sc.count

        def tail_bcast(u, i, bb):
            w5 = bk.war_waits(bb) + [(self.s_sc, u["rz_sc"])]
            pr.op("tensor", lambda t, bb=bb, i=i: t.matmul(
                bk.ap[bb][0:64, :], self.ones_f1[:, 0:64],
                self.rz[:, i * 512:(i + 1) * 512], start=True, stop=True),
                waits=w5, incs=[(self.s_mm, 1)])
            bk.wrote(bb)
            u["bb%d" % i] = bb

        def tail_evac1(u):
            for i, cb in enumerate(u["cb"]):
                bb = u["bb%d" % i]
                pr.op("scalar", lambda sx, bb=bb, i=i: sx.copy(
                    self.rzb[:, i * 512:(i + 1) * 512], bk.ap[bb][0:64, :]),
                    waits=bk.raw_wait(bb))
                bk.read(bb, "scalar")
                pr.op("vector",
                      lambda v, cb=cb, i=i: v.tensor_tensor(
                          self.ctx32[:, i * 512:(i + 1) * 512], bk.ap[cb][0:64, :],
                          self.rzb[:, i * 512:(i + 1) * 512], ALU.mult),
                      waits=[(self.s_sc, self.s_sc.count)])
                bk.read(cb, "vector")
            u["tt_ve"] = self.s_ve.count

        def tail_evac2(u):
            dd, qc = u["dd"], u["qc"]
            pr.op("scalar",
                  lambda sx, dd=dd, qc=qc: sx.activation(
                      q_fm[0:64, dd * S + qc * 512: dd * S + (qc + 1) * 512],
                      self.ctx32[:, 0:512], AF.Identity,
                      bias=small["bv"][0:64, dd:dd + 1]),
                  waits=[(self.s_ve, u["tt_ve"])])
            par = (dd * NCH + qc) % 2
            csem = self.s_ctx[par]
            w = []
            if par in ctx16_dma:
                w.append((csem, ctx16_dma[par]))
            pr.op("scalar",
                  lambda sx, dd=dd, par=par: sx.activation(
                      self.ctx16[:, par * 512:(par + 1) * 512],
                      self.ctx32[:, 512:1024], AF.Identity,
                      bias=small["bv"][64:128, dd:dd + 1]),
                  waits=w)
            pr.dma("sync",
                   q_fm[64:128, dd * S + qc * 512: dd * S + (qc + 1) * 512],
                   self.ctx16[:, par * 512:(par + 1) * 512],
                   csem, waits=[(self.s_sc, self.s_sc.count)])
            ctx16_dma[par] = csem.count

        def emit_sT(kb, dd, qc, first):
            nonlocal ei
            out = []
            for i in range(2):
                sb_ = SCB[(2 * ei + i) % 4]
                ew = (2 * ei + i) % 8
                po = i * 64
                waits = bk.war_waits(sb_) + (
                    [(self.s_sc, qk_done_sc), (self.s_ve, v_done)]
                    if (first and i == 0) else [])
                pr.op("tensor",
                      lambda t, sb_=sb_, dd=dd, po=po, kb=kb, qc=qc, i=i: t.matmul(
                          bk.ap[sb_],
                          k_fm[po:po + 64, dd * S + kb * 128: dd * S + (kb + 1) * 128],
                          q_fm[po:po + 64, dd * S + qc * 512: dd * S + (qc + 1) * 512],
                          start=True, stop=True),
                      waits=waits, incs=[(self.s_mm, 1)])
                bk.wrote(sb_)
                ew_war = [(self.s_mm, ectx[ew])] if ew in ectx else []
                pr.op("scalar",
                      lambda sx, sb_=sb_, ew=ew, dd=dd: sx.activation(
                          E[:, ew * 512:(ew + 1) * 512], bk.ap[sb_], AF.Exp,
                          bias=self.negmb[:, dd:dd + 1], scale=QSCALE),
                      waits=bk.raw_wait(sb_) + ew_war +
                      ([(self.s_sc, negmb_done)] if (first and i == 0) else []))
                bk.read(sb_, "scalar")
                e_ready = self.s_sc.count
                mask_j = kb - 4 * qc
                if mask_j >= 0:
                    pr.op("vector",
                          lambda v, ew=ew, j=mask_j: v.tensor_tensor(
                              E[:, ew * 512:(ew + 1) * 512],
                              E[:, ew * 512:(ew + 1) * 512],
                              masks[:, j * 512:(j + 1) * 512], ALU.mult),
                          waits=[(self.s_sc, e_ready)] +
                          ([(self.s_small, masks_done)] if ei == 0 else []))
                    ewait = [(self.s_ve, self.s_ve.count)]
                else:
                    ewait = [(self.s_sc, e_ready)]
                out.append((ew, ewait))
            ei += 1
            return out

        def emit_ctx(cbs, kb, dd, pair, nkb, ctx_war):
            for i in range(2):
                ew, ewait = pair[i]
                h = 2 * dd + i
                cb = cbs[i]
                pr.op("tensor",
                      lambda t, cb=cb, kb=kb, h=h, ew=ew, nkb=nkb: t.matmul(
                          bk.ap[cb][0:65, :],
                          v_aug[:, kb * 1040 + h * 65: kb * 1040 + (h + 1) * 65],
                          E[:, ew * 512:(ew + 1) * 512],
                          start=(kb == 0), stop=(kb == nkb - 1)),
                      waits=ewait + (ctx_war[i] if kb == 0 else []),
                      incs=[(self.s_mm, 1)])
                ectx[ew] = self.s_mm.count

        for dd in range(DT):
            for qc in range(NCH):
                nkb = 4 * (qc + 1)
                ui = dd * NCH + qc
                cbs = (2, 3) if ui % 2 == 0 else (4, 5)
                ctx_war = [bk.war_waits(cbs[0]), bk.war_waits(cbs[1])]
                pend = []
                if prev is not None:
                    tail_rz(prev)
                for kb in range(nkb):
                    pair = emit_sT(kb, dd, qc, ei == 0)
                    pend.append((kb, pair))
                    if kb == 1 and prev is not None:
                        tail_bcast(prev, 0, SCB[(2 * ei) % 4])
                        tail_bcast(prev, 1, SCB[(2 * ei + 1) % 4])
                        tail_evac1(prev)
                    if kb == 2 and prev is not None:
                        tail_evac2(prev)
                    if len(pend) > 2:
                        kbp, pairp = pend.pop(0)
                        emit_ctx(cbs, kbp, dd, pairp, nkb, ctx_war)
                while pend:
                    kbp, pairp = pend.pop(0)
                    emit_ctx(cbs, kbp, dd, pairp, nkb, ctx_war)
                bk.wrote(cbs[0])
                bk.wrote(cbs[1])
                prev = {"cb": cbs, "dd": dd, "qc": qc}
        tail_rz(prev)
        tail_bcast(prev, 0, SCB[0])
        tail_bcast(prev, 1, SCB[1])
        tail_evac1(prev)
        tail_evac2(prev)
        ctx_sc = self.s_sc.count
        ar.free("k_fm", "v_aug", "E", "masks", "rz", "rzb", "ctx32", "ctx16")

        stA = self.ln_start("a")
        for c in range(NCH):
            for m in range(DT):
                b = (c * DT + m) % 4
                waits = bk.war_waits(b) + [(self.s_wA2, wo_done), (self.s_sc, ctx_sc),
                                           (self.s_ctx[0], self.s_ctx[0].count),
                                           (self.s_ctx[1], self.s_ctx[1].count)]
                for k in range(DT):
                    pr.op("tensor",
                          lambda t, b=b, m=m, c=c, k=k: t.matmul(
                              bk.ap[b],
                              wo[:, k * D + m * 128: k * D + (m + 1) * 128],
                              q_fm[:, k * S + c * 512: k * S + (c + 1) * 512],
                              start=(k == 0), stop=(k == DT - 1)),
                          waits=waits if k == 0 else (),
                          incs=[(self.s_mm, 1)] if k == DT - 1 else ())
                bk.wrote(b)
                pr.op("vector",
                      lambda v, b=b, m=m, c=c: v.scalar_tensor_tensor(
                          self.y[:, m * S + c * 512: m * S + (c + 1) * 512],
                          bk.ap[b], small["bo"][:, m:m + 1],
                          self.y[:, m * S + c * 512: m * S + (c + 1) * 512],
                          ALU.add, ALU.add),
                      waits=bk.raw_wait(b) + [(self.s_small, small_done)],
                      incs=[(self.s_ve, 1)])
                bk.read(b, "vector")
            self.ln_chunk(stA, c, wait=(self.s_ve, self.s_ve.count))
        ar.free("q_fm", "wo")
        ybf_a, done_a = self.ln_end(stA)
        return ybf_a, done_a, stA

    def mk_ybf(self, tag):
        pr, ar = self.pr, self.ar
        ybf = ar.alloc("ybf_" + tag, DT * S * 2, BF16, DT * S)
        for dd in range(DT):
            pr.op("scalar",
                  lambda s, dd=dd: s.copy(ybf[:, dd * S:(dd + 1) * S],
                                          self.y[:, dd * S:(dd + 1) * S]),
                  waits=self.ywait() if dd == 0 else ())
        return ybf, self.s_sc.count

    def phase_B_dense(self, ybf, ybf_done):
        pr, bk, ar, d = self.pr, self.banks, self.ar, self.d
        s_wg = pr.sem("wgld")
        s_bc = pr.sem("bcld")
        s_wB = [pr.sem("wB0"), pr.sem("wB1")]
        s_wsc = pr.sem("wscr")
        s_wbc = [pr.sem("wbcA"), pr.sem("wbcB")]

        cw = [(self.s_mm, self.s_mm.count), (self.s_ve, self.s_ve.count),
              (self.s_sc, self.s_sc.count)]
        identf = ar.alloc("identf", 128 * 4, F32, 128, side="high")
        self.oneh = ar.alloc("oneh", 32 * 128 * 2, BF16, 32 * 128, side="high")[0:32, :]
        pr.dma("sync", identf, d["identf"], s_bc, waits=cw)
        pr.dma("sync", self.oneh, d["oneh"], s_bc)
        wg = ar.alloc("wgmat", DT * EXP * 4, F32, DT * EXP, side="high")
        pr.dma("sync", wg.rearrange("p (a o) -> p a o", a=DT),
               d["Wg"].rearrange("(a p) o -> p a o", p=128), s_wg,
               waits=[(self.s_mm, self.s_mm.count), (self.s_ve, self.s_ve.count),
                      (self.s_sc, self.s_sc.count)])
        wg_done = s_wg.count
        bc_done = s_bc.count

        w_tm = ar.alloc("w_tm", 16 * EXP * 4, F32, 16 * EXP, side="high")
        w_fmT = ar.alloc("w_fmT", 16 * 128 * 2, BF16, 16 * 128, side="high")[0:32, :]
        gb, pw, msk, sel, mx1 = self.gb, self.pw, self.msk, self.sel, self.mx1

        w1b = [ar.alloc(f"w1b{p}", DT * HEXP * 2, BF16, DT * HEXP, side="high") for p in range(2)]
        w3b = [ar.alloc(f"w3b{p}", DT * HEXP * 2, BF16, DT * HEXP, side="high") for p in range(2)]
        w2b = [ar.alloc(f"w2b{p}", 4 * D * 2, BF16, 4 * D, side="high") for p in range(2)]
        wbcb = [ar.alloc(f"wbc{p}", S * 2, BF16, S, side="high") for p in range(2)]
        silu_s = ar.alloc("silu_s", 4 * 512 * 2, BF16, 4 * 512, side="high")
        hb = ar.alloc("hb", 4 * 512 * 2, BF16, 4 * 512, side="high")

        gball = ar.alloc("gball", 512 * 4, F32, 512, side="high")
        gsub = ar.alloc("gsub", 512 * 4, F32, 512, side="high")
        pwb = ar.alloc("pwb", 512 * 4, F32, 512, side="high")
        mskb = ar.alloc("mskb", 512 * 4, F32, 512, side="high")
        rst = ar.alloc("rst", 3 * 16 * 4, F32, 3 * 16, side="high")

        def routing_mms():
            for tt in range(16):
                b = 4 + tt % 2
                waits = bk.war_waits(b) + [(s_wg, wg_done)] + yw_chunk(tt // 4)
                for k in range(DT):
                    pr.op("tensor",
                          lambda t, b=b, tt=tt, k=k: t.matmul(
                              bk.ap[b][:, 0:EXP],
                              self.y[:, k * S + tt * 128:k * S + (tt + 1) * 128],
                              wg[:, k * EXP:(k + 1) * EXP],
                              start=(k == 0), stop=(k == DT - 1)),
                          waits=waits if k == 0 else (),
                          incs=[(self.s_mm, 1)] if k == DT - 1 else ())
                bk.wrote(b)
                pr.op("scalar",
                      lambda sx, b=b, tt=tt: sx.copy(gball[:, tt * 32:(tt + 1) * 32],
                                                     bk.ap[b][:, 0:EXP]),
                      waits=bk.raw_wait(b))
                bk.read(b, "scalar")
            return self.s_sc.count

        def routing_top4(g_sc):
            g3 = gball.rearrange("p (t e) -> p t e", t=16)
            pw3 = pwb.rearrange("p (t e) -> p t e", t=16)
            msk3 = mskb.rearrange("p (t e) -> p t e", t=16)
            w3 = w_tm.rearrange("p (t e) -> p t e", t=16)
            mx = rst[:, 0:16].rearrange("p (t o) -> p t o", o=1)
            m2 = rst[:, 16:32].rearrange("p (t o) -> p t o", o=1)
            bc = lambda apx: apx.broadcast_to([128, 16, 32])
            V, SX = "vector", "scalar"
            pr.op(V, lambda v: v.reduce_max(mx, g3, axis=AX.X),
                  waits=[(self.s_sc, g_sc)])
            pr.op(V, lambda v: v.tensor_tensor(
                gsub.rearrange("p (t e) -> p t e", t=16), g3, bc(mx), ALU.subtract))
            ve = self.s_ve.count
            pr.op(SX, lambda sx: sx.activation(gsub, gsub, AF.Exp),
                  waits=[(self.s_ve, ve)])
            e_sc = self.s_sc.count
            eg3 = gsub.rearrange("p (t e) -> p t e", t=16)
            pr.op(V, lambda v: v.reduce_sum(m2, eg3, axis=AX.X),
                  waits=[(self.s_sc, e_sc)])
            pr.op(V, lambda v: v.reciprocal(rst[:, 16:32], rst[:, 16:32]))
            pr.op(V, lambda v: v.tensor_tensor(w3, eg3, bc(m2), ALU.mult))
            pr.op(V, lambda v: v.tensor_copy(pwb, gball))
            for it in range(3):
                pr.op(V, lambda v: v.reduce_max(mx, pw3, axis=AX.X))
                pr.op(V, lambda v: v.tensor_tensor(msk3, pw3, bc(mx), ALU.is_ge))
                pr.op(V, lambda v: v.scalar_tensor_tensor(
                    pwb, mskb, -1e30, pwb, ALU.mult, ALU.add))
            pr.op(V, lambda v: v.reduce_max(mx, pw3, axis=AX.X))
            pr.op(V, lambda v: v.tensor_tensor(msk3, g3, bc(mx), ALU.is_ge))
            pr.op(V, lambda v: v.tensor_tensor(w_tm, w_tm, mskb, ALU.mult))
            return self.s_ve.count

        def routing_transposes(w_ve):
            for tt in range(16):
                b2 = 6 + tt % 2
                w2w = bk.war_waits(b2) + ([(self.s_ve, w_ve), (s_bc, bc_done)]
                                          if tt == 0 else [])
                pr.op("tensor",
                      lambda t, b2=b2, tt=tt: t.transpose(
                          bk.ap[b2][0:32, 0:128], w_tm[:, tt * EXP:(tt + 1) * EXP], identf),
                      waits=w2w, incs=[(self.s_mm, 1)])
                bk.wrote(b2)
                pr.op("scalar",
                      lambda sx, b2=b2, tt=tt: sx.copy(
                          w_fmT[:, tt * 128:(tt + 1) * 128], bk.ap[b2][0:32, 0:128]),
                      waits=bk.raw_wait(b2))
                bk.read(b2, "scalar")
            return self.s_sc.count

        def routing_wscr(r_sc):
            pr.dma("sync", self.w_scr, w_fmT, s_wsc, waits=[(self.s_sc, r_sc)])
            return s_wsc.count

        def emit_load(ei):
            par = ei % 2
            lw = [(self.s_mm, mm_end[ei - 2])] if ei - 2 in mm_end else [
                (self.s_mm, self.s_mm.count), (self.s_ve, self.s_ve.count),
                (self.s_sc, self.s_sc.count)]
            if ei < 2:
                half = ei
                src1 = d["Ws1"][:, half * HEXP:(half + 1) * HEXP]
                src3 = d["Ws3"][:, half * HEXP:(half + 1) * HEXP]
                src2 = d["Ws2"][half * HEXP:(half + 1) * HEXP, :]
            else:
                e = ei - 2
                src1, src3, src2 = d["W1"][e], d["W3"][e], d["W2"][e]
            pr.dma("gpsimd", w1b[par].rearrange("p (a o) -> p a o", a=DT),
                   src1.rearrange("(a p) o -> p a o", p=128), s_wB[par], waits=lw)
            pr.dma("gpsimd", w3b[par].rearrange("p (a o) -> p a o", a=DT),
                   src3.rearrange("(a p) o -> p a o", p=128), s_wB[par])
            pr.dma("gpsimd", w2b[par].rearrange("p (a o) -> p a o", a=4),
                   src2.rearrange("(a p) o -> p a o", p=128), s_wB[par])
            if ei >= 4:
                e = ei - 2
                wl = [(s_wsc, s_wsc.count)]
                if wbc_gp_free[par] > 0:
                    wl.append((self.s_gp, wbc_gp_free[par]))
                pr.dma("sync", wbcb[par],
                       self.w_scr[e:e + 1, :].broadcast_to([128, S]),
                       s_wbc[par], waits=wl)
                wbc_loaded[ei] = s_wbc[par].count
            wB_loaded[ei] = s_wB[par].count

        wB_loaded = {}
        wbc_loaded = {}
        mm_end = {}
        wB_cnt = [0, 0]
        wbc_free = [0, 0]
        wbc_gp_free = [0, 0]

        def emit_expert(ei):
            par = ei % 2
            shared = ei < 2
            e = ei - 2
            wB_cnt[par] = wB_loaded[ei]

            wbc_done = wbc_loaded.get(ei, 0)

            NCH2 = 8
            pend_c = None
            hw_gp_c = {}
            hready = {}

            def emit_w13(c):
                for m in range(4):
                    b = m // 2
                    off = (m % 2) * 256
                    w = bk.war_waits(b) + [(s_wB[par], wB_cnt[par])]
                    if ei == 0 and c == 0 and m == 0:
                        w.append((self.s_sc, ybf_done))
                    for k in range(DT):
                        pr.op("tensor",
                              lambda t, b=b, off=off, m=m, c=c, k=k, par=par: t.matmul(
                                  bk.ap[b][:, off:off + 256],
                                  w1b[par][:, k * HEXP + m * 128: k * HEXP + (m + 1) * 128],
                                  ybf[:, k * S + c * 256: k * S + (c + 1) * 256],
                                  start=(k == 0), stop=(k == DT - 1)),
                              waits=w if k == 0 else (),
                              incs=[(self.s_mm, 1)] if k == DT - 1 else ())
                    if m % 2 == 1:
                        bk.wrote(b)
                for m in range(4):
                    b = 2 + m // 2
                    off = (m % 2) * 256
                    w = bk.war_waits(b) + [(s_wB[par], wB_cnt[par])]
                    for k in range(DT):
                        pr.op("tensor",
                              lambda t, b=b, off=off, m=m, c=c, k=k, par=par: t.matmul(
                                  bk.ap[b][:, off:off + 256],
                                  w3b[par][:, k * HEXP + m * 128: k * HEXP + (m + 1) * 128],
                                  ybf[:, k * S + c * 256: k * S + (c + 1) * 256],
                                  start=(k == 0), stop=(k == DT - 1)),
                              waits=w if k == 0 else (),
                              incs=[(self.s_mm, 1)] if k == DT - 1 else ())
                    if m % 2 == 1:
                        bk.wrote(b)

            def emit_h(c):
                w2mm = self.s_mm.count
                prev_hv = hready[c - 1][0] if c - 1 in hready else 0
                sil_m = []
                for b in range(2):
                    pr.op("scalar",
                          lambda sx, b=b: sx.activation(
                              silu_s[:, b * 512:(b + 1) * 512], bk.ap[b], AF.Silu),
                          waits=bk.raw_wait(b) +
                          ([(self.s_ve, prev_hv)] if b == 0 else []))
                    bk.read(b, "scalar")
                    sil_m.append(self.s_sc.count)
                for b in range(2):
                    pr.op("vector",
                          lambda v, b=b: v.tensor_tensor(
                              hb[:, b * 512:(b + 1) * 512],
                              silu_s[:, b * 512:(b + 1) * 512],
                              bk.ap[2 + b], ALU.mult),
                          waits=bk.raw_wait(2 + b) + [(self.s_sc, sil_m[b]),
                                                      (self.s_mm, w2mm)])
                    bk.read(2 + b, "vector")
                hv = self.s_ve.count
                if not shared:
                    for m in range(4):
                        pr.op("gpsimd",
                              lambda g, m=m, par=par, c=c: g.tensor_tensor(
                                  hb[:, m * 256:(m + 1) * 256],
                                  hb[:, m * 256:(m + 1) * 256],
                                  wbcb[par][:, c * 256:(c + 1) * 256],
                                  ALU.mult),
                              waits=[(self.s_ve, hv - (1 - m // 2))] +
                              ([(s_wbc[par], wbc_done)] if (c == 0 and m == 0) else []))
                hready[c] = (self.s_ve.count, self.s_gp.count)

            def emit_w2(c):
                hv, hg = hready[c]
                for m2 in range(DT):
                    b = 4 + m2 // 2
                    off = (m2 % 2) * 256
                    w = bk.war_waits(b) + [(self.s_ve, hv), (self.s_gp, hg)]
                    for k in range(4):
                        pr.op("tensor",
                              lambda t, b=b, off=off, m2=m2, k=k, c=c, par=par: t.matmul(
                                  bk.ap[b][:, off:off + 256],
                                  w2b[par][:, k * D + m2 * 128: k * D + (m2 + 1) * 128],
                                  hb[:, k * 256:(k + 1) * 256],
                                  start=(k == 0), stop=(k == 3)),
                              waits=w if k == 0 else (),
                              incs=[(self.s_mm, 1)] if k == 3 else ())
                    if m2 % 2 == 1:
                        bk.wrote(b)
                for bb in range(4):
                    b = 4 + bb
                    pr.op("vector",
                          lambda v, b=b, bb=bb, c=c: v.tensor_tensor(
                              self.y.rearrange("p (a t) -> p a t", a=DT)
                              [:, 2 * bb:2 * bb + 2, c * 256:(c + 1) * 256],
                              self.y.rearrange("p (a t) -> p a t", a=DT)
                              [:, 2 * bb:2 * bb + 2, c * 256:(c + 1) * 256],
                              bk.ap[b].rearrange("p (a t) -> p a t", a=2),
                              ALU.add),
                          waits=bk.raw_wait(b))
                    bk.read(b, "vector")

            for c in range(NCH2):
                emit_w13(c)
                if pend_c is not None:
                    emit_w2(pend_c)
                emit_h(c)
                pend_c = c
            emit_w2(pend_c)
            mm_end[ei] = self.s_mm.count
            wbc_free[par] = 0
            wbc_gp_free[par] = self.s_gp.count
            if ei + 2 < 34:
                emit_load(ei + 2)

        emit_load(0)
        emit_load(1)
        g_sc = routing_mms()
        top4_ve = routing_top4(g_sc)
        emit_expert(0)
        tr_sc = routing_transposes(top4_ve)
        wsc_done = routing_wscr(tr_sc)
        for _e in (2, 3):
            par2 = _e % 2
            pr.dma("sync", wbcb[par2],
                   self.w_scr[_e - 2:_e - 1, :].broadcast_to([128, S]),
                   s_wbc[par2], waits=[(s_wsc, wsc_done)])
            wbc_loaded[_e] = s_wbc[par2].count
        emit_expert(1)
        for ei in range(2, 34):
            emit_expert(ei)
        ar.free("w1b0", "w1b1", "w3b0", "w3b1", "w2b0", "w2b1", "wbc0", "wbc1",
                "silu_s", "hb", "wgmat", "identf", "oneh", "w_tm", "w_fmT")
        self.y_ready = ("ve", self.s_ve.count)

    def phase_B(self, ybf, ybf_done, ln_hook=None, ln_st=None):
        def yw_chunk(c512):
            if ln_st is not None:
                return [(self.s_ve, ln_st["ve"][c512]), (self.s_sc, ln_st["sc"][c512])]
            return self.ywait() + [(self.s_sc, ybf_done)]
        pr, bk, ar, d = self.pr, self.banks, self.ar, self.d
        s_wg = pr.sem("wgld")
        s_wB = [pr.sem("wB0g"), pr.sem("wB1g")]
        s_w2 = pr.sem("w2g")
        s_xz = pr.sem("xzero")
        s_sct = [pr.sem("sct0"), pr.sem("sct1")]
        s_xg = [pr.sem("xg0"), pr.sem("xg1"), pr.sem("xg2")]
        s_oe = pr.sem("oest")
        s_cg = [pr.sem("cg0"), pr.sem("cg1"), pr.sem("cg2"), pr.sem("cg3")]

        cw = [(self.s_mm, self.s_mm.count), (self.s_ve, self.s_ve.count),
              (self.s_sc, self.s_sc.count)]
        identf = ar.alloc("identf", 128 * 4, F32, 128, side="high")
        iotae = ar.alloc("iotae", 16 * EXP * 4, F32, 16 * EXP, side="high")
        wg = ar.alloc("wgmat", DT * EXP * 4, F32, DT * EXP, side="high")
        pr.dma("sync", identf, d["identf"], s_wg, waits=cw)
        pr.dma("sync", iotae, d["iotae"], s_wg)
        pr.dma("sync", wg.rearrange("p (a o) -> p a o", a=DT),
               d["Wg"].rearrange("(a p) o -> p a o", p=128), s_wg)
        identb = ar.alloc("identb", 128 * 2, BF16, 128, side="high")
        pr.dma("sync", identb, d["identb"], s_wg)
        wg_done = s_wg.count

        sidx = self.sb("sidx", [128, 4 * 16], I32)
        wr = self.sb("wr", [128, 4 * 16], F32)

        w_tm = ar.alloc("w_tm", 16 * EXP * 4, F32, 16 * EXP, side="high")
        w_fmT = ar.alloc("w_fmT", 16 * 128 * 2, BF16, 16 * 128, side="high")[0:32, :]
        gball = ar.alloc("gball", 512 * 4, F32, 512, side="high")
        gsub = ar.alloc("gsub", 512 * 4, F32, 512, side="high")
        pwb = ar.alloc("pwb", 512 * 4, F32, 512, side="high")
        mskb = ar.alloc("mskb", 512 * 4, F32, 512, side="high")
        mxs = ar.alloc("mxs", 4 * 16 * 4, F32, 4 * 16, side="high")
        rs1 = ar.alloc("rs1", 16 * 4, F32, 16, side="high")
        pos_tm = ar.alloc("pos_tm", 16 * EXP * 4, F32, 16 * EXP, side="high")

        w1b = [ar.alloc(f"w1g{p}", DT * HEXP * 2, BF16, DT * HEXP, side="high") for p in range(2)]
        w3b = [ar.alloc(f"w3g{p}", DT * HEXP * 2, BF16, DT * HEXP, side="high") for p in range(2)]
        w2b = ar.alloc("w2g", 4 * D * 2, BF16, 4 * D, side="high")
        silu_s = ar.alloc("silu_sg", 4 * 512 * 2, BF16, 4 * 512, side="high")
        hb = ar.alloc("hbg", 4 * 512 * 2, BF16, 4 * 512, side="high")

        mm_end = {}
        wB_cnt = {}
        w2_cnt = {}

        def load13(ei):
            par = ei % 2
            lw = [(self.s_mm, mm_end[ei - 2])] if ei - 2 in mm_end else cw
            if ei < 2:
                src1 = d["Ws1"][:, ei * HEXP:(ei + 1) * HEXP]
                src3 = d["Ws3"][:, ei * HEXP:(ei + 1) * HEXP]
            else:
                src1, src3 = d["W1"][ei - 2], d["W3"][ei - 2]
            pr.dma("gpsimd", w1b[par].rearrange("p (a o) -> p a o", a=DT),
                   src1.rearrange("(a p) o -> p a o", p=128), s_wB[par], waits=lw)
            pr.dma("gpsimd", w3b[par].rearrange("p (a o) -> p a o", a=DT),
                   src3.rearrange("(a p) o -> p a o", p=128), s_wB[par])
            wB_cnt[ei] = s_wB[par].count

        def loadw2(ei):
            lw = [(self.s_mm, mm_end[ei - 1])] if ei - 1 in mm_end else cw
            src2 = (d["Ws2"][ei * HEXP:(ei + 1) * HEXP, :] if ei < 2
                    else d["W2"][ei - 2])
            pr.dma("gpsimd", w2b.rearrange("p (a o) -> p a o", a=4),
                   src2.rearrange("(a p) o -> p a o", p=128), s_w2, waits=lw)
            w2_cnt[ei] = s_w2.count

        zt = ar.alloc("zt", 1024 * 2, BF16, 1024, side="high")
        pr.op("vector", lambda v: v.memset(zt, 0.0))
        zt_ve = self.s_ve.count
        nxt = (EXP * CAP + 128) // 128
        xall_t = self.Xall.rearrange("(g p) d -> g p d", p=128)
        for g in range(nxt):
            pr.dma("sync", xall_t[g], zt, s_xz,
                   waits=[(self.s_ve, zt_ve)] if g == 0 else ())
        pr.dma("sync", self.OEall.rearrange("(g p) d -> g p d", p=128)[nxt - 1],
               zt, s_xz)
        xz_done = s_xz.count

        def routing_mms():
            for tt in range(16):
                b = 4 + tt % 2
                waits = bk.war_waits(b) + [(s_wg, wg_done)] + yw_chunk(tt // 4)
                for k in range(DT):
                    pr.op("tensor",
                          lambda t, b=b, tt=tt, k=k: t.matmul(
                              bk.ap[b][:, 0:EXP],
                              self.y[:, k * S + tt * 128:k * S + (tt + 1) * 128],
                              wg[:, k * EXP:(k + 1) * EXP],
                              start=(k == 0), stop=(k == DT - 1)),
                          waits=waits if k == 0 else (),
                          incs=[(self.s_mm, 1)] if k == DT - 1 else ())
                bk.wrote(b)
                pr.op("scalar",
                      lambda sx, b=b, tt=tt: sx.copy(gball[:, tt * 32:(tt + 1) * 32],
                                                     bk.ap[b][:, 0:EXP]),
                      waits=bk.raw_wait(b))
                bk.read(b, "scalar")
            return self.s_sc.count

        g3 = gball.rearrange("p (t e) -> p t e", t=16)
        pw3 = pwb.rearrange("p (t e) -> p t e", t=16)
        msk3 = mskb.rearrange("p (t e) -> p t e", t=16)
        bc = lambda apx: apx.broadcast_to([128, 16, 32])
        mxv = lambda r: mxs[:, r * 16:(r + 1) * 16].rearrange("p (t o) -> p t o", o=1)

        def routing_top4(g_sc):
            w3 = w_tm.rearrange("p (t e) -> p t e", t=16)
            m2 = rs1.rearrange("p (t o) -> p t o", o=1)
            V, SX = "vector", "scalar"
            pr.op(V, lambda v: v.tensor_copy(pwb, gball), waits=[(self.s_sc, g_sc)])
            for it in range(4):
                pr.op(V, lambda v, it=it: v.reduce_max(mxv(it), pw3, axis=AX.X))
                if it < 3:
                    pr.op(V, lambda v, it=it: v.tensor_tensor(msk3, pw3, bc(mxv(it)), ALU.is_ge))
                    pr.op(V, lambda v: v.scalar_tensor_tensor(
                        pwb, mskb, -1e30, pwb, ALU.mult, ALU.add))
            pr.op(V, lambda v: v.tensor_tensor(
                gsub.rearrange("p (t e) -> p t e", t=16), g3, bc(mxv(0)), ALU.subtract))
            ve = self.s_ve.count
            pr.op(SX, lambda sx: sx.activation(gsub, gsub, AF.Exp),
                  waits=[(self.s_ve, ve)])
            e_sc = self.s_sc.count
            eg3 = gsub.rearrange("p (t e) -> p t e", t=16)
            pr.op(V, lambda v: v.reduce_sum(m2, eg3, axis=AX.X),
                  waits=[(self.s_sc, e_sc)])
            pr.op(V, lambda v: v.reciprocal(rs1, rs1))
            pr.op(V, lambda v: v.tensor_tensor(msk3, g3, bc(mxv(3)), ALU.is_ge))
            pr.op(V, lambda v: v.tensor_tensor(
                w3, eg3, bc(m2), ALU.mult))
            pr.op(V, lambda v: v.tensor_tensor(w_tm, w_tm, mskb, ALU.mult))
            for r in range(4):
                pr.op(V, lambda v, r=r: v.tensor_tensor(
                    wr[:, r * 16:(r + 1) * 16], mxs[:, r * 16:(r + 1) * 16],
                    mxs[:, 0:16], ALU.subtract))
            ve2 = self.s_ve.count
            pr.op(SX, lambda sx: sx.activation(wr, wr, AF.Exp),
                  waits=[(self.s_ve, ve2)])
            e2 = self.s_sc.count
            pr.op(V, lambda v: v.tensor_tensor(
                wr.rearrange("p (r t) -> p r t", r=4), wr.rearrange("p (r t) -> p r t", r=4),
                rs1.rearrange("p (o t) -> p o t", o=1).broadcast_to([128, 4, 16]),
                ALU.mult), waits=[(self.s_sc, e2)])
            return self.s_ve.count

        def sel_transposes(w_ve):
            for tt in range(16):
                b2 = 6 + tt % 2
                w2w = bk.war_waits(b2) + ([(self.s_ve, w_ve), (s_wg, wg_done)]
                                          if tt == 0 else [])
                pr.op("tensor",
                      lambda t, b2=b2, tt=tt: t.transpose(
                          bk.ap[b2][0:32, 0:128], w_tm[:, tt * EXP:(tt + 1) * EXP], identf),
                      waits=w2w, incs=[(self.s_mm, 1)])
                bk.wrote(b2)
                pr.op("scalar",
                      lambda sx, b2=b2, tt=tt: sx.copy(
                          w_fmT[:, tt * 128:(tt + 1) * 128], bk.ap[b2][0:32, 0:128]),
                      waits=bk.raw_wait(b2))
                bk.read(b2, "scalar")
            return self.s_sc.count

        def positions(tr_sc):
            sel32 = ar.alloc("sel32", S * 4, F32, S, side="high")
            ones32 = ar.alloc("ones32", S * 4, F32, S, side="high")
            pos_fm = ar.alloc("pos_fm", S * 4, F32, S, side="high")
            pr.op("vector", lambda v: v.memset(ones32[0:32, :], 1.0))
            pr.op("vector", lambda v: v.tensor_scalar(
                sel32[0:32, :], w_fmT, 0.0, None, ALU.is_gt),
                waits=[(self.s_sc, tr_sc)])
            pr.op("vector", lambda v: v.tensor_tensor_scan(
                pos_fm[0:32, :], ones32[0:32, :], sel32[0:32, :], 0.0,
                ALU.mult, ALU.add))
            sc_ve = self.s_ve.count
            for tt in range(16):
                b2 = 6 + tt % 2
                w2w = bk.war_waits(b2) + ([(self.s_ve, sc_ve)] if tt == 0 else [])
                pr.op("tensor",
                      lambda t, b2=b2, tt=tt: t.transpose(
                          bk.ap[b2][0:128, 0:32],
                          pos_fm[0:32, tt * 128:(tt + 1) * 128], identf[0:32, 0:32]),
                      waits=w2w, incs=[(self.s_mm, 1)])
                bk.wrote(b2)
                pr.op("scalar",
                      lambda sx, b2=b2, tt=tt: sx.copy(
                          pos_tm[:, tt * 32:(tt + 1) * 32], bk.ap[b2][:, 0:32]),
                      waits=bk.raw_wait(b2))
                bk.read(b2, "scalar")
            ar.free("sel32", "ones32", "pos_fm")
            return self.s_sc.count

        def indices(pos_sc):
            iot3 = iotae.rearrange("p (t e) -> p t e", t=16)
            pos3 = pos_tm.rearrange("p (t e) -> p t e", t=16)
            tmp = gsub
            tmp3 = tmp.rearrange("p (t e) -> p t e", t=16)
            er = pwb
            for r in range(4):
                w = [(self.s_sc, pos_sc)] if r == 0 else []
                pr.op("vector", lambda v, r=r: v.tensor_tensor(
                    msk3, g3, bc(mxv(r)), ALU.is_equal), waits=w)
                pr.op("vector", lambda v: v.tensor_tensor(tmp3, msk3, iot3, ALU.mult))
                pr.op("vector", lambda v: v.reduce_sum(
                    er[:, 0:16].rearrange("p (t o) -> p t o", o=1), tmp3, axis=AX.X))
                pr.op("vector", lambda v: v.tensor_tensor(tmp3, msk3, pos3, ALU.mult))
                pr.op("vector", lambda v: v.reduce_sum(
                    er[:, 16:32].rearrange("p (t o) -> p t o", o=1), tmp3, axis=AX.X))
                pr.op("vector", lambda v: v.scalar_tensor_tensor(
                    er[:, 32:48], er[:, 0:16], float(CAP), er[:, 16:32],
                    ALU.mult, ALU.add))
                pr.op("vector", lambda v: v.tensor_scalar(
                    er[:, 32:48], er[:, 32:48], -1.0, None, ALU.add))
                pr.op("vector", lambda v, r=r: v.tensor_copy(
                    sidx[:, r * 16:(r + 1) * 16], er[:, 32:48]))
            return self.s_ve.count

        def emit_shared(ei):
            par = ei % 2
            NCH2 = 8
            pend_c = None
            hready = {}

            def emit_w13(c):
                for m in range(4):
                    b = m // 2
                    off = (m % 2) * 256
                    w = bk.war_waits(b) + [(s_wB[par], wB_cnt[ei])]
                    if ei == 0 and m == 0:
                        w += yw_chunk(c // 2)
                    for k in range(DT):
                        pr.op("tensor",
                              lambda t, b=b, off=off, m=m, c=c, k=k: t.matmul(
                                  bk.ap[b][:, off:off + 256],
                                  w1b[par][:, k * HEXP + m * 128: k * HEXP + (m + 1) * 128],
                                  ybf[:, k * S + c * 256: k * S + (c + 1) * 256],
                                  start=(k == 0), stop=(k == DT - 1)),
                              waits=w if k == 0 else (),
                              incs=[(self.s_mm, 1)] if k == DT - 1 else ())
                    if m % 2 == 1:
                        bk.wrote(b)
                for m in range(4):
                    b = 2 + m // 2
                    off = (m % 2) * 256
                    w = bk.war_waits(b) + [(s_wB[par], wB_cnt[ei])]
                    for k in range(DT):
                        pr.op("tensor",
                              lambda t, b=b, off=off, m=m, c=c, k=k: t.matmul(
                                  bk.ap[b][:, off:off + 256],
                                  w3b[par][:, k * HEXP + m * 128: k * HEXP + (m + 1) * 128],
                                  ybf[:, k * S + c * 256: k * S + (c + 1) * 256],
                                  start=(k == 0), stop=(k == DT - 1)),
                              waits=w if k == 0 else (),
                              incs=[(self.s_mm, 1)] if k == DT - 1 else ())
                    if m % 2 == 1:
                        bk.wrote(b)

            def emit_h(c):
                w2mm = self.s_mm.count
                prev_hv = hready[c - 1] if c - 1 in hready else 0
                sil_m = []
                for b in range(2):
                    pr.op("scalar",
                          lambda sx, b=b: sx.activation(
                              silu_s[:, b * 512:(b + 1) * 512], bk.ap[b], AF.Silu),
                          waits=bk.raw_wait(b) +
                          ([(self.s_ve, prev_hv)] if b == 0 else []))
                    bk.read(b, "scalar")
                    sil_m.append(self.s_sc.count)
                for b in range(2):
                    pr.op("vector",
                          lambda v, b=b: v.tensor_tensor(
                              hb[:, b * 512:(b + 1) * 512],
                              silu_s[:, b * 512:(b + 1) * 512],
                              bk.ap[2 + b], ALU.mult),
                          waits=bk.raw_wait(2 + b) + [(self.s_sc, sil_m[b]),
                                                      (self.s_mm, w2mm)])
                    bk.read(2 + b, "vector")
                hready[c] = self.s_ve.count

            def emit_w2(c):
                hv = hready[c]
                for m2 in range(DT):
                    b = 4 + m2 // 2
                    off = (m2 % 2) * 256
                    w = bk.war_waits(b) + [(self.s_ve, hv), (s_w2, w2_cnt[ei])]
                    for k in range(4):
                        pr.op("tensor",
                              lambda t, b=b, off=off, m2=m2, k=k, c=c: t.matmul(
                                  bk.ap[b][:, off:off + 256],
                                  w2b[:, k * D + m2 * 128: k * D + (m2 + 1) * 128],
                                  hb[:, k * 256:(k + 1) * 256],
                                  start=(k == 0), stop=(k == 3)),
                              waits=w if k == 0 else (),
                              incs=[(self.s_mm, 1)] if k == 3 else ())
                    if m2 % 2 == 1:
                        bk.wrote(b)
                for bb in range(4):
                    b = 4 + bb
                    pr.op("vector",
                          lambda v, b=b, bb=bb, c=c: v.tensor_tensor(
                              self.y.rearrange("p (a t) -> p a t", a=DT)
                              [:, 2 * bb:2 * bb + 2, c * 256:(c + 1) * 256],
                              self.y.rearrange("p (a t) -> p a t", a=DT)
                              [:, 2 * bb:2 * bb + 2, c * 256:(c + 1) * 256],
                              bk.ap[b].rearrange("p (a t) -> p a t", a=2),
                              ALU.add),
                          waits=bk.raw_wait(b))
                    bk.read(b, "vector")

            for c in range(NCH2):
                emit_w13(c)
                if pend_c is not None:
                    emit_w2(pend_c)
                emit_h(c)
                pend_c = c
            emit_w2(pend_c)
            mm_end[ei] = self.s_mm.count

        def x_stage(idx_ve):
            xst = [ar.alloc(f"xst{p}", 1024 * 2, BF16, 1024, side="high") for p in range(2)]
            sct_done = {}
            for g in range(16):
                par = g % 2
                for dd in range(DT):
                    b2 = 6 + dd % 2
                    w2w = bk.war_waits(b2) + (yw_chunk(g // 4) if dd == 0 else [])
                    pr.op("tensor",
                          lambda t, b2=b2, g=g, dd=dd: t.transpose(
                              bk.ap[b2].bitcast(BF16)[:, 0:128],
                              ybf[:, dd * S + g * 128:dd * S + (g + 1) * 128], identb),
                          waits=w2w, incs=[(self.s_mm, 1)])
                    bk.wrote(b2)
                    w = bk.raw_wait(b2)
                    if dd < 2 and par in sct_done:
                        w = w + [(s_sct[par], sct_done[par])]
                    pr.op("scalar",
                          lambda sx, b2=b2, par=par, dd=dd: sx.copy(
                              xst[par][:, dd * 128:(dd + 1) * 128],
                              bk.ap[b2].bitcast(BF16)[:, 0:128]),
                          waits=w)
                    bk.read(b2, "scalar")
                ev_sc = self.s_sc.count
                for r in range(4):
                    w = [(self.s_sc, ev_sc), (s_xz, xz_done)]
                    if r == 0:
                        w.append((self.s_ve, idx_ve))
                    pr.op("gpsimd",
                          lambda gp, par=par, r=r, g=g: gp.indirect_dma_start(
                              self.Xall,
                              IndirectOffsetOnAxis(sidx[:, r * 16 + g:r * 16 + g + 1], 0),
                              xst[par], None),
                          waits=w, incs=[(s_sct[par], 16)], _dma=True)
                sct_done[par] = s_sct[par].count
            ar.free("xst0", "xst1")
            return [(s_sct[0], s_sct[0].count), (s_sct[1], s_sct[1].count)]

        xe_trmm = {}

        def emit_xe_gather(e, scat_waits):
            par = e % 3
            w = list(scat_waits)
            if e - 3 in xe_trmm:
                w.append((self.s_mm, xe_trmm[e - 3]))
            else:
                w.append((self.s_mm, self.s_mm.count))
            for j in range(3):
                pr.dma("sync", xetm3[par][:, j * 1024:(j + 1) * 1024],
                       self.Xall.rearrange("(g p) d -> g p d", p=128)[(e * CAP) // 128 + j],
                       s_xg[par], waits=w if j == 0 else ())
            return s_xg[par].count

        def emit_xe_transpose(e, xg_c):
            par = e % 3
            pxf = e % 2
            for j in range(3):
                for dd in range(DT):
                    b2 = 6 + dd % 2
                    w2w = bk.war_waits(b2) + (
                        [(s_xg[par], xg_c)] if (j == 0 and dd == 0) else [])
                    pr.op("tensor",
                          lambda t, b2=b2, par=par, j=j, dd=dd: t.transpose(
                              bk.ap[b2].bitcast(BF16)[:, 0:128],
                              xetm3[par][:, j * 1024 + dd * 128: j * 1024 + (dd + 1) * 128],
                              identb),
                          waits=w2w, incs=[(self.s_mm, 1)])
                    bk.wrote(b2)
                    pr.op("scalar",
                          lambda sx, b2=b2, pxf=pxf, dd=dd, j=j: sx.copy(
                              xefm[pxf][:, dd * CAP + j * 128: dd * CAP + (j + 1) * 128],
                              bk.ap[b2].bitcast(BF16)[:, 0:128]),
                          waits=bk.raw_wait(b2))
                    bk.read(b2, "scalar")
            xe_trmm[e] = self.s_mm.count
            return self.s_sc.count

        def emit_routed(e, xe_sc, mid=None):
            ei = e + 2
            par = e % 2
            for which, wb, b0 in ((0, w1b, 0), (1, w3b, 4)):
                for m in range(4):
                    b = b0 + m
                    w = bk.war_waits(b) + [(s_wB[par], wB_cnt[ei]),
                                           (self.s_sc, xe_sc)]
                    for k in range(DT):
                        pr.op("tensor",
                              lambda t, b=b, m=m, k=k, wb=wb: t.matmul(
                                  bk.ap[b][:, 0:CAP],
                                  wb[par][:, k * HEXP + m * 128: k * HEXP + (m + 1) * 128],
                                  xefm[par][:, k * CAP:(k + 1) * CAP],
                                  start=(k == 0), stop=(k == DT - 1)),
                              waits=w if k == 0 else (),
                              incs=[(self.s_mm, 1)] if k == DT - 1 else ())
                    bk.wrote(b)
            sil_m = []
            for m in range(4):
                pr.op("scalar",
                      lambda sx, m=m: sx.activation(
                          hbr[:, m * CAP:(m + 1) * CAP], bk.ap[m][:, 0:CAP], AF.Silu),
                      waits=bk.raw_wait(m))
                bk.read(m, "scalar")
                sil_m.append(self.s_sc.count)
            for m in range(4):
                pr.op("vector",
                      lambda v, m=m: v.tensor_tensor(
                          hbr[:, m * CAP:(m + 1) * CAP],
                          hbr[:, m * CAP:(m + 1) * CAP],
                          bk.ap[4 + m][:, 0:CAP], ALU.mult),
                      waits=bk.raw_wait(4 + m) + [(self.s_sc, sil_m[m])])
                bk.read(4 + m, "vector")
            h_ve = self.s_ve.count
            mid_sc = mid() if mid is not None else None
            oev = []
            for st in range(3):
                for do in range(2):
                    b = st * 2 + do
                    w = bk.war_waits(b) + [(self.s_ve, h_ve), (s_w2, w2_cnt[ei])]
                    for k in range(4):
                        pr.op("tensor",
                              lambda t, b=b, st=st, do=do, k=k: t.matmul(
                                  bk.ap[b],
                                  hbr[:, k * CAP + st * 128: k * CAP + (st + 1) * 128],
                                  w2b[:, k * D + do * 512:(k * D) + (do + 1) * 512],
                                  start=(k == 0), stop=(k == 3)),
                              waits=w if k == 0 else (),
                              incs=[(self.s_mm, 1)] if k == 3 else ())
                    bk.wrote(b)
                    pr.op("vector",
                          lambda v, b=b, st=st, do=do: v.tensor_copy(
                              oest[:, st * 1024 + do * 512: st * 1024 + (do + 1) * 512],
                              bk.ap[b]),
                          waits=bk.raw_wait(b) + ([(s_oe, s_oe.count)]
                                                  if (st == 0 and do == 0) else []))
                    bk.read(b, "vector")
            oe_ve = self.s_ve.count
            pr.dma("sync",
                   self.OEall.rearrange("(g p) d -> p g d", p=128)
                   [:, (e * CAP) // 128:(e * CAP) // 128 + 3, :],
                   oest.rearrange("p (g d) -> p g d", g=3),
                   s_oe, waits=[(self.s_ve, oe_ve)])
            mm_end[ei] = self.s_mm.count

        def combine(oe_total):
            gbuf = [ar.alloc(f"gcb{p}", 4 * 1024 * 2, BF16, 4 * 1024, side="high")
                    for p in range(4)]
            dgall = ar.alloc("dgall", 64 * 128 * 2, BF16, 64 * 128, side="high")
            dgc = {}
            for g in range(16):
                for r in range(4):
                    gr = r * 16 + g
                    pr.op("vector",
                          lambda v, gr=gr: v.tensor_scalar(
                              dgall[:, gr * 128:(gr + 1) * 128], identb,
                              wr[:, gr:gr + 1], None, ALU.mult))
                dgc[g] = self.s_ve.count
            trdone = {}
            for g in range(16):
                par = g % 4
                for r in range(4):
                    w = [(s_oe, oe_total)] if (g < 4 and r == 0) else []
                    if r == 0 and par in trdone:
                        w.append((self.s_mm, trdone[par]))
                    pr.op("gpsimd",
                          lambda gp, par=par, r=r, g=g: gp.indirect_dma_start(
                              gbuf[par][:, r * 1024:(r + 1) * 1024], None,
                              self.OEall,
                              IndirectOffsetOnAxis(sidx[:, r * 16 + g:r * 16 + g + 1], 0)),
                          waits=w, incs=[(s_cg[par], 16)], _dma=True)
                cg_c = s_cg[par].count
                for half in range(2):
                    b2 = 6 + half
                    w2w = bk.war_waits(b2) + [(s_cg[par], cg_c),
                                              (self.s_ve, dgc[g])]
                    first = True
                    for j in range(4):
                        dd = half * 4 + j
                        for r in range(4):
                            pr.op("tensor",
                                  lambda t, b2=b2, par=par, dd=dd, j=j, r=r, g=g: t.matmul(
                                      bk.ap[b2][:, j * 128:(j + 1) * 128],
                                      gbuf[par][:, r * 1024 + dd * 128:
                                                r * 1024 + (dd + 1) * 128],
                                      dgall[:, (r * 16 + g) * 128:
                                             (r * 16 + g + 1) * 128],
                                      start=(r == 0), stop=(r == 3)),
                                  waits=w2w if first else (),
                                  incs=[(self.s_mm, 1)]
                                  if (j == 3 and r == 3) else ())
                            first = False
                    bk.wrote(b2)
                    pr.op("vector",
                          lambda v, b2=b2, g=g, half=half: v.tensor_tensor(
                              self.y.rearrange("p (a t) -> p a t", a=DT)
                              [:, half * 4:(half + 1) * 4, g * 128:(g + 1) * 128],
                              self.y.rearrange("p (a t) -> p a t", a=DT)
                              [:, half * 4:(half + 1) * 4, g * 128:(g + 1) * 128],
                              bk.ap[b2].rearrange("p (a t) -> p a t", a=4),
                              ALU.add),
                          waits=bk.raw_wait(b2))
                    bk.read(b2, "vector")
                trdone[par] = self.s_mm.count
                if g % 4 == 3 and ln_hook is not None:
                    ln_hook(g // 4, (self.s_ve, self.s_ve.count))
            ar.free("gcb0", "gcb1", "gcb2", "gcb3", "dgall")

        load13(0)
        load13(1)
        loadw2(0)
        g_sc = routing_mms()
        top4_ve = routing_top4(g_sc)
        emit_shared(0)
        loadw2(1)
        load13(2)
        tr_sc = sel_transposes(top4_ve)
        pos_sc = positions(tr_sc)
        idx_ve = indices(pos_sc)
        scat_waits = x_stage(idx_ve)
        emit_shared(1)
        loadw2(2)
        load13(3)
        ar.free("ybf_" + self.ybf_tag)
        xetm3 = [ar.alloc(f"xetm{p}", 3 * 1024 * 2, BF16, 3 * 1024, side="high") for p in range(3)]
        xefm = [ar.alloc(f"xefm{p}", DT * CAP * 2, BF16, DT * CAP, side="high") for p in range(2)]
        hbr = ar.alloc("hbr", 4 * CAP * 2, BF16, 4 * CAP, side="high")
        oest = ar.alloc("oest", 3 * 1024 * 2, BF16, 3 * 1024, side="high")
        xg_c = {0: emit_xe_gather(0, scat_waits)}
        xg_c[1] = emit_xe_gather(1, scat_waits)
        xe_sc = {0: emit_xe_transpose(0, xg_c[0])}
        for e in range(32):
            if e + 2 < 32:
                xg_c[e + 2] = emit_xe_gather(e + 2, scat_waits)

            def mid(e=e):
                if e + 1 < 32:
                    xe_sc[e + 1] = emit_xe_transpose(e + 1, xg_c[e + 1])
                return None
            emit_routed(e, xe_sc[e], mid=mid)
            if e + 4 < 34:
                load13(e + 4)
            if e + 3 < 34:
                loadw2(e + 3)
        ar.free("iotae", "wgmat", "w_tm", "w_fmT", "gball",
                "gsub", "pwb", "mskb", "mxs", "rs1", "pos_tm",
                "w1g0", "w1g1", "w3g0", "w3g1", "w2g", "silu_sg", "hbg", "zt",
                "xetm0", "xetm1", "xefm0", "xefm1", "hbr", "oest")
        combine(s_oe.count)
        ar.free("identf", "identb")
        self.y_ready = ("ve", self.s_ve.count)

    def phase_C(self, ybf, ybf_done, free_ybf=None, ln_st=None):
        def yw_chunk(c512):
            if ln_st is not None:
                return [(self.s_sc, ln_st["sc"][c512])]
            return [(self.s_sc, ybf_done)]
        pr, bk, ar, d = self.pr, self.banks, self.ar, self.d
        s_wC = pr.sem("wC")
        cw = [(self.s_mm, self.s_mm.count), (self.s_ve, self.s_ve.count),
              (self.s_sc, self.s_sc.count)]
        wqT = ar.alloc("l1wq", DT * 64 * 2, BF16, DT * 64, side="high")
        wkT = ar.alloc("l1wk", DT * 64 * 2, BF16, DT * 64, side="high")
        wvT = ar.alloc("l1wv", DT * 64 * 2, BF16, DT * 64, side="high")
        woT = ar.alloc("l1wo", D * 2, BF16, D, side="high")[0:64, :]
        pr.dma("gpsimd", wqT.rearrange("p (a o) -> p a o", a=DT),
               d["l1wqT"].rearrange("(a p) o -> p a o", p=128), s_wC, waits=cw)
        pr.dma("gpsimd", wkT.rearrange("p (a o) -> p a o", a=DT),
               d["l1wkT"].rearrange("(a p) o -> p a o", p=128), s_wC)
        pr.dma("gpsimd", wvT.rearrange("p (a o) -> p a o", a=DT),
               d["l1wvT"].rearrange("(a p) o -> p a o", p=128), s_wC)
        pr.dma("gpsimd", woT, d["l1woT"], s_wC)
        wC_done = s_wC.count

        Q_s = ar.alloc("l1Q", S * 2, BF16, S, side="high")[0:64, :]
        K_s = ar.alloc("l1K", 16 * 64 * 2, BF16, 16 * 64, side="high")
        V_s = ar.alloc("l1V", 16 * 64 * 2, BF16, 16 * 64, side="high")
        en_s = ar.alloc("l1en", 64 * 2, BF16, 64, side="high")[0:64, :]
        ctx_s = ar.alloc("l1ctx", S * 2, BF16, S, side="high")[0:64, :]
        CSC = math.sqrt(3.0) / 8.0

        for c in range(NCH):
            b = c % 2
            w = bk.war_waits(b) + [(s_wC, wC_done)] + yw_chunk(c)
            for k in range(DT):
                pr.op("tensor",
                      lambda t, b=b, c=c, k=k: t.matmul(
                          bk.ap[b][0:64, :],
                          wqT[:, k * 64:(k + 1) * 64],
                          ybf[:, k * S + c * 512:k * S + (c + 1) * 512],
                          start=(k == 0), stop=(k == DT - 1)),
                      waits=w if k == 0 else (),
                      incs=[(self.s_mm, 1)] if k == DT - 1 else ())
            bk.wrote(b)
            pr.op("scalar",
                  lambda s, b=b, c=c: s.activation(
                      Q_s[:, c * 512:(c + 1) * 512], bk.ap[b][0:64, :],
                      AF.Identity, scale=CSC),
                  waits=bk.raw_wait(b))
            bk.read(b, "scalar")
        kv_half = {}
        for half in range(2):
            for which, wT, dst in ((0, wkT, K_s), (1, wvT, V_s)):
                b = 2 + which
                w = bk.war_waits(b) + [(s_wC, wC_done)] + yw_chunk(half * 2)
                if which == 0:
                    w += yw_chunk(half * 2 + 1)
                for sub in range(8):
                    tt = half * 8 + sub
                    for k in range(DT):
                        pr.op("tensor",
                              lambda t, b=b, tt=tt, k=k, sub=sub, wT=wT: t.matmul(
                                  bk.ap[b][:, sub * 64:(sub + 1) * 64],
                                  ybf[:, k * S + tt * 128:k * S + (tt + 1) * 128],
                                  wT[:, k * 64:(k + 1) * 64],
                                  start=(k == 0), stop=(k == DT - 1)),
                              waits=w if (sub == 0 and k == 0) else (),
                              incs=[(self.s_mm, 1)] if k == DT - 1 else ())
                bk.wrote(b)
                pr.op("vector",
                      lambda v, b=b, half=half, dst=dst: v.tensor_copy(
                          dst[:, half * 512:(half + 1) * 512], bk.ap[b]),
                      waits=bk.raw_wait(b))
                bk.read(b, "vector")
            kv_half[half] = self.s_ve.count
        kv_done = self.s_ve.count
        for half in range(2):
            ve_h = kv_half[half]
            w = (bk.war_waits(4) if half == 0 else []) + [(self.s_ve, ve_h)]
            for sub in range(8):
                tt = half * 8 + sub
                pr.op("tensor",
                      lambda t, tt=tt: t.matmul(
                          bk.ap[4][0:64, 0:64],
                          K_s[:, tt * 64:(tt + 1) * 64],
                          V_s[:, tt * 64:(tt + 1) * 64],
                          start=(tt == 0), stop=(tt == 15)),
                      waits=w if sub == 0 else (),
                      incs=[(self.s_mm, 1)] if tt == 15 else ())
        bk.wrote(4)
        pr.op("scalar", lambda s: s.copy(en_s, bk.ap[4][0:64, 0:64]),
              waits=bk.raw_wait(4))
        bk.read(4, "scalar")
        en_done = self.s_sc.count
        for c in range(NCH):
            b = 5 + c % 2
            w = bk.war_waits(b) + [(self.s_sc, en_done)]
            pr.op("tensor",
                  lambda t, b=b, c=c: t.matmul(
                      bk.ap[b][0:64, :], en_s, Q_s[:, c * 512:(c + 1) * 512],
                      start=True, stop=True),
                  waits=w, incs=[(self.s_mm, 1)])
            bk.wrote(b)
            pr.op("scalar",
                  lambda s, b=b, c=c: s.copy(ctx_s[:, c * 512:(c + 1) * 512],
                                             bk.ap[b][0:64, :]),
                  waits=bk.raw_wait(b))
            bk.read(b, "scalar")
        ctx_done = self.s_sc.count
        if free_ybf is not None:
            ar.free(free_ybf)
        stC = self.ln_start("c")
        for c in range(NCH):
            for m in range(DT):
                b = (c * DT + m) % 4
                w = bk.war_waits(b) + [(self.s_sc, ctx_done)]
                pr.op("tensor",
                      lambda t, b=b, m=m, c=c: t.matmul(
                          bk.ap[b],
                          woT[:, m * 128:(m + 1) * 128],
                          ctx_s[:, c * 512:(c + 1) * 512],
                          start=True, stop=True),
                      waits=w, incs=[(self.s_mm, 1)])
                bk.wrote(b)
                pr.op("vector",
                      lambda v, b=b, m=m, c=c: v.tensor_tensor(
                          self.y[:, m * S + c * 512: m * S + (c + 1) * 512],
                          self.y[:, m * S + c * 512: m * S + (c + 1) * 512],
                          bk.ap[b], ALU.add),
                      waits=bk.raw_wait(b), incs=[(self.s_ve, 1)])
                bk.read(b, "vector")
            self.ln_chunk(stC, c, wait=(self.s_ve, self.s_ve.count))
        ar.free("l1wq", "l1wk", "l1wv", "l1wo", "l1Q", "l1K", "l1V", "l1en", "l1ctx")
        ybf_c, done_c = self.ln_end(stC)
        return ybf_c, done_c, stC

    def phase_D_pre(self):
        pr, ar, d = self.pr, self.ar, self.d
        self._s_wD = [pr.sem("wD0"), pr.sem("wD1")]
        self._wgb = [ar.alloc(f"wgD{p}", DT * HEXP * 2, BF16, DT * HEXP, side="high") for p in range(2)]
        self._wub = [ar.alloc(f"wuD{p}", DT * HEXP * 2, BF16, DT * HEXP, side="high") for p in range(2)]
        self._wdb = [ar.alloc(f"wdD{p}", 4 * D * 2, BF16, 4 * D, side="high") for p in range(2)]
        self._wD_loaded = {}
        self._mm_endD = {}

        def emit_loadD(g):
            par = g % 2
            lw = ([(self.s_mm, self._mm_endD[g - 2])] if g - 2 in self._mm_endD
                  else [(self.s_mm, self.s_mm.count), (self.s_ve, self.s_ve.count),
                        (self.s_sc, self.s_sc.count)])
            pr.dma("gpsimd", self._wgb[par].rearrange("p (a o) -> p a o", a=DT),
                   d["wgT"][:, g * HEXP:(g + 1) * HEXP]
                   .rearrange("(a p) o -> p a o", p=128), self._s_wD[par], waits=lw)
            pr.dma("gpsimd", self._wub[par].rearrange("p (a o) -> p a o", a=DT),
                   d["wuT"][:, g * HEXP:(g + 1) * HEXP]
                   .rearrange("(a p) o -> p a o", p=128), self._s_wD[par])
            pr.dma("gpsimd", self._wdb[par].rearrange("p (a o) -> p a o", a=4),
                   d["wdT"][g * HEXP:(g + 1) * HEXP, :]
                   .rearrange("(a p) o -> p a o", p=128), self._s_wD[par])
            self._wD_loaded[g] = self._s_wD[par].count

        self._emit_loadD = emit_loadD
        emit_loadD(0)
        emit_loadD(1)

    def phase_D(self, ybf, ybf_done, ln_st=None, ln_hook=None):
        def yw_chunk(c512):
            if ln_st is not None:
                return [(self.s_sc, ln_st["sc"][c512])]
            return [(self.s_sc, ybf_done)]
        pr, bk, ar, d = self.pr, self.banks, self.ar, self.d
        if not hasattr(self, "_s_wD"):
            self.phase_D_pre()
        s_wD = self._s_wD
        NG = HID1 // HEXP
        wgb, wub, wdb = self._wgb, self._wub, self._wdb
        silu_s = ar.alloc("siluD", 4 * 512 * 2, BF16, 4 * 512, side="high")
        hb = ar.alloc("hbD", 4 * 512 * 2, BF16, 4 * 512, side="high")
        SQ = float(INVQ) * float(INVQ)

        mm_end = self._mm_endD
        wD_cnt = [0, 0]
        wD_loaded = self._wD_loaded
        emit_loadD = self._emit_loadD
        for g in range(NG):
            par = g % 2
            wD_cnt[par] = wD_loaded[g]

            NCH2 = 8
            pend_c = None
            hready = {}

            def emit_gu(c):
                for m in range(4):
                    b = m // 2
                    off = (m % 2) * 256
                    w = bk.war_waits(b) + [(s_wD[par], wD_cnt[par])]
                    if g == 0 and m == 0:
                        w += yw_chunk(c // 2)
                    for k in range(DT):
                        pr.op("tensor",
                              lambda t, b=b, off=off, m=m, c=c, k=k, par=par: t.matmul(
                                  bk.ap[b][:, off:off + 256],
                                  wgb[par][:, k * HEXP + m * 128: k * HEXP + (m + 1) * 128],
                                  ybf[:, k * S + c * 256: k * S + (c + 1) * 256],
                                  start=(k == 0), stop=(k == DT - 1)),
                              waits=w if k == 0 else (),
                              incs=[(self.s_mm, 1)] if k == DT - 1 else ())
                    if m % 2 == 1:
                        bk.wrote(b)
                for m in range(4):
                    b = 2 + m // 2
                    off = (m % 2) * 256
                    w = bk.war_waits(b) + [(s_wD[par], wD_cnt[par])]
                    for k in range(DT):
                        pr.op("tensor",
                              lambda t, b=b, off=off, m=m, c=c, k=k, par=par: t.matmul(
                                  bk.ap[b][:, off:off + 256],
                                  wub[par][:, k * HEXP + m * 128: k * HEXP + (m + 1) * 128],
                                  ybf[:, k * S + c * 256: k * S + (c + 1) * 256],
                                  start=(k == 0), stop=(k == DT - 1)),
                              waits=w if k == 0 else (),
                              incs=[(self.s_mm, 1)] if k == DT - 1 else ())
                    if m % 2 == 1:
                        bk.wrote(b)

            def emit_h(c):
                w2mm = self.s_mm.count
                prev_hv = hready[c - 1][0] if c - 1 in hready else 0
                sil_m = []
                for b in range(2):
                    pr.op("scalar",
                          lambda sx, b=b: sx.activation(
                              silu_s[:, b * 512:(b + 1) * 512], bk.ap[b],
                              AF.Silu, scale=INVQ),
                          waits=bk.raw_wait(b) +
                          ([(self.s_ve, prev_hv)] if b == 0 else []))
                    bk.read(b, "scalar")
                    sil_m.append(self.s_sc.count)
                for b in range(2):
                    pr.op("vector",
                          lambda v, b=b: v.tensor_tensor(
                              hb[:, b * 512:(b + 1) * 512],
                              silu_s[:, b * 512:(b + 1) * 512],
                              bk.ap[2 + b], ALU.mult),
                          waits=bk.raw_wait(2 + b) + [(self.s_sc, sil_m[b]),
                                                      (self.s_mm, w2mm)])
                    bk.read(2 + b, "vector")
                hready[c] = (self.s_ve.count, 0)

            def emit_down(c):
                hv, _ = hready[c]
                for m2 in range(DT):
                    b = 4 + m2 // 2
                    off = (m2 % 2) * 256
                    w = bk.war_waits(b) + [(self.s_ve, hv)]
                    for k in range(4):
                        pr.op("tensor",
                              lambda t, b=b, off=off, m2=m2, k=k, par=par: t.matmul(
                                  bk.ap[b][:, off:off + 256],
                                  wdb[par][:, k * D + m2 * 128: k * D + (m2 + 1) * 128],
                                  hb[:, k * 256:(k + 1) * 256],
                                  start=(k == 0), stop=(k == 3)),
                              waits=w if k == 0 else (),
                              incs=[(self.s_mm, 1)] if k == 3 else ())
                    if m2 % 2 == 1:
                        bk.wrote(b)
                for bb in range(4):
                    b = 4 + bb
                    pr.op("vector",
                          lambda v, b=b, bb=bb, c=c: v.scalar_tensor_tensor(
                              self.y.rearrange("p (a t) -> p a t", a=DT)
                              [:, 2 * bb:2 * bb + 2, c * 256:(c + 1) * 256],
                              bk.ap[b].rearrange("p (a t) -> p a t", a=2), SQ,
                              self.y.rearrange("p (a t) -> p a t", a=DT)
                              [:, 2 * bb:2 * bb + 2, c * 256:(c + 1) * 256],
                              ALU.mult, ALU.add),
                          waits=bk.raw_wait(b))
                    bk.read(b, "vector")

            lnw = {}
            hook_k = None
            for c in range(NCH2):
                emit_gu(c)
                if pend_c is not None:
                    emit_down(pend_c)
                    if g == NG - 1 and ln_hook is not None and pend_c % 2 == 1:
                        k = pend_c // 2
                        lnw[k] = (self.s_ve, self.s_ve.count)
                        if k - 1 >= 0:
                            hook_k = k - 1
                emit_h(c)
                if hook_k is not None:
                    ln_hook(hook_k, lnw[hook_k])
                    hook_k = None
                pend_c = c
            emit_down(pend_c)
            if g == NG - 1 and ln_hook is not None:
                lnw[pend_c // 2] = (self.s_ve, self.s_ve.count)
                ln_hook(pend_c // 2 - 1, lnw[pend_c // 2 - 1])
                ln_hook(pend_c // 2, lnw[pend_c // 2])
            mm_end[g] = self.s_mm.count
            if g + 2 < NG:
                emit_loadD(g + 2)
        ar.free("wgD0", "wgD1", "wuD0", "wuD1", "wdD0", "wdD1", "siluD", "hbD")
        self.y_ready = ("ve", self.s_ve.count)

    def finish(self, stream_st=None):
        pr = self.pr
        yv = self.y.rearrange("p (a t) -> p a t", a=DT)
        out = self.out_d.rearrange("(a p) t -> p a t", p=128)
        if stream_st is None:
            for half in range(2):
                pr.dma("sync", out[:, half * 4:(half + 1) * 4, :],
                       yv[:, half * 4:(half + 1) * 4, :], self.s_st,
                       waits=self.ywait() if half == 0 else ())
        else:
            for c in range(NCH):
                pr.dma("sync", out[:, :, c * 512:(c + 1) * 512],
                       yv[:, :, c * 512:(c + 1) * 512], self.s_st,
                       waits=[(self.s_ve, stream_st["ve"][c])])
        pr.op("sync", lambda s: s.nop(), waits=[(self.s_st, self.s_st.count)])


def build_program(phases="A"):
    b = Build(phases)
    ybf = ybf_done = None
    ln_st = None
    if "A" in phases:
        ybf, ybf_done, ln_st = b.phase_A()
        ybf_tag = "a"
    if "B" in phases:
        if ybf is None:
            ybf, ybf_done = b.mk_ybf("b_in")
            ybf_tag = "b_in"
        else:
            ybf_tag = "a"
        b.ybf_tag = ybf_tag
        stB = [None]

        def hookB(c, w):
            if stB[0] is None:
                stB[0] = b.ln_start("b")
            b.ln_chunk(stB[0], c, wait=w)
        b.phase_B(ybf, ybf_done, ln_hook=hookB, ln_st=ln_st)
        ybf, ybf_done = b.ln_end(stB[0])
        ln_st = stB[0]
        ybf_tag = "b"
    if "C" in phases:
        if "D" in phases:
            b.phase_D_pre()
        if ybf is None:
            ybf, ybf_done = b.mk_ybf("c_in")
            ybf_tag = "c_in"
        ybf, ybf_done, ln_st = b.phase_C(ybf, ybf_done, free_ybf="ybf_" + ybf_tag,
                                         ln_st=ln_st)
        ybf_tag = "c"
    std = None
    if "D" in phases:
        if ybf is None:
            ybf, ybf_done = b.mk_ybf("d_in")
            ybf_tag = "d_in"
        std = b.ln_start("d", ybf_buf=ybf)

        def hookD(c, w):
            b.ln_chunk(std, c, wait=w, emit_ybf=False)
        b.phase_D(ybf, ybf_done, ln_st=ln_st, ln_hook=hookD)
        b.ln_end(std)
        b.ar.free("ybf_" + ybf_tag)
    b.finish(stream_st=std)
    b.pr.emit(b.nc)
    return b


def _prep_shared(inputs, phases="ABCD"):
    f = np.float32
    out = {}
    if "A" in phases:
        for nm, key in (("wq", "L0_attn_wq"), ("wqc", "L0_attn_wqc"),
                        ("wqr", "L0_attn_wqr"), ("wkv", "L0_attn_wkv"),
                        ("wkr", "L0_attn_wkr"), ("wkc", "L0_attn_wkc"),
                        ("wv", "L0_attn_wv"), ("wo", "L0_attn_wo")):
            out[nm] = np.ascontiguousarray(np.asarray(inputs[key], f))

        def bias_tile(b, n):
            return np.ascontiguousarray(np.asarray(b, f).reshape(n, 128).T)

        out["bq"] = bias_tile(inputs["L0_attn_bq"], 2)
        out["bqc"] = bias_tile(np.concatenate([np.asarray(inputs["L0_attn_bqc"], f)]), 7)
        out["bqr"] = bias_tile(inputs["L0_attn_bqr"], 1)
        out["bkv"] = bias_tile(inputs["L0_attn_bkv"], 2)
        out["bkr"] = bias_tile(inputs["L0_attn_bkr"], 1)
        out["bkc"] = bias_tile(inputs["L0_attn_bkc"], 7)
        out["bv"] = bias_tile(inputs["L0_attn_bv"], 8)
        out["bo"] = bias_tile(inputs["L0_attn_bo"], 8)

        inv_freq = (1.0 / 10000.0 ** (np.arange(0, DR, 2, dtype=f) / DR)).astype(f)
        t = np.arange(S, dtype=f)
        freqs = (t[:, None] * inv_freq[None, :]).astype(f)
        emb = np.concatenate([freqs, freqs], axis=1)
        cos = np.cos(emb).astype(f).T
        sin = np.sin(emb).astype(f).T
        sin_eff = sin.copy()
        sin_eff[0:64, :] *= -1.0
        out["ropecos"] = np.ascontiguousarray(cos)
        out["ropesin"] = np.ascontiguousarray(sin_eff)

        m = np.zeros((128, 4 * 512), dtype=ml_dtypes.bfloat16)
        p = np.arange(128)[:, None]
        fe = np.arange(512)[None, :]
        for j in range(4):
            m[:, j * 512:(j + 1) * 512] = (p <= fe - 128 * j).astype(ml_dtypes.bfloat16)
        out["masks"] = m
    if "B" in phases:
        out["identf"] = np.eye(128, dtype=f)
        out["iotae"] = np.tile(np.arange(EXP, dtype=f)[None, :], (128, 16))
        out["identb"] = np.eye(128, dtype=ml_dtypes.bfloat16)
        oneh = np.zeros((32, 32 * 128), dtype=ml_dtypes.bfloat16)
        for e in range(32):
            oneh[e, e * 128:(e + 1) * 128] = 1
        out["oneh"] = oneh
        for nm, key in (("Wg", "L0_ffn_Wg"), ("W1", "L0_ffn_W1"), ("W2", "L0_ffn_W2"),
                        ("W3", "L0_ffn_W3"), ("Ws1", "L0_ffn_Ws1"),
                        ("Ws2", "L0_ffn_Ws2"), ("Ws3", "L0_ffn_Ws3")):
            out[nm] = np.ascontiguousarray(np.asarray(inputs[key], f))
    if "C" in phases:
        out["l1wqT"] = np.ascontiguousarray(np.asarray(inputs["L1_attn_wq"], f).T)
        out["l1wkT"] = np.ascontiguousarray(np.asarray(inputs["L1_attn_wk"], f).T)
        out["l1wvT"] = np.ascontiguousarray(np.asarray(inputs["L1_attn_wv"], f).T)
        out["l1woT"] = np.ascontiguousarray(np.asarray(inputs["L1_attn_wo"], f).T)
    if "D" in phases:
        out["wgT"] = np.ascontiguousarray(np.asarray(inputs["L1_ffn_w_gate"]).T)
        out["wuT"] = np.ascontiguousarray(np.asarray(inputs["L1_ffn_w_up"]).T)
        out["wdT"] = np.ascontiguousarray(np.asarray(inputs["L1_ffn_w_down"]).T)
    return out


def run_phases(x, inputs, phases):
    shared = _prep_shared(inputs, phases)
    b = build_program(phases)
    in_maps = []
    for c in range(8):
        m = dict(shared)
        m["xT"] = np.ascontiguousarray(np.asarray(x[c], np.float32).T)
        in_maps.append(m)
    res = run_bass_kernel_spmd(b.nc, in_maps, core_ids=list(range(8)))
    return np.stack([np.asarray(r["y_out"]).T for r in res.results], axis=0)


def kernel(**inputs):
    x = np.asarray(inputs["L0_attn_x"], np.float32)
    return run_phases(x, inputs, "ABCD")
